# version 24
# baseline (speedup 1.0000x reference)
import numpy as np
import ml_dtypes
from contextlib import ExitStack
import concourse.bass as bass
import concourse.mybir as mybir
from concourse.bass_utils import run_bass_kernel_spmd

F32 = mybir.dt.float32
BF16 = mybir.dt.bfloat16
AF = mybir.ActivationFunctionType
ALU = mybir.AluOpType
AX = mybir.AxisListType
BF = ml_dtypes.bfloat16

T = 4096
D = 1024
NH = 8
HD = 256
KVR = 64
BLK = 256
NBLK = T // BLK
C0 = 16
N0 = T // C0
C1 = 16
N1 = N0 // C1

FFN_PIPE = True
ALL_PHASES = ("spec", "ln1", "attn", "wo", "ffn1", "xbp", "scan", "ys", "ffn2")
PHASES = set(ALL_PHASES)
NHEADS_EMIT = NH

_DTSIZE = {F32: 4, BF16: 2, mybir.dt.float16: 2, mybir.dt.int32: 4}


def _span(ap):
    sz = _DTSIZE.get(ap.dtype, 4)
    off = ap.offset
    if not isinstance(off, int):
        off = 0
    dims = list(ap.ap)
    tname = type(ap.tensor).__name__
    if tname in ("SBTensorHandle", "PSumTensorHandle") and len(dims) >= 1:
        pitch = abs(int(dims[0][0])) if dims[0][0] else None
        rest = dims[1:]
        if pitch:
            off = off % pitch
        ext = off
        for step, count in rest:
            ext += abs(int(step)) * (int(count) - 1)
        lo, hi = off * sz, (ext + 1) * sz
        if tname == "PSumTensorHandle":
            lo = (lo // 2048) * 2048
            hi = ((hi + 2047) // 2048) * 2048
        return ap.tensor.name, lo, hi
    ext = off
    for step, count in dims:
        ext += abs(int(step)) * (int(count) - 1)
    return ap.tensor.name, off * sz, (ext + 1) * sz


class Prog:
    ENGS = ("pe", "act", "dve", "pool", "sp")
    NSD = 8

    def __init__(self, nc, stack):
        self.nc = nc
        self.sems = {e: stack.enter_context(nc.semaphore(f"s_{e}")) for e in self.ENGS}
        self.dsems = {}
        for q in ("sp", "pool", "act"):
            for r in range(self.NSD):
                self.dsems[f"d_{q}{r}"] = stack.enter_context(nc.semaphore(f"d_{q}{r}"))
        self.cnt = {e: 0 for e in self.ENGS}
        self.dseq = {"sp": 0, "pool": 0, "act": 0}
        self.dval = {k: 0 for k in self.dsems}
        self.stream = {e: [] for e in self.ENGS}
        self.writers = {}
        self.readers = {}
        self.waited = {e: {} for e in self.ENGS}
        self.sem_by_name = {f"s_{e}": s for e, s in self.sems.items()}
        self.sem_by_name.update(self.dsems)

    def g(self, eng, fn, reads=(), writes=(), accum=False, dma=False):
        pre_wait = None
        if dma:
            q = eng if eng in ("sp", "act") else "pool"
            r = self.dseq[q] % self.NSD
            self.dseq[q] += 1
            my_name = f"d_{q}{r}"
            prev = self.dval[my_name]
            if prev > 0:
                pre_wait = (my_name, prev)
            self.dval[my_name] = prev + 16
            my = (my_name, prev + 16)
            inc_sem, inc_amt = self.dsems[my_name], 16
        else:
            my_name = f"s_{eng}"
            self.cnt[eng] += 1
            my = (my_name, self.cnt[eng])
            inc_sem, inc_amt = self.sems[eng], 1
        waits = {}
        def need(sn, v):
            if waits.get(sn, 0) < v:
                waits[sn] = v
        if pre_wait is not None:
            need(*pre_wait)
        rspans = [_span(a) for a in reads]
        wspans = [_span(a) for a in writes]
        for nm, lo, hi in rspans:
            for wlo, whi, sn, v in self.writers.get(nm, ()):
                if wlo < hi and lo < whi:
                    need(sn, v)
        for nm, lo, hi in wspans:
            for wlo, whi, sn, v in self.writers.get(nm, ()):
                if wlo < hi and lo < whi and not (accum and sn == my_name):
                    need(sn, v)
            for rlo, rhi, sn, v in self.readers.get(nm, ()):
                if rlo < hi and lo < rhi:
                    need(sn, v)
        w2 = {}
        for sn, v in waits.items():
            if self.waited[eng].get(sn, -1) >= v:
                continue
            w2[sn] = v
            self.waited[eng][sn] = v
        for nm, lo, hi in wspans:
            wl = self.writers.setdefault(nm, [])
            wl[:] = [e_ for e_ in wl if not (lo <= e_[0] and e_[1] <= hi)]
            wl.append((lo, hi, my_name, my[1]))
            rl = self.readers.setdefault(nm, [])
            rl[:] = [e_ for e_ in rl if not (lo <= e_[0] and e_[1] <= hi)]
        for nm, lo, hi in rspans:
            self.readers.setdefault(nm, []).append((lo, hi, my_name, my[1]))
        self.stream[eng].append((w2, fn, inc_sem, inc_amt))
        return my

    def run_stream(self, eng, handle):
        for w2, fn, inc_sem, inc_amt in self.stream[eng]:
            for sn, v in w2.items():
                handle.wait_ge(self.sem_by_name[sn], v)
            ins = fn(handle)
            assert ins is not None, f"group on {eng} emitted no instruction"
            ins.then_inc(inc_sem, inc_amt)

    def finalize(self, block):
        p = self

        @block.tensor
        def _(t):
            p.run_stream("pe", t)

        @block.scalar
        def _(s):
            p.run_stream("act", s)
            for r in range(p.NSD):
                nm = f"d_act{r}"
                if p.dval[nm]:
                    s.wait_ge(p.dsems[nm], p.dval[nm])

        @block.vector
        def _(v):
            p.run_stream("dve", v)

        @block.gpsimd
        def _(g):
            p.run_stream("pool", g)
            for r in range(p.NSD):
                nm = f"d_pool{r}"
                if p.dval[nm]:
                    g.wait_ge(p.dsems[nm], p.dval[nm])

        @block.sync
        def _(sy):
            p.run_stream("sp", sy)
            for r in range(p.NSD):
                nm = f"d_sp{r}"
                if p.dval[nm]:
                    sy.wait_ge(p.dsems[nm], p.dval[nm])


def build_nc():
    nc = bass.Bass("TRN2", target_bir_lowering=False, debug=False)
    I = {}
    def inp(name, shape, dt=F32):
        I[name] = nc.dram_tensor(name, shape, dt, kind="ExternalInput").ap()
        return I[name]

    inp("xT", [D, T])
    inp("wqT", [D, NH * HD]); inp("wkvT", [D, KVR])
    inp("wkuT", [KVR, NH * HD]); inp("wvuT", [KVR, NH * HD])
    inp("woT", [NH * HD, D])
    inp("wa", [D, D]); inp("waT", [D, D])
    inp("wb", [D, D]); inp("wbT", [D, D])
    inp("wc", [D, D]); inp("wcT", [D, D])
    inp("sraw", [64, D]); inp("knots", [1, 8]); inp("temp", [1, 1])
    inp("cw1", [D, 256]); inp("cb1c", [128, 2]); inp("cw2", [256, 2])
    inp("cb2", [2, 1]); inp("spl", [64, 65]); inp("dirs", [64, D])
    inp("gam", [128, 8]); inp("bet", [128, 8]); inp("osc", [1, 1])
    inp("idn", [128, 128]); inp("onesr", [1, 128]); inp("w8", [128, 8])
    inp("ind8", [64, 8])
    inp("epsln", [1, 1]); inp("epstiny", [1, 1])

    xout = nc.dram_tensor("xout", [D, T], F32, kind="ExternalOutput").ap()
    xres = nc.dram_tensor("xres", [D, T], F32, kind="Internal").ap()
    ottok = nc.dram_tensor("ottok", [T, NH * HD], BF16, kind="Internal").ap()
    atn_d = nc.dram_tensor("atn_d", [D, D], BF16, kind="Internal").ap()
    ann_d = nc.dram_tensor("ann_d", [D, D], BF16, kind="Internal").ap()
    m1t_d = nc.dram_tensor("m1t_d", [D, D], BF16, kind="Internal").ap()
    m2t_d = nc.dram_tensor("m2t_d", [D, D], BF16, kind="Internal").ap()
    cnt_d = nc.dram_tensor("cnt_d", [D, D], BF16, kind="Internal").ap()
    mrow_d = nc.dram_tensor("mrow_d", [16, 512], F32, kind="Internal").ap()
    m3t_d = nc.dram_tensor("m3t_d", [D, D], BF16, kind="Internal").ap()

    with ExitStack() as stack:
        def sbt(name, shape, dt=F32):
            return stack.enter_context(nc.sbuf_tensor(name, shape, dt))
        def pst(name, shape, dt=F32):
            return stack.enter_context(nc.psum_tensor(name, shape, dt))

        idn_f = sbt("idn_f", [128, 72])
        idn_b = sbt("idn_b", [128, 128], BF16)
        onesr_f = sbt("onesr_f", [1, 128])
        onesc_f = sbt("onesc_f", [128, 1])
        onesc_b = sbt("onesc_b", [128, 1], BF16)
        w8_f = sbt("w8_f", [128, 8])
        eps_c = sbt("eps_c", [1, 1])
        tiny_c = sbt("tiny_c", [1, 1])
        dkn = sbt("dkn", [128, 8])
        itemp = sbt("itemp", [128, 1])
        oscb = sbt("oscb", [128, 1])
        crow = sbt("crow", [128, 17])
        mrow_s = sbt("mrow_s", [1, 1024])
        gam_c = sbt("gam_c", [128, 8])
        bet_c = sbt("bet_c", [128, 8])
        cb1_c = sbt("cb1_c", [128, 2])
        cb1e_c = sbt("cb1e_c", [128, 2])
        cb2_c = sbt("cb2_c", [2, 1])
        sigsT = sbt("sigsT", [128, 8 * 72], BF16)
        spl_b = sbt("spl_b", [64, 65], BF16)
        cw2_b = sbt("cw2_b", [128, 2 * 2], BF16)
        srow = sbt("srow", [1, 64])
        isig = sbt("isig", [128, 4])
        uvb = sbt("uvb", [128, 6 * 8], BF16)
        uvf = sbt("uvf", [128, 3 * 24])
        negC = sbt("negC", [128, 1])
        mstat = sbt("mstat", [1, 4 * BLK])
        RCt = sbt("RCt", [128, 8])
        FRC = sbt("FRC", [128, 8])
        S2s = sbt("S2s", [128, 8 * N1], BF16)
        C1b = sbt("C1b", [128, 8 * N1], BF16)
        C1c = sbt("C1c", [128, 8 * N1], BF16)

        BIG1 = sbt("BIG1", [128, 32768], BF16)
        BIG2 = sbt("BIG2", [128, 32768], BF16)
        WA = sbt("WA", [128, 16384], BF16)
        SCR = sbt("SCR", [128, 6144])
        SCB = sbt("SCB", [128, 4096], BF16)
        XB1 = SCR[:, 0:2048]
        XB2 = SCR[:, 2048:4096]
        XB3 = SCR[:, 4096:6144]
        XBb = SCB[:, 0:2048]
        SQb = SCB[:, 2048:4096]

        psA = pst("psA", [128, 2048])
        psB = pst("psB", [128, 2048])
        psA_b = psA.bitcast(BF16)
        psB_b = psB.bitcast(BF16)

        P = Prog(nc, stack)
        _defer = [None]

        def gg(*a, **k):
            if _defer[0] is not None:
                _defer[0].append((a, k))
                return None
            return P.g(*a, **k)

        def collect(fn, *args, **kw):
            buf = []
            _defer[0] = buf
            try:
                fn(*args, **kw)
            finally:
                _defer[0] = None
            return buf

        def interleave(*bufs):
            n = max(len(b) for b in bufs)
            for i in range(n):
                for b in bufs:
                    if i < len(b):
                        a, k = b[i]
                        P.g(*a, **k)

        def dma_sp(dst, src, extra_reads=()):
            return gg("sp", lambda e: e.dma_start(out=dst, in_=src),
                      reads=[src] + list(extra_reads), writes=[dst], dma=True)

        def dma_cast(dst, src, extra_reads=()):
            return gg("pool", lambda e: e.dma_start(out=dst, in_=src),
                      reads=[src] + list(extra_reads), writes=[dst], dma=True)

        dma_sp(idn_f[:], I["idn"][:, 0:72])
        dma_cast(idn_b[:], I["idn"])
        dma_sp(onesr_f[:], I["onesr"])
        dma_sp(onesc_f[:], I["onesr"].rearrange("o p -> p o"))
        dma_cast(onesc_b[:], I["onesr"].rearrange("o p -> p o"))
        dma_sp(w8_f[:], I["w8"])
        dma_sp(eps_c[:], I["epsln"])
        dma_sp(tiny_c[:], I["epstiny"])
        dma_sp(gam_c[:], I["gam"])
        dma_sp(bet_c[:], I["bet"])
        dma_sp(cb1_c[:], I["cb1c"])
        dma_sp(cb2_c[:], I["cb2"])
        dma_cast(spl_b[:], I["spl"])
        dma_cast(cw2_b[:].rearrange("p (c n) -> p c n", c=2),
                 I["cw2"].rearrange("(c p) n -> p c n", p=128))
        dma_sp(srow[0:1, 0:8], I["knots"])
        dma_sp(srow[0:1, 8:9], I["temp"])
        dma_sp(srow[0:1, 9:10], I["osc"])
        dma_sp(SCR[0:64, 2048:3072], I["sraw"])

        gg("dve", lambda e: e.tensor_scalar_mul(cb1e_c[:], cb1_c[:], 0.7071067811865476),
           reads=[cb1_c[:]], writes=[cb1e_c[:]])
        gg("dve", lambda e: e.tensor_tensor(srow[0:1, 16:23], srow[0:1, 1:8],
                                            srow[0:1, 0:7], ALU.subtract),
           reads=[srow[0:1, 0:8]], writes=[srow[0:1, 16:23]])
        gg("dve", lambda e: e.tensor_scalar_add(srow[0:1, 26:27], srow[0:1, 8:9], 1e-6),
           reads=[srow[0:1, 8:9]], writes=[srow[0:1, 26:27]])
        gg("dve", lambda e: e.reciprocal(srow[0:1, 24:25], srow[0:1, 26:27]),
           reads=[srow[0:1, 26:27]], writes=[srow[0:1, 24:25]])
        gg("dve", lambda e: e.tensor_copy(srow[0:1, 23:24], srow[0:1, 0:1]),
           reads=[srow[0:1, 0:1]], writes=[srow[0:1, 23:24]])
        gg("dve", lambda e: e.tensor_copy(srow[0:1, 25:26], srow[0:1, 9:10]),
           reads=[srow[0:1, 9:10]], writes=[srow[0:1, 25:26]])
        gg("pe", lambda e: e.matmul(psA[:, 0:10], onesr_f[:], srow[0:1, 16:26],
                                    start=True, stop=True),
           reads=[srow[0:1, 16:26], onesr_f[:]], writes=[psA[:, 0:10]])
        gg("dve", lambda e: e.tensor_copy(dkn[:, 0:8], psA[:, 0:8]),
           reads=[psA[:, 0:10]], writes=[dkn[:, 0:8]])
        gg("dve", lambda e: e.tensor_copy(itemp[:], psA[:, 8:9]),
           reads=[psA[:, 0:10]], writes=[itemp[:]])
        gg("dve", lambda e: e.tensor_copy(oscb[:], psA[:, 9:10]),
           reads=[psA[:, 0:10]], writes=[oscb[:]])
        gg("dve", lambda e: e.tensor_copy(srow[0:1, 27:28], srow[0:1, 16:17]),
           reads=[srow[0:1, 16:17]], writes=[srow[0:1, 27:28]])
        gg("dve", lambda e: e.tensor_tensor(srow[0:1, 28:34], srow[0:1, 17:23],
                                            srow[0:1, 16:22], ALU.subtract),
           reads=[srow[0:1, 16:23]], writes=[srow[0:1, 28:34]])
        gg("dve", lambda e: e.tensor_scalar_mul(srow[0:1, 34:35], srow[0:1, 22:23], -1.0),
           reads=[srow[0:1, 22:23]], writes=[srow[0:1, 34:35]])
        gg("pe", lambda e: e.matmul(psA[:, 32:40], onesr_f[:], srow[0:1, 27:35],
                                    start=True, stop=True),
           reads=[srow[0:1, 27:35], onesr_f[:]], writes=[psA[:, 32:40]])
        gg("dve", lambda e: e.tensor_copy(crow[:, 0:8], psA[:, 32:40]),
           reads=[psA[:, 32:40]], writes=[crow[:, 0:8]])
        gg("dve", lambda e: e.tensor_scalar_add(crow[:, 8:16], w8_f[:], -4.5),
           reads=[w8_f[:]], writes=[crow[:, 8:16]])
        gg("dve", lambda e: e.tensor_scalar_mul(crow[:, 16:17], itemp[:], 0.5),
           reads=[itemp[:]], writes=[crow[:, 16:17]])

        sraw_sb = SCR[0:64, 2048:3072]
        gg("dve", lambda e: e.tensor_single_scalar(XB1[0:64, 0:D], sraw_sb, 0.3, ALU.is_gt),
           reads=[sraw_sb], writes=[XB1[0:64, 0:D]])
        gg("dve", lambda e: e.tensor_single_scalar(XB3[0:64, 0:D], sraw_sb, -0.3, ALU.is_lt),
           reads=[sraw_sb], writes=[XB3[0:64, 0:D]])
        gg("dve", lambda e: e.tensor_tensor(XB1[0:64, 1024:2048], XB1[0:64, 0:D],
                                            XB3[0:64, 0:D], ALU.subtract),
           reads=[XB1[0:64, 0:D], XB3[0:64, 0:D]], writes=[XB1[0:64, 1024:2048]])
        gg("pool", lambda e: e.tensor_copy(SCB[0:64, 0:D], XB1[0:64, 1024:2048]),
           reads=[XB1[0:64, 1024:2048]], writes=[SCB[0:64, 0:D]])
        dma_cast(SCB[0:64, 3072:3080], I["ind8"])
        def csig_mm(e):
            for n in range(2):
                ins = e.matmul(psA[0:8, 512 + n * 512:512 + (n + 1) * 512],
                               SCB[0:64, 3072:3080], SCB[0:64, n * 512:(n + 1) * 512],
                               start=True, stop=True)
            return ins
        gg("pe", csig_mm, reads=[SCB[0:64, 3072:3080], SCB[0:64, 0:D]],
           writes=[psA[0:8, 512:1536]])
        gg("dve", lambda e: e.tensor_single_scalar(XB1[0:8, 0:D], psA[0:8, 512:1536],
                                                   0.0, ALU.is_gt),
           reads=[psA[0:8, 512:1536]], writes=[XB1[0:8, 0:D]])
        gg("dve", lambda e: e.tensor_single_scalar(XB3[0:8, 0:D], psA[0:8, 512:1536],
                                                   0.0, ALU.is_lt),
           reads=[psA[0:8, 512:1536]], writes=[XB3[0:8, 0:D]])
        gg("dve", lambda e: e.tensor_tensor(XB1[0:8, 1024:2048], XB1[0:8, 0:D],
                                            XB3[0:8, 0:D], ALU.subtract),
           reads=[XB1[0:8, 0:D], XB3[0:8, 0:D]], writes=[XB1[0:8, 1024:2048]])
        gg("pool", lambda e: e.tensor_copy(SCB[64:72, 0:D], XB1[0:8, 1024:2048]),
           reads=[XB1[0:8, 1024:2048]], writes=[SCB[64:72, 0:D]])
        for fc in range(8):
            gg("pe", (lambda fc=fc: lambda e: e.transpose(
                psA_b[:, 0:64], SCB[0:64, fc * 128:(fc + 1) * 128], idn_b[0:64, 0:64]))(),
               reads=[SCB[0:64, 0:D], idn_b[:]], writes=[psA[:, 0:32]])
            gg("pe", (lambda fc=fc: lambda e: e.transpose(
                psA_b[:, 64:72], SCB[64:72, fc * 128:(fc + 1) * 128], idn_b[64:72, 64:72]))(),
               reads=[SCB[64:72, 0:D], idn_b[:]], writes=[psA[:, 32:36]])
            gg("dve", (lambda fc=fc: lambda e: e.tensor_copy(
                sigsT[:, fc * 72:(fc + 1) * 72], psA_b[:, 0:72]))(),
               reads=[psA[:, 0:36]], writes=[sigsT[:, fc * 72:(fc + 1) * 72]])

        def colsum_f32(src_col, dst_ps):
            gg("pe", lambda e: e.matmul(dst_ps, onesc_f[:], src_col, start=True, stop=True),
               reads=[src_col, onesc_f[:]], writes=[dst_ps])

        def bcast_col(dst_col, src11, ps):
            gg("pe", lambda e: e.matmul(ps, onesr_f[:], src11, start=True, stop=True),
               reads=[src11, onesr_f[:]], writes=[ps])
            gg("dve", lambda e: e.tensor_copy(dst_col, ps),
               reads=[ps], writes=[dst_col])

        def load_x_block(b, src, want16=True):
            dma_sp(XB1.rearrange("p (f n) -> p f n", f=8),
                   src.rearrange("(f p) n -> p f n", p=128)[:, :, b * BLK:(b + 1) * BLK])
            if want16:
                dma_cast(XBb.rearrange("p (f n) -> p f n", f=8),
                         src.rearrange("(f p) n -> p f n", p=128)[:, :, b * BLK:(b + 1) * BLK])

        STB = {0: psB[0:1, 1024:1280], 1: psB[0:1, 1280:1536]}
        STQ = {0: psB[0:1, 1536:1792], 1: psB[0:1, 1792:2048]}
        BCM = {0: psB[:, 0:256], 1: psB[:, 1024:1280]}
        BCR = {0: psB[:, 512:768], 1: psB[:, 1536:1792]}
        MS = {0: (mstat[0:1, 0:256], mstat[0:1, 256:512]),
              1: (mstat[0:1, 512:768], mstat[0:1, 768:1024])}
        def Wbuf(p):
            return XB1 if p == 0 else XB2
        def Bbuf(p):
            return SCB[:, p * 2048:(p + 1) * 2048]

        def mmsum8(src, ps):
            def f(e):
                for fc in range(8):
                    ins = e.matmul(ps, onesc_b[:], src[:, fc * BLK:(fc + 1) * BLK],
                                   start=(fc == 0), stop=(fc == 7))
                return ins
            return f

        def ln_stats2(p, b, src_dram, rmul=None, have_b16=False, bcast=True):
            B_p = Bbuf(p)
            if not have_b16:
                dma_cast(B_p.rearrange("p (f n) -> p f n", f=8),
                         src_dram.rearrange("(f p) n -> p f n", p=128)
                         [:, :, b * BLK:(b + 1) * BLK])
            gg("pe", mmsum8(B_p, STB[p]), reads=[B_p, onesc_b[:]], writes=[STB[p]])
            gg("pool", lambda e: e.tensor_tensor(B_p, B_p, B_p, ALU.mult),
               reads=[B_p], writes=[B_p])
            gg("pe", mmsum8(B_p, STQ[p]), reads=[B_p, onesc_b[:]], writes=[STQ[p]])
            m_r, r_r = MS[p]
            gg("dve", (lambda p=p, m_r=m_r: lambda e: e.tensor_scalar_mul(
                m_r, STB[p], 1.0 / D))(), reads=[STB[p]], writes=[m_r])
            gg("dve", (lambda m_r=m_r, r_r=r_r: lambda e: e.tensor_tensor(
                r_r, m_r, m_r, ALU.mult))(), reads=[m_r], writes=[r_r])
            gg("dve", (lambda p=p, r_r=r_r: lambda e: e.scalar_tensor_tensor(
                r_r, STQ[p], 1.0 / D, r_r, ALU.mult, ALU.subtract))(),
               reads=[STQ[p], r_r], writes=[r_r])
            gg("act", (lambda r_r=r_r: lambda e: e.activation(
                r_r, r_r, AF.Ln, bias=eps_c[:]))(),
               reads=[r_r, eps_c[:]], writes=[r_r])
            gg("act", (lambda r_r=r_r: lambda e: e.activation(
                r_r, r_r, AF.Exp, scale=-0.5))(), reads=[r_r], writes=[r_r])
            if rmul is not None:
                gg("dve", (lambda r_r=r_r: lambda e: e.tensor_scalar(
                    r_r, r_r, rmul, None, ALU.mult))(),
                   reads=[r_r, rmul], writes=[r_r])
            if not bcast:
                return
            gg("pe", (lambda p=p, m_r=m_r: lambda e: e.matmul(
                BCM[p], onesr_f[:], m_r, start=True, stop=True))(),
               reads=[m_r, onesr_f[:]], writes=[BCM[p]])
            gg("pe", (lambda p=p, r_r=r_r: lambda e: e.matmul(
                BCR[p], onesr_f[:], r_r, start=True, stop=True))(),
               reads=[r_r, onesr_f[:]], writes=[BCR[p]])

        def ln_apply16(p, src, dst16v):
            gg("dve", (lambda p=p, src=src: lambda e: [
                e.tensor_tensor(src[:, fc * BLK:(fc + 1) * BLK],
                                src[:, fc * BLK:(fc + 1) * BLK], BCM[p], ALU.subtract)
                for fc in range(8)][-1])(),
               reads=[src, BCM[p]], writes=[src])
            gg("dve", (lambda p=p, src=src, dst16v=dst16v: lambda e: [
                e.tensor_tensor(dst16v[:, fc, :],
                                src[:, fc * BLK:(fc + 1) * BLK], BCR[p], ALU.mult)
                for fc in range(8)][-1])(),
               reads=[src, BCR[p]], writes=[dst16v])

        xn = BIG1[:, 0:8 * T]
        xnv = xn.rearrange("p (f n) -> p f n", f=8)

        def ln1_stats(b):
            p = b % 2
            ln_stats2(p, b, I["xT"], bcast=False)
            dma_sp(mrow_d[b:b + 1, :], mstat[0:1, p * 512:(p + 1) * 512])

        def ln1_apply(b):
            p = b % 2
            mr = mrow_s[0:1, p * 512:(p + 1) * 512]
            dma_sp(mr, mrow_d[b:b + 1, :])
            gg("pe", (lambda p=p, mr=mr: lambda e: e.matmul(
                BCM[p], onesr_f[:], mr[0:1, 0:256], start=True, stop=True))(),
               reads=[mr, onesr_f[:]], writes=[BCM[p]])
            gg("pe", (lambda p=p, mr=mr: lambda e: e.matmul(
                BCR[p], onesr_f[:], mr[0:1, 256:512], start=True, stop=True))(),
               reads=[mr, onesr_f[:]], writes=[BCR[p]])
            B_p = Bbuf(p)
            dma_cast(B_p.rearrange("p (f n) -> p f n", f=8),
                     I["xT"].rearrange("(f p) n -> p f n", p=128)
                     [:, :, b * BLK:(b + 1) * BLK])
            ln_apply16(p, B_p, xnv[:, :, b * BLK:(b + 1) * BLK])

        if "spec" in PHASES:
            SPW = {0: (WA[:, 0:8192], WA[:, 8192:16384]),
                   1: (BIG1[:, 0:8192], BIG1[:, 8192:16384]),
                   2: (BIG2[:, 0:8192], BIG2[:, 8192:16384])}

            def spec_weight(wi, wn_name, wt_name):
                wnat, wtr = SPW[wi]
                ub = uvb[:, wi * 16:wi * 16 + 16]
                uf = uvf[:, wi * 24:(wi + 1) * 24]
                pmv = psA[:, wi * 512:wi * 512 + 8]
                pcs = psA[0:1, wi * 512 + 16:wi * 512 + 17]
                pbc = psA[:, wi * 512 + 17:wi * 512 + 18]
                pds = psA[0:1, wi * 512 + 18:wi * 512 + 19]
                pbc2 = psA[:, wi * 512 + 19:wi * 512 + 20]
                sr1 = srow[0:1, 32 + wi:33 + wi]
                sr2 = srow[0:1, 36 + wi:37 + wi]
                fcol = FRC[:, 4 + wi:5 + wi]
                dma_cast(wnat.rearrange("p (c n) -> p c n", c=8),
                         I[wn_name].rearrange("(c p) n -> p c n", p=128))
                dma_cast(wtr.rearrange("p (c n) -> p c n", c=8),
                         I[wt_name].rearrange("(c p) n -> p c n", p=128))
                gg("pool", (lambda ub=ub: lambda e: e.memset(ub[:, 0:8], 0.03125))(),
                   writes=[ub[:, 0:8]])

                def matvec(lhs, src_sl):
                    def f(e, lhs=lhs, src_sl=src_sl):
                        for mc in range(8):
                            for kc in range(8):
                                ins = e.matmul(pmv[:, mc:mc + 1],
                                               lhs[:, kc * 1024 + mc * 128:
                                                   kc * 1024 + (mc + 1) * 128],
                                               ub[:, src_sl * 8 + kc:src_sl * 8 + kc + 1],
                                               start=(kc == 0), stop=(kc == 7))
                        return ins
                    gg("pe", f, reads=[lhs, ub[:, src_sl * 8:(src_sl + 1) * 8]],
                       writes=[pmv])

                def normalize(dst_sl):
                    gg("dve", lambda e: e.tensor_copy(uf[:, 0:8], pmv),
                       reads=[pmv], writes=[uf[:, 0:8]])
                    gg("dve", lambda e: e.tensor_tensor(uf[:, 8:16], uf[:, 0:8],
                                                        uf[:, 0:8], ALU.mult),
                       reads=[uf[:, 0:8]], writes=[uf[:, 8:16]])
                    gg("dve", lambda e: e.reduce_sum(uf[:, 16:17], uf[:, 8:16], axis=AX.X),
                       reads=[uf[:, 8:16]], writes=[uf[:, 16:17]])
                    gg("pe", lambda e: e.matmul(pcs, onesc_f[:], uf[:, 16:17],
                                                start=True, stop=True),
                       reads=[uf[:, 16:17], onesc_f[:]], writes=[pcs])
                    gg("act", lambda e: e.activation(sr1, pcs, AF.Ln, bias=tiny_c[:]),
                       reads=[pcs, tiny_c[:]], writes=[sr1])
                    gg("act", lambda e: e.activation(sr1, sr1, AF.Exp, scale=-0.5),
                       reads=[sr1], writes=[sr1])
                    gg("pe", lambda e: e.matmul(pbc, onesr_f[:], sr1, start=True, stop=True),
                       reads=[sr1, onesr_f[:]], writes=[pbc])
                    gg("dve", lambda e: e.tensor_copy(fcol, pbc),
                       reads=[pbc], writes=[fcol])
                    gg("dve", (lambda dst_sl=dst_sl: lambda e: e.tensor_scalar(
                        ub[:, dst_sl * 8:(dst_sl + 1) * 8], uf[:, 0:8],
                        fcol, None, ALU.mult))(),
                       reads=[uf[:, 0:8], fcol],
                       writes=[ub[:, dst_sl * 8:(dst_sl + 1) * 8]])

                for it in range(10):
                    matvec(wnat, 0); normalize(1)
                    matvec(wtr, 1); normalize(0)
                matvec(wnat, 0); normalize(1)
                matvec(wtr, 1)
                gg("dve", lambda e: e.tensor_copy(uf[:, 0:8], pmv),
                   reads=[pmv], writes=[uf[:, 0:8]])
                gg("dve", lambda e: e.tensor_tensor(uf[:, 8:16], uf[:, 0:8],
                                                    ub[:, 0:8], ALU.mult),
                   reads=[uf[:, 0:8], ub[:, 0:8]], writes=[uf[:, 8:16]])
                gg("dve", lambda e: e.reduce_sum(uf[:, 16:17], uf[:, 8:16], axis=AX.X),
                   reads=[uf[:, 8:16]], writes=[uf[:, 16:17]])
                gg("pe", lambda e: e.matmul(pds, onesc_f[:], uf[:, 16:17],
                                            start=True, stop=True),
                   reads=[uf[:, 16:17], onesc_f[:]], writes=[pds])
                gg("dve", lambda e: e.reciprocal(sr2, pds),
                   reads=[pds], writes=[sr2])
                gg("pe", lambda e: e.matmul(pbc2, onesr_f[:], sr2, start=True, stop=True),
                   reads=[sr2, onesr_f[:]], writes=[pbc2])
                gg("dve", (lambda wi=wi: lambda e: e.tensor_copy(
                    isig[:, wi:wi + 1], pbc2))(),
                   reads=[pbc2], writes=[isig[:, wi:wi + 1]])

            spec_streams = [collect(spec_weight, 0, "wa", "waT"),
                            collect(spec_weight, 1, "wb", "wbT"),
                            collect(spec_weight, 2, "wc", "wcT")]
            if "ln1" in PHASES:
                def ln1_stats_run(par):
                    for b in range(par, NBLK, 2):
                        ln1_stats(b)
                spec_streams += [collect(ln1_stats_run, 0), collect(ln1_stats_run, 1)]
            interleave(*spec_streams)

            for (srcname, dstd, sg) in (("waT", atn_d, 0), ("wa", ann_d, 0),
                                        ("wcT", cnt_d, 2)):
                for hh in range(2):
                    half_src = I[srcname].rearrange("(c p) n -> p c n", p=128)[:, hh * 4:(hh + 1) * 4, :]
                    dma_sp(SCR[:, 0:4096].rearrange("p (c n) -> p c n", c=4), half_src)
                    gg("dve", (lambda sg=sg: lambda e: e.tensor_scalar(
                        SCB[:, 0:4096], SCR[:, 0:4096], isig[:, sg:sg + 1], None, ALU.mult))(),
                       reads=[SCR[:, 0:4096], isig[:, sg:sg + 1]], writes=[SCB[:, 0:4096]])
                    dma_sp(dstd.rearrange("(c p) n -> p c n", p=128)[:, hh * 4:(hh + 1) * 4, :],
                           SCB[:, 0:4096].rearrange("p (c n) -> p c n", c=4))

        if "ln1" in PHASES and "spec" not in PHASES:
            for b0 in range(0, NBLK, 2):
                interleave(collect(ln1_stats, b0), collect(ln1_stats, b0 + 1))
        if "ln1" in PHASES and "attn" not in PHASES:
            for b0 in range(0, NBLK, 2):
                interleave(collect(ln1_apply, b0), collect(ln1_apply, b0 + 1))

        if "attn" in PHASES:
            qT = BIG2[:, 0:2 * T]
            kT = BIG2[:, 2 * T:4 * T]
            VV = BIG2[:, 4 * T:4 * T + 32 * 257]
            WQ = WA[:, 0:4096]
            LAT = WA[0:64, 4096:8192]
            PT = BIG2[:, 24608:28704]
            OB = WA[:, 10240:11264]
            KUh = WA[0:64, 11264:11520]
            VUh = WA[0:64, 11520:11776]
            kvT_b = WA[:, 11776:12288]
            dma_cast(kvT_b.rearrange("p (c n) -> p c n", c=8),
                     I["wkvT"].rearrange("(c p) n -> p c n", p=128))

            def vones(e):
                for tc in range(32):
                    ins = e.memset(VV[:, tc * 257 + 256:tc * 257 + 257], 1.0)
                return ins
            gg("pool", vones, writes=[BIG2[:, 4 * T:4 * T + 32 * 257]])

            def load_head_w(h):
                hp = h % 2
                WQh_ = WQ[:, hp * 2048:(hp + 1) * 2048]
                dma_cast(WQh_.rearrange("p (c n) -> p c n", c=8),
                         I["wqT"].rearrange("(c p) n -> p c n", p=128)[:, :, h * HD:(h + 1) * HD])
                dma_cast(KUh, I["wkuT"][:, h * HD:(h + 1) * HD])
                dma_cast(VUh, I["wvuT"][:, h * HD:(h + 1) * HD])
            load_head_w(0)

            def attn_nb(nb):
                WQh = WQ[:, 0:2048]
                bk = [psA[:, ((2 * nb + j) % 4) * 512:((2 * nb + j) % 4) * 512 + 512]
                      for j in range(2)]
                def latmm(e, nb=nb, ps=bk[0]):
                    for kc in range(8):
                        ins = e.matmul(ps[0:64, 0:512],
                                       kvT_b[:, kc * KVR:(kc + 1) * KVR],
                                       xn[:, kc * T + nb * 512:kc * T + (nb + 1) * 512],
                                       start=(kc == 0), stop=(kc == 7))
                    return ins
                gg("pe", latmm, reads=[kvT_b, xn], writes=[bk[0][0:64, 0:512]])
                gg("dve", (lambda nb=nb, ps=bk[0]: lambda e: e.tensor_copy(
                    LAT[:, nb * 512:(nb + 1) * 512], ps[0:64, 0:512]))(),
                   reads=[bk[0][0:64, 0:512]], writes=[LAT[:, nb * 512:(nb + 1) * 512]])
                for mc in range(2):
                    ps = bk[1 - mc]
                    def qmm(e, mc=mc, nb=nb, WQh=WQh, ps=ps):
                        for kc in range(8):
                            ins = e.matmul(ps,
                                           WQh[:, kc * 256 + mc * 128:kc * 256 + (mc + 1) * 128],
                                           xn[:, kc * T + nb * 512:kc * T + (nb + 1) * 512],
                                           start=(kc == 0), stop=(kc == 7))
                        return ins
                    gg("pe", qmm, reads=[WQh, xn], writes=[ps])
                    dst = qT[:, mc * T + nb * 512:mc * T + (nb + 1) * 512]
                    gg("dve", (lambda dst=dst, ps=ps: lambda e: e.tensor_copy(dst, ps))(),
                       reads=[ps], writes=[dst])

            for step in range(9):
                streams = []
                if "ln1" in PHASES and step < 8:
                    streams += [collect(ln1_apply, 2 * step),
                                collect(ln1_apply, 2 * step + 1)]
                if step >= 1:
                    streams += [collect(attn_nb, step - 1)]
                if streams:
                    interleave(*streams)

            for h in range(NHEADS_EMIT):
                hp = h % 2
                WQh = WQ[:, hp * 2048:(hp + 1) * 2048]
                if h > 0:
                    for mc in range(2):
                        for nb in range(8):
                            ps = psA[:, (nb % 4) * 512:(nb % 4) * 512 + 512]
                            def qmm(e, mc=mc, nb=nb, WQh=WQh, ps=ps):
                                for kc in range(8):
                                    ins = e.matmul(ps,
                                                   WQh[:, kc * 256 + mc * 128:kc * 256 + (mc + 1) * 128],
                                                   xn[:, kc * T + nb * 512:kc * T + (nb + 1) * 512],
                                                   start=(kc == 0), stop=(kc == 7))
                                return ins
                            gg("pe", qmm, reads=[WQh, xn], writes=[ps])
                            dst = qT[:, mc * T + nb * 512:mc * T + (nb + 1) * 512]
                            gg("dve", (lambda dst=dst, ps=ps: lambda e: e.tensor_copy(dst, ps))(),
                               reads=[ps], writes=[dst])
                for mc in range(2):
                    for nb in range(8):
                        ps = psA[:, (nb % 4) * 512:(nb % 4) * 512 + 512]
                        gg("pe", (lambda mc=mc, nb=nb, ps=ps: lambda e: e.matmul(
                            ps, KUh[:, mc * 128:(mc + 1) * 128],
                            LAT[:, nb * 512:(nb + 1) * 512], start=True, stop=True))(),
                           reads=[KUh, LAT[:]], writes=[ps])
                        dst = kT[:, mc * T + nb * 512:mc * T + (nb + 1) * 512]
                        gg("dve", (lambda dst=dst, ps=ps: lambda e: e.tensor_copy(dst, ps))(),
                           reads=[ps], writes=[dst])
                def vproj():
                    for tc in range(32):
                        ps = psA[:, (tc % 4) * 512:(tc % 4) * 512 + 256]
                        gg("pe", (lambda tc=tc, ps=ps: lambda e: e.matmul(
                            ps, LAT[:, tc * 128:(tc + 1) * 128],
                            VUh, start=True, stop=True))(),
                           reads=[VUh, LAT[:]], writes=[ps])
                        dst = VV[:, tc * 257:tc * 257 + 256]
                        gg("dve", (lambda dst=dst, ps=ps: lambda e: e.tensor_copy(dst, ps))(),
                           reads=[ps], writes=[dst])

                def cbound():
                    for mi, (mat, so) in enumerate(((qT, 40), (kT, 48))):
                        for nb in range(8):
                            def sqf(e, mat=mat, nb=nb):
                                for mc in range(2):
                                    ins = e.tensor_tensor(
                                        SCB[:, mc * 512:(mc + 1) * 512],
                                        mat[:, mc * T + nb * 512:mc * T + (nb + 1) * 512],
                                        mat[:, mc * T + nb * 512:mc * T + (nb + 1) * 512],
                                        ALU.mult)
                                return ins
                            gg("pool", sqf, reads=[mat], writes=[SCB[:, 0:1024]])
                            def csmm(e):
                                for mc in range(2):
                                    ins = e.matmul(psB[0:1, 1024:1536], onesc_b[:],
                                                   SCB[:, mc * 512:(mc + 1) * 512],
                                                   start=(mc == 0), stop=(mc == 1))
                                return ins
                            gg("pe", csmm, reads=[SCB[:, 0:1024], onesc_b[:]],
                               writes=[psB[0:1, 1024:1536]])
                            dst = srow[0:1, so + nb:so + nb + 1]
                            gg("dve", (lambda dst=dst: lambda e: e.reduce_max(
                                dst, psB[0:1, 1024:1536], axis=AX.X))(),
                               reads=[psB[0:1, 1024:1536]], writes=[dst])
                        gg("dve", (lambda so=so: lambda e: e.reduce_max(
                            srow[0:1, so - 2:so - 1], srow[0:1, so:so + 8], axis=AX.X))(),
                           reads=[srow[0:1, so:so + 8]], writes=[srow[0:1, so - 2:so - 1]])
                    gg("act", lambda e: e.activation(srow[0:1, 56:57], srow[0:1, 38:39],
                                                     AF.Ln, bias=tiny_c[:]),
                       reads=[srow[0:1, 38:39], tiny_c[:]], writes=[srow[0:1, 56:57]])
                    gg("act", lambda e: e.activation(srow[0:1, 57:58], srow[0:1, 46:47],
                                                     AF.Ln, bias=tiny_c[:]),
                       reads=[srow[0:1, 46:47], tiny_c[:]], writes=[srow[0:1, 57:58]])
                    gg("dve", lambda e: e.tensor_tensor(srow[0:1, 56:57], srow[0:1, 56:57],
                                                        srow[0:1, 57:58], ALU.add),
                       reads=[srow[0:1, 56:58]], writes=[srow[0:1, 56:57]])
                    gg("act", lambda e: e.activation(srow[0:1, 56:57], srow[0:1, 56:57],
                                                     AF.Exp, scale=0.5),
                       reads=[srow[0:1, 56:57]], writes=[srow[0:1, 56:57]])
                    gg("dve", lambda e: e.tensor_scalar(srow[0:1, 56:57], srow[0:1, 56:57],
                                                        -0.0625, -1.0, ALU.mult, ALU.add),
                       reads=[srow[0:1, 56:57]], writes=[srow[0:1, 56:57]])
                    bcast_col(negC[:], srow[0:1, 56:57], psA[:, 300:301])

                interleave(collect(vproj), collect(cbound))
                if h + 1 < NHEADS_EMIT:
                    hp2 = (h + 1) % 2
                    WQn = WQ[:, hp2 * 2048:(hp2 + 1) * 2048]
                    dma_cast(WQn.rearrange("p (c n) -> p c n", c=8),
                             I["wqT"].rearrange("(c p) n -> p c n", p=128)
                             [:, :, (h + 1) * HD:(h + 2) * HD])
                    dma_cast(KUh, I["wkuT"][:, (h + 1) * HD:(h + 2) * HD])
                    dma_cast(VUh, I["wvuT"][:, (h + 1) * HD:(h + 2) * HD])

                def attn_chain(qb512, c, h=h):
                    psm = psA if c == 0 else psB
                    psm_b = psA_b if c == 0 else psB_b
                    PTc = PT[:, c * 2048:(c + 1) * 2048]
                    q0 = qb512 * 512 + c * 256

                    def emit_S(g):
                        bank = psm[:, (g % 2) * 512:(g % 2) * 512 + 512]
                        def f(e, g=g, bank=bank):
                            for kk in range(2):
                                kc = 2 * g + kk
                                for dc in range(2):
                                    ins = e.matmul(
                                        bank[:, kk * 256:(kk + 1) * 256],
                                        kT[:, dc * T + kc * 128:dc * T + (kc + 1) * 128],
                                        qT[:, dc * T + q0:dc * T + q0 + 256],
                                        start=(dc == 0), stop=(dc == 1))
                            return ins
                        gg("pe", f, reads=[qT, kT], writes=[bank])

                    def emit_exp(g):
                        bank = psm[:, (g % 2) * 512:(g % 2) * 512 + 512]
                        dst = PTc[:, (g % 4) * 512:(g % 4 + 1) * 512]
                        gg("act", (lambda bank=bank, dst=dst: lambda e: e.activation(
                            dst, bank, AF.Exp, bias=negC[:], scale=0.0625))(),
                           reads=[bank, negC[:]], writes=[dst])

                    def emit_PV(kc):
                        g = kc // 2
                        src = PTc[:, (g % 4) * 512 + (kc % 2) * 256:
                                  (g % 4) * 512 + (kc % 2) * 256 + 256]
                        def f(e, kc=kc, src=src):
                            for u in range(2):
                                ins = e.matmul(
                                    psm[:, 1024 + u * 512:1024 + u * 512 + 257],
                                    src[:, u * 128:(u + 1) * 128],
                                    VV[:, kc * 257:(kc + 1) * 257],
                                    start=(kc == 0), stop=(kc == 31))
                            return ins
                        gg("pe", f, reads=[src, BIG2[:, 4 * T:4 * T + 32 * 257]],
                           writes=[psm[:, 1024:2048]], accum=(kc > 0))

                    emit_S(0)
                    emit_S(1)
                    for g in range(16):
                        emit_exp(g)
                        if g + 2 < 16:
                            emit_S(g + 2)
                        emit_PV(2 * g)
                        emit_PV(2 * g + 1)
                    OBc = OB[:, c * 512:(c + 1) * 512]
                    RCc = RCt[:, c * 4:c * 4 + 2]
                    gg("dve", (lambda RCc=RCc: lambda e: [
                        e.reciprocal(RCc[:, u:u + 1],
                                     psm[:, 1024 + u * 512 + 256:1024 + u * 512 + 257])
                        for u in range(2)][-1])(),
                       reads=[psm[:, 1024:2048]], writes=[RCc])
                    gg("dve", (lambda OBc=OBc, RCc=RCc: lambda e: [
                        e.tensor_scalar(OBc[:, u * 256:(u + 1) * 256],
                                        psm[:, 1024 + u * 512:1024 + u * 512 + 256],
                                        RCc[:, u:u + 1], None, ALU.mult)
                        for u in range(2)][-1])(),
                       reads=[psm[:, 1024:2048], RCc], writes=[OBc])
                    dma_sp(ottok[q0:q0 + 256, h * HD:(h + 1) * HD]
                           .rearrange("(u p) c -> p u c", p=128),
                           OBc.rearrange("p (u c) -> p u c", u=2))

                for qb512 in range(8):
                    interleave(collect(attn_chain, qb512, 0),
                               collect(attn_chain, qb512, 1))

        SPP = {0: psA[:, 0:1024], 1: psA[:, 1024:2048]}
        BBP = {0: psB[:, 0:1024], 1: psB[:, 1024:2048]}

        def ln_stats3(p, b, src_dram, bb, have=False, rmul=None, bcast=True,
                      keep=False):
            SP = SPP[p]
            stb = SP[0:1, 0:256]
            stq = SP[0:1, 512:768]
            Mb = SP[:, 256:512]
            Rb = SP[:, 768:1024]
            if not have:
                dma_cast(bb.rearrange("p (f n) -> p f n", f=8),
                         src_dram.rearrange("(f p) n -> p f n", p=128)
                         [:, :, b * BLK:(b + 1) * BLK])
            gg("pe", mmsum8(bb, stb), reads=[bb, onesc_b[:]], writes=[stb])
            sq = WA[:, 11264 + p * 2048:11264 + (p + 1) * 2048] if keep else bb
            gg("pool", (lambda bb=bb, sq=sq: lambda e: e.tensor_tensor(
                sq, bb, bb, ALU.mult))(), reads=[bb], writes=[sq])
            gg("pe", mmsum8(sq, stq), reads=[sq, onesc_b[:]], writes=[stq])
            m_r, r_r = MS[p]
            gg("dve", (lambda m_r=m_r, stb=stb: lambda e: e.tensor_scalar_mul(
                m_r, stb, 1.0 / D))(), reads=[stb], writes=[m_r])
            gg("dve", (lambda m_r=m_r, r_r=r_r: lambda e: e.tensor_tensor(
                r_r, m_r, m_r, ALU.mult))(), reads=[m_r], writes=[r_r])
            gg("dve", (lambda r_r=r_r, stq=stq: lambda e: e.scalar_tensor_tensor(
                r_r, stq, 1.0 / D, r_r, ALU.mult, ALU.subtract))(),
               reads=[stq, r_r], writes=[r_r])
            gg("act", (lambda r_r=r_r: lambda e: e.activation(
                r_r, r_r, AF.Ln, bias=eps_c[:]))(), reads=[r_r, eps_c[:]], writes=[r_r])
            gg("act", (lambda r_r=r_r: lambda e: e.activation(
                r_r, r_r, AF.Exp, scale=-0.5))(), reads=[r_r], writes=[r_r])
            if rmul is not None:
                gg("dve", (lambda r_r=r_r: lambda e: e.tensor_scalar(
                    r_r, r_r, rmul, None, ALU.mult))(), reads=[r_r, rmul], writes=[r_r])
            if not bcast:
                return None, None
            gg("pe", (lambda Mb=Mb, m_r=m_r: lambda e: e.matmul(
                Mb, onesr_f[:], m_r, start=True, stop=True))(),
               reads=[m_r, onesr_f[:]], writes=[Mb])
            gg("pe", (lambda Rb=Rb, r_r=r_r: lambda e: e.matmul(
                Rb, onesr_f[:], r_r, start=True, stop=True))(),
               reads=[r_r, onesr_f[:]], writes=[Rb])
            return Mb, Rb

        xn2 = BIG1[:, 0:8 * T]
        xn2v = xn2.rearrange("p (f n) -> p f n", f=8)

        def subA(b):
            p = b % 2
            B_p = Bbuf(p)
            Mb, Rb = ln_stats3(p, b, xres, B_p, keep=True)
            dma_sp(mrow_d[b:b + 1, :], mstat[0:1, p * 512:(p + 1) * 512])
            dst16v = xn2v[:, :, b * BLK:(b + 1) * BLK]
            gg("dve", (lambda B_p=B_p, Mb=Mb: lambda e: [
                e.tensor_tensor(B_p[:, fc * BLK:(fc + 1) * BLK],
                                B_p[:, fc * BLK:(fc + 1) * BLK], Mb, ALU.subtract)
                for fc in range(8)][-1])(),
               reads=[B_p, Mb], writes=[B_p])
            gg("dve", (lambda B_p=B_p, Rb=Rb, dst16v=dst16v: lambda e: [
                e.tensor_tensor(dst16v[:, fc, :],
                                B_p[:, fc * BLK:(fc + 1) * BLK], Rb, ALU.mult)
                for fc in range(8)][-1])(),
               reads=[B_p, Rb], writes=[dst16v])

        cw1_b = WA[:, 0:2048]
        dirs_b = WA[0:64, 2048:3072]
        fo = BIG2[:, 0:8 * T]
        fov = fo.rearrange("p (f n) -> p f n", f=8)

        def load_ffn_w():
            dma_cast(cw1_b.rearrange("p (c n) -> p c n", c=8),
                     I["cw1"].rearrange("(c p) n -> p c n", p=128))
            dma_cast(dirs_b, I["dirs"])

        if True:

            def subB(b):
                p = b % 2
                BB = BBP[p]
                BB_b = psB_b[:, p * 2048:(p + 1) * 2048]
                FS = XB3[:, p * 1024:(p + 1) * 1024]
                SCF = FS[0:72, 0:256]
                SCT = FS[:, 256:400]
                U_ = FS[:, 400:544]
                G0 = FS[:, 544:688]
                G1 = FS[:, 688:832]
                TB = FS[:, 832:976]
                SEL = FS[:, 0:130]
                CT = FS[:, 400:656]
                T1 = FS[:, 976:992]
                T2 = FS[:, 992:1008]
                RT = FS[:, 1008:1016]
                RT2 = FS[:, 1016:1024]
                FR = FRC if p == 0 else RCt
                BS = WA[:, 4096 + p * 1024:4096 + (p + 1) * 1024]
                HB = BS[:, 0:512]
                OH = BS[:, 512:640]
                OHS = BS[:, 640:768]
                O64 = BS[0:64, 768:1024]
                gg("pe", (lambda b=b, BB=BB: lambda e: [
                    e.matmul(BB[0:72, 0:BLK], sigsT[:, kc * 72:(kc + 1) * 72],
                             xn2[:, kc * T + b * BLK:kc * T + (b + 1) * BLK],
                             start=(kc == 0), stop=(kc == 7)) for kc in range(8)][-1])(),
                   reads=[sigsT[:], xn2v[:, :, b * BLK:(b + 1) * BLK]],
                   writes=[BB[0:72, 0:BLK]])
                gg("act", (lambda BB=BB, SCF=SCF: lambda e: e.copy(
                    SCF, BB[0:72, 0:BLK]))(), reads=[BB[0:72, 0:BLK]], writes=[SCF])
                gg("pe", (lambda BB=BB, SCF=SCF: lambda e: [
                    e.transpose(BB[:, 256 + c * 72:256 + (c + 1) * 72],
                                SCF[0:72, c * 128:(c + 1) * 128],
                                idn_f[0:72, 0:72]) for c in range(2)][-1])(),
                   reads=[SCF, idn_f[:]], writes=[BB[:, 256:400]])
                gg("act", (lambda BB=BB, SCT=SCT: lambda e: e.copy(
                    SCT, BB[:, 256:400]))(), reads=[BB[:, 256:400]], writes=[SCT])
                gg("act", (lambda SCT=SCT, U_=U_: lambda e: e.activation(
                    U_, SCT, AF.Tanh, scale=crow[:, 16:17]))(),
                   reads=[SCT, crow[:]], writes=[U_])
                for kknot in range(7):
                    gg("dve", (lambda U_=U_, TB=TB, kknot=kknot: lambda e: e.tensor_scalar(
                        TB, U_, 3.5, float(3.5 - kknot), ALU.mult, ALU.add))(),
                       reads=[U_], writes=[TB])
                    gg("dve", (lambda TB=TB: lambda e: e.tensor_scalar(
                        TB, TB, 0.0, 1.0, ALU.max, ALU.min))(),
                       reads=[TB], writes=[TB])
                    if kknot == 0:
                        gg("dve", (lambda G0=G0, TB=TB: lambda e: e.tensor_scalar(
                            G0, TB, dkn[:, 0:1], dkn[:, 7:8], ALU.mult, ALU.add))(),
                           reads=[TB, dkn[:]], writes=[G0])
                    else:
                        gg("dve", (lambda kknot=kknot, TB=TB:
                                   lambda e: e.scalar_tensor_tensor(
                                       G0, TB, dkn[:, kknot:kknot + 1], G0,
                                       ALU.mult, ALU.add))(),
                           reads=[TB, dkn[:], G0], writes=[G0])
                G = G0
                for c in range(2):
                    gv = G[:, c * 72:(c + 1) * 72]
                    gg("dve", (lambda gv=gv, FR=FR: lambda e: e.reduce_max(
                        FR[:, 0:1], gv[:, 64:72], axis=AX.X))(),
                       reads=[G], writes=[FR[:, 0:1]])
                    gg("dve", (lambda gv=gv, FR=FR, RT=RT: lambda e: e.tensor_scalar(
                        RT, gv[:, 64:72], FR[:, 0:1], None, ALU.is_ge))(),
                       reads=[G, FR[:, 0:1]], writes=[RT])
                    gg("pool", (lambda RT=RT, RT2=RT2: lambda e: e.tensor_tensor(
                        RT2, RT, w8_f[:], ALU.mult))(),
                       reads=[RT, w8_f[:]], writes=[RT2])
                    gg("dve", (lambda RT2=RT2, FR=FR: lambda e: e.reduce_max(
                        FR[:, 1:2], RT2, axis=AX.X))(), reads=[RT2], writes=[FR[:, 1:2]])
                    gg("dve", (lambda RT=RT, RT2=RT2, FR=FR: lambda e: e.tensor_scalar(
                        RT, RT2, FR[:, 1:2], None, ALU.is_ge))(),
                       reads=[RT2, FR[:, 1:2]], writes=[RT])
                    for cc in range(8):
                        if cc == 0:
                            gg("dve", (lambda gv=gv, RT=RT, T1=T1: lambda e: e.tensor_scalar(
                                T1[:, 0:8], gv[:, 0:8], RT[:, 0:1], None, ALU.mult))(),
                               reads=[G, RT], writes=[T1])
                        else:
                            gg("dve", (lambda gv=gv, RT=RT, T1=T1, cc=cc:
                                       lambda e: e.scalar_tensor_tensor(
                                           T1[:, 0:8], gv[:, cc * 8:(cc + 1) * 8],
                                           RT[:, cc:cc + 1], T1[:, 0:8],
                                           ALU.mult, ALU.add))(),
                               reads=[G, RT, T1], writes=[T1])
                    gg("dve", (lambda T1=T1, FR=FR: lambda e: e.reduce_max(
                        FR[:, 2:3], T1[:, 0:8], axis=AX.X))(),
                       reads=[T1], writes=[FR[:, 2:3]])
                    gg("dve", (lambda T1=T1, T2=T2, FR=FR: lambda e: e.tensor_scalar(
                        T2[:, 0:8], T1[:, 0:8], FR[:, 2:3], None, ALU.is_ge))(),
                       reads=[T1, FR[:, 2:3]], writes=[T2])
                    gg("pool", (lambda T2=T2, T1=T1: lambda e: e.tensor_tensor(
                        T1[:, 8:16], T2[:, 0:8], w8_f[:], ALU.mult))(),
                       reads=[T2, w8_f[:]], writes=[T1])
                    gg("dve", (lambda T1=T1, FR=FR: lambda e: e.reduce_max(
                        FR[:, 3:4], T1[:, 8:16], axis=AX.X))(),
                       reads=[T1], writes=[FR[:, 3:4]])
                    gg("dve", (lambda T1=T1, T2=T2, FR=FR: lambda e: e.tensor_scalar(
                        T2[:, 8:16], T1[:, 8:16], FR[:, 3:4], None, ALU.is_ge))(),
                       reads=[T1, FR[:, 3:4]], writes=[T2])
                    gg("dve", (lambda T2=T2, RT=RT, OH=OH, c=c: lambda e: [
                        e.tensor_scalar(OH[:, c * 64 + cc * 8:c * 64 + (cc + 1) * 8],
                                        T2[:, 8:16], RT[:, cc:cc + 1], None, ALU.mult)
                        for cc in range(8)][-1])(),
                       reads=[T2, RT], writes=[OH])
                gg("pe", (lambda b=b, BB=BB: lambda e: [
                    e.matmul(BB[:, 512 + mc * 256:512 + (mc + 1) * 256],
                             cw1_b[:, kc * 256 + mc * 128:kc * 256 + (mc + 1) * 128],
                             xn2[:, kc * T + b * BLK:kc * T + (b + 1) * BLK],
                             start=(kc == 0), stop=(kc == 7))
                    for mc in range(2) for kc in range(8)][-1])(),
                   reads=[cw1_b, xn2v[:, :, b * BLK:(b + 1) * BLK]],
                   writes=[BB[:, 512:1024]])
                gg("act", (lambda BB=BB, HB=HB: lambda e: [
                    e.activation(HB[:, mc * 256:(mc + 1) * 256],
                                 BB[:, 512 + mc * 256:512 + (mc + 1) * 256],
                                 AF.Gelu, bias=cb1_c[:, mc:mc + 1])
                    for mc in range(2)][-1])(),
                   reads=[BB[:, 512:1024], cb1_c[:]], writes=[HB])
                gg("pe", (lambda BB=BB, HB=HB: lambda e: [
                    e.matmul(BB[0:2, 0:BLK], cw2_b[:, kc * 2:(kc + 1) * 2],
                             HB[:, kc * 256:(kc + 1) * 256],
                             start=(kc == 0), stop=(kc == 1)) for kc in range(2)][-1])(),
                   reads=[HB, cw2_b[:]], writes=[BB[0:2, 0:BLK]])
                gg("act", (lambda BB=BB, CT=CT: lambda e: e.activation(
                    CT[0:2, 0:BLK], BB[0:2, 0:BLK], AF.Tanh, bias=cb2_c[:]))(),
                   reads=[BB[0:2, 0:BLK], cb2_c[:]], writes=[CT])
                gg("pe", (lambda BB=BB, CT=CT: lambda e: [
                    e.transpose(BB[:, 256 + c * 2:256 + (c + 1) * 2],
                                CT[0:2, c * 128:(c + 1) * 128], idn_f[0:2, 0:2])
                    for c in range(2)][-1])(),
                   reads=[CT, idn_f[:]], writes=[BB[:, 256:260]])
                gg("dve", (lambda BB=BB, FR=FR: lambda e: e.tensor_copy(
                    FR[:, 4:8], BB[:, 256:260]))(),
                   reads=[BB[:, 256:260]], writes=[FR[:, 4:8]])
                gg("pe", (lambda BB_b=BB_b, OH=OH: lambda e: [
                    e.transpose(BB_b[0:64, c * 128:(c + 1) * 128],
                                OH[:, c * 64:(c + 1) * 64], idn_b[:])
                    for c in range(2)][-1])(),
                   reads=[OH, idn_b[:]], writes=[BB[:, 0:128]])
                gg("dve", (lambda BB_b=BB_b, O64=O64: lambda e: e.tensor_copy(
                    O64, BB_b[0:64, 0:256]))(), reads=[BB[:, 0:128]], writes=[O64])
                gg("pe", (lambda BB=BB, O64=O64: lambda e: [
                    e.matmul(BB[:, 512 + c * 65:512 + (c + 1) * 65],
                             O64[:, c * 128:(c + 1) * 128], spl_b[0:64, :],
                             start=True, stop=True) for c in range(2)][-1])(),
                   reads=[O64, spl_b[:]], writes=[BB[:, 512:642]])
                gg("act", (lambda BB=BB, SEL=SEL: lambda e: e.copy(
                    SEL, BB[:, 512:642]))(), reads=[BB[:, 512:642]], writes=[SEL])
                for c in range(2):
                    SELc = SEL[:, c * 65:(c + 1) * 65]
                    gg("dve", (lambda SELc=SELc, FR=FR, T1=T1, c=c: lambda e: e.tensor_scalar(
                        T1[:, 0:16], SELc[:, 0:16], FR[:, 4 + 2 * c:5 + 2 * c],
                        None, ALU.mult))(),
                       reads=[SEL, FR[:, 4:8]], writes=[T1])
                    gg("dve", (lambda SELc=SELc, FR=FR, T1=T1, T2=T2, c=c:
                               lambda e: e.scalar_tensor_tensor(
                                   T2[:, 0:16], SELc[:, 16:32],
                                   FR[:, 5 + 2 * c:6 + 2 * c], T1[:, 0:16],
                                   ALU.mult, ALU.add))(),
                       reads=[SEL, FR[:, 4:8], T1], writes=[T2])
                    gg("dve", (lambda SELc=SELc, T1=T1, T2=T2: lambda e: e.tensor_tensor(
                        T1[:, 0:16], T2[:, 0:16], SELc[:, 32:48], ALU.add))(),
                       reads=[SEL, T2], writes=[T1])
                    gg("act", (lambda T1=T1: lambda e: e.activation(
                        T1[:, 0:16], T1[:, 0:16], AF.Relu))(), reads=[T1], writes=[T1])
                    gg("dve", (lambda SELc=SELc, T1=T1, T2=T2: lambda e: e.tensor_tensor(
                        T2[:, 0:16], T1[:, 0:16], SELc[:, 48:64], ALU.mult))(),
                       reads=[SEL, T1], writes=[T2])
                    gg("dve", (lambda T2=T2, FR=FR: lambda e: e.reduce_sum(
                        FR[:, 0:1], T2[:, 0:16], axis=AX.X))(),
                       reads=[T2], writes=[FR[:, 0:1]])
                    gg("dve", (lambda SELc=SELc, FR=FR: lambda e: e.tensor_scalar(
                        FR[:, 1:2], FR[:, 0:1], SELc[:, 64:65], oscb[:],
                        ALU.add, ALU.mult))(),
                       reads=[FR[:, 0:1], SEL, oscb[:]], writes=[FR[:, 1:2]])
                    gg("dve", (lambda OH=OH, OHS=OHS, FR=FR, c=c: lambda e: e.tensor_scalar(
                        OHS[:, c * 64:(c + 1) * 64], OH[:, c * 64:(c + 1) * 64],
                        FR[:, 1:2], None, ALU.mult))(),
                       reads=[OH, FR[:, 1:2]], writes=[OHS])
                gg("pe", (lambda BB_b=BB_b, OHS=OHS: lambda e: [
                    e.transpose(BB_b[0:64, c * 128:(c + 1) * 128],
                                OHS[:, c * 64:(c + 1) * 64], idn_b[:])
                    for c in range(2)][-1])(),
                   reads=[OHS, idn_b[:]], writes=[BB[:, 0:128]])
                gg("dve", (lambda BB_b=BB_b, O64=O64: lambda e: e.tensor_copy(
                    O64, BB_b[0:64, 0:256]))(), reads=[BB[:, 0:128]], writes=[O64])
                for hw in range(2):
                    gg("pe", (lambda BB=BB, O64=O64, hw=hw: lambda e: [
                        e.matmul(BB[:, mi * 256 + c * 128:mi * 256 + (c + 1) * 128],
                                 dirs_b[0:64, (hw * 4 + mi) * 128:(hw * 4 + mi + 1) * 128],
                                 O64[:, c * 128:(c + 1) * 128], start=True, stop=True)
                        for mi in range(4) for c in range(2)][-1])(),
                       reads=[O64, dirs_b], writes=[BB[:, 0:1024]])
                    fdst = fo[:, b * 2048 + hw * 1024:b * 2048 + (hw + 1) * 1024]
                    gg("act", (lambda BB=BB, fdst=fdst: lambda e: e.copy(
                        fdst, BB[:, 0:1024]))(),
                       reads=[BB[:, 0:1024]], writes=[fdst])

            def subC(b, lastphase=False):
                p = b % 2
                W = Wbuf(p)
                CY = WA[:, 6144 + p * 2048:6144 + (p + 1) * 2048]
                MRS = WA[:, 10240 + p * 1024:10240 + (p + 1) * 1024]
                fblk = fo[:, b * 2048:(b + 1) * 2048]
                SP = SPP[p]
                mr = mrow_s[0:1, p * 512:(p + 1) * 512]
                dma_sp(mr, mrow_d[b:b + 1, :])
                gg("pe", (lambda mr=mr, SP=SP: lambda e: e.matmul(
                    SP[:, 0:512], onesr_f[:], mr,
                    start=True, stop=True))(),
                   reads=[mr, onesr_f[:]], writes=[SP[:, 0:512]])
                gg("act", (lambda SP=SP, MRS=MRS: lambda e: e.copy(
                    MRS[:, 0:512], SP[:, 0:512]))(),
                   reads=[SP[:, 0:512]], writes=[MRS[:, 0:512]])
                wsrc = xres.rearrange("(f p) n -> p f n", p=128)[
                    :, :, b * BLK:(b + 1) * BLK]
                gg("act", (lambda W=W, wsrc=wsrc: lambda e: e.dma_start(
                    out=W.rearrange("p (f n) -> p f n", f=8), in_=wsrc))(),
                   reads=[wsrc], writes=[W], dma=True)
                gg("dve", (lambda W=W, MRS=MRS: lambda e: [
                    e.tensor_tensor(W[:, fc * BLK:(fc + 1) * BLK],
                                    W[:, fc * BLK:(fc + 1) * BLK], MRS[:, 0:256],
                                    ALU.subtract)
                    for fc in range(8)][-1])(),
                   reads=[W, MRS], writes=[W])
                gg("dve", (lambda W=W, MRS=MRS: lambda e: [
                    e.tensor_tensor(W[:, fc * BLK:(fc + 1) * BLK],
                                    W[:, fc * BLK:(fc + 1) * BLK], MRS[:, 256:512],
                                    ALU.mult)
                    for fc in range(8)][-1])(),
                   reads=[W, MRS], writes=[W])
                gg("dve", (lambda W=W, fblk=fblk: lambda e: e.tensor_tensor(
                    W, W, fblk, ALU.add))(),
                   reads=[W, fblk], writes=[W])
                gg("act", (lambda W=W, CY=CY: lambda e: e.copy(CY, W))(),
                   reads=[W], writes=[CY])
                ln_stats3(p, b, xres, CY, have=True, bcast=False)
                m2row = mstat[0:1, p * 512:(p + 1) * 512]
                gg("pe", (lambda m2row=m2row, SP=SP: lambda e: e.matmul(
                    SP[:, 0:512], onesr_f[:], m2row,
                    start=True, stop=True))(),
                   reads=[m2row, onesr_f[:]], writes=[SP[:, 0:512]])
                gg("act", (lambda SP=SP, MRS=MRS: lambda e: e.copy(
                    MRS[:, 512:1024], SP[:, 0:512]))(),
                   reads=[SP[:, 0:512]], writes=[MRS[:, 512:1024]])
                gg("dve", (lambda W=W, MRS=MRS: lambda e: [
                    e.tensor_tensor(W[:, fc * BLK:(fc + 1) * BLK],
                                    W[:, fc * BLK:(fc + 1) * BLK], MRS[:, 512:768],
                                    ALU.subtract)
                    for fc in range(8)][-1])(),
                   reads=[W, MRS], writes=[W])
                gg("dve", (lambda W=W, MRS=MRS: lambda e: [
                    e.tensor_tensor(W[:, fc * BLK:(fc + 1) * BLK],
                                    W[:, fc * BLK:(fc + 1) * BLK], MRS[:, 768:1024],
                                    ALU.mult)
                    for fc in range(8)][-1])(),
                   reads=[W, MRS], writes=[W])
                gg("dve", (lambda W=W: lambda e: [
                    e.tensor_scalar(W[:, fc * BLK:(fc + 1) * BLK],
                                    W[:, fc * BLK:(fc + 1) * BLK],
                                    gam_c[:, fc:fc + 1], bet_c[:, fc:fc + 1],
                                    ALU.mult, ALU.add)
                    for fc in range(8)][-1])(),
                   reads=[W, gam_c[:], bet_c[:]], writes=[W])
                xdst = xres.rearrange("(f p) n -> p f n", p=128)[:, :, b * BLK:(b + 1) * BLK]
                if lastphase:
                    odst = xout.rearrange("(f p) n -> p f n", p=128)[:, :, b * BLK:(b + 1) * BLK]
                    dma_sp(odst, xdst)
                    gg("pool", (lambda W=W, odst=odst: lambda e: e.dma_start(
                        out=odst, in_=W.rearrange("p (f n) -> p f n", f=8),
                        accum_op=ALU.add))(),
                       reads=[W, odst], writes=[odst], dma=True)
                else:
                    gg("pool", (lambda W=W, xdst=xdst: lambda e: e.dma_start(
                        out=xdst, in_=W.rearrange("p (f n) -> p f n", f=8),
                        accum_op=ALU.add))(),
                       reads=[W, xdst], writes=[xdst], dma=True)

        def emit_ffn(app, emit_subA=True, skip_b=0, skip_loads=False):
            lastphase = (app == 2)
            if not skip_loads:
                load_ffn_w()
            if emit_subA:
                for b0 in range(0, NBLK, 2):
                    interleave(collect(subA, b0), collect(subA, b0 + 1))
            for k in range(0, NBLK, 2):
                if k >= skip_b:
                    interleave(collect(subB, k), collect(subB, k + 1))
                if k - 2 >= 0:
                    interleave(collect(subC, k - 2, lastphase),
                               collect(subC, k - 1, lastphase))
            interleave(collect(subC, NBLK - 2, lastphase),
                       collect(subC, NBLK - 1, lastphase))


        if "wo" in PHASES:
            WO = BIG2[:, 16384:32768]
            dma_cast(WO.rearrange("p (c n) -> p c n", c=16),
                     I["woT"].rearrange("(c p) n -> p c n", p=128))
            XBR = [XB1, XB2, XB3]

            def wo_macro(m):
                TRG = BIG2[:, (m % 2) * 8192:(m % 2 + 1) * 8192]
                gg("sp", (lambda m=m, TRG=TRG: lambda e: e.dma_start_transpose(
                    TRG.rearrange("p (c n) -> p c n", c=16),
                    ottok[m * 512:(m + 1) * 512, :]))(),
                   reads=[ottok[m * 512:(m + 1) * 512, :]], writes=[TRG], dma=True)
                xb0 = XBR[(2 * m) % 3]
                xb1 = XBR[(2 * m + 1) % 3]
                for bb, xbuf in ((2 * m, xb0), (2 * m + 1, xb1)):
                    src = I["xT"].rearrange("(f p) n -> p f n", p=128)[
                        :, :, bb * BLK:(bb + 1) * BLK]
                    gg("act", (lambda xbuf=xbuf, src=src: lambda e: e.dma_start(
                        out=xbuf.rearrange("p (f n) -> p f n", f=8), in_=src))(),
                       reads=[src], writes=[xbuf], dma=True)
                for g in range(4):
                    bank = psB[:, (g % 2) * 1024:(g % 2 + 1) * 1024]
                    def womm(e, g=g, bank=bank, TRG=TRG):
                        for mi in range(2):
                            mc = 2 * g + mi
                            for kc in range(16):
                                ins = e.matmul(
                                    bank[:, mi * 512:(mi + 1) * 512],
                                    WO[:, kc * 1024 + mc * 128:kc * 1024 + (mc + 1) * 128],
                                    TRG[:, kc * 512:(kc + 1) * 512],
                                    start=(kc == 0), stop=(kc == 15))
                        return ins
                    gg("pe", womm, reads=[WO, TRG], writes=[bank])
                    def woadd(e, g=g, bank=bank, xb0=xb0, xb1=xb1):
                        for mi in range(2):
                            mc = 2 * g + mi
                            for hh, xbuf in ((0, xb0), (1, xb1)):
                                ins = e.tensor_tensor(
                                    xbuf[:, mc * BLK:(mc + 1) * BLK],
                                    bank[:, mi * 512 + hh * 256:mi * 512 + hh * 256 + 256],
                                    xbuf[:, mc * BLK:(mc + 1) * BLK], ALU.add)
                        return ins
                    gg("dve", woadd, reads=[bank, xb0, xb1],
                       writes=[xb0[:, 2 * g * BLK:(2 * g + 2) * BLK],
                               xb1[:, 2 * g * BLK:(2 * g + 2) * BLK]])
                for bb, xbuf in ((2 * m, xb0), (2 * m + 1, xb1)):
                    dma_sp(xres.rearrange("(f p) n -> p f n", p=128)
                           [:, :, bb * BLK:(bb + 1) * BLK],
                           xbuf.rearrange("p (f n) -> p f n", f=8))

            do_subA = "ffn1" in PHASES
            for m in range(8):
                if m == 0 or not do_subA:
                    interleave(collect(wo_macro, m))
                else:
                    interleave(collect(wo_macro, m),
                               collect(subA, 2 * m - 2), collect(subA, 2 * m - 1))
            if do_subA:
                interleave(collect(subA, 14), collect(subA, 15))

        if "ffn1" in PHASES:
            emit_ffn(1, emit_subA=("wo" not in PHASES))

        if "xbp" in PHASES:
            WB = BIG1[:, 0:8192]
            dma_cast(WB.rearrange("p (c n) -> p c n", c=8),
                     I["wbT"].rearrange("(c p) n -> p c n", p=128))
            xb_ = BIG2[:, 0:8 * T]
            xbv_ = xb_.rearrange("p (f n) -> p f n", f=8)
            XSTB = {0: psA[0:1, 0:256], 1: psA[0:1, 1024:1280]}
            XSTQ = {0: psA[0:1, 512:768], 1: psA[0:1, 1536:1792]}
            XBCM = {0: psA[:, 256:512], 1: psA[:, 1280:1536]}
            XBCR = {0: psA[:, 768:1024], 1: psA[:, 1792:2048]}

            def xbp_ln(p, b, rmul):
                B_p = Bbuf(p)
                dma_cast(B_p.rearrange("p (f n) -> p f n", f=8),
                         xres.rearrange("(f p) n -> p f n", p=128)
                         [:, :, b * BLK:(b + 1) * BLK])
                gg("pe", mmsum8(B_p, XSTB[p]), reads=[B_p, onesc_b[:]], writes=[XSTB[p]])
                gg("pool", (lambda B_p=B_p: lambda e: e.tensor_tensor(
                    B_p, B_p, B_p, ALU.mult))(), reads=[B_p], writes=[B_p])
                gg("pe", mmsum8(B_p, XSTQ[p]), reads=[B_p, onesc_b[:]], writes=[XSTQ[p]])
                m_r, r_r = MS[p]
                gg("dve", (lambda p=p, m_r=m_r: lambda e: e.tensor_scalar_mul(
                    m_r, XSTB[p], 1.0 / D))(), reads=[XSTB[p]], writes=[m_r])
                gg("dve", (lambda m_r=m_r, r_r=r_r: lambda e: e.tensor_tensor(
                    r_r, m_r, m_r, ALU.mult))(), reads=[m_r], writes=[r_r])
                gg("dve", (lambda p=p, r_r=r_r: lambda e: e.scalar_tensor_tensor(
                    r_r, XSTQ[p], 1.0 / D, r_r, ALU.mult, ALU.subtract))(),
                   reads=[XSTQ[p], r_r], writes=[r_r])
                gg("act", (lambda r_r=r_r: lambda e: e.activation(
                    r_r, r_r, AF.Ln, bias=eps_c[:]))(),
                   reads=[r_r, eps_c[:]], writes=[r_r])
                gg("act", (lambda r_r=r_r: lambda e: e.activation(
                    r_r, r_r, AF.Exp, scale=-0.5))(), reads=[r_r], writes=[r_r])
                gg("dve", (lambda r_r=r_r: lambda e: e.tensor_scalar(
                    r_r, r_r, rmul, None, ALU.mult))(),
                   reads=[r_r, rmul], writes=[r_r])
                gg("pe", (lambda p=p, m_r=m_r: lambda e: e.matmul(
                    XBCM[p], onesr_f[:], m_r, start=True, stop=True))(),
                   reads=[m_r, onesr_f[:]], writes=[XBCM[p]])
                gg("pe", (lambda p=p, r_r=r_r: lambda e: e.matmul(
                    XBCR[p], onesr_f[:], r_r, start=True, stop=True))(),
                   reads=[r_r, onesr_f[:]], writes=[XBCR[p]])

            def xbp_block(b):
                p = b % 2
                xbp_ln(p, b, isig[0:1, 1:2])
                B_p = Bbuf(p)
                dma_cast(B_p.rearrange("p (f n) -> p f n", f=8),
                         xres.rearrange("(f p) n -> p f n", p=128)
                         [:, :, b * BLK:(b + 1) * BLK])
                gg("dve", (lambda p=p, B_p=B_p: lambda e: [
                    e.tensor_tensor(B_p[:, fc * BLK:(fc + 1) * BLK],
                                    B_p[:, fc * BLK:(fc + 1) * BLK], XBCM[p], ALU.subtract)
                    for fc in range(8)][-1])(),
                   reads=[B_p, XBCM[p]], writes=[B_p])
                gg("dve", (lambda p=p, B_p=B_p: lambda e: [
                    e.tensor_tensor(B_p[:, fc * BLK:(fc + 1) * BLK],
                                    B_p[:, fc * BLK:(fc + 1) * BLK], XBCR[p], ALU.mult)
                    for fc in range(8)][-1])(),
                   reads=[B_p, XBCR[p]], writes=[B_p])
                bank = psA[:, p * 1024:(p + 1) * 1024]
                dst = xbv_[:, :, b * BLK:(b + 1) * BLK]
                for hw in range(2):
                    def xbmm(e, B_p=B_p, bank=bank, hw=hw):
                        for mi in range(4):
                            mc = hw * 4 + mi
                            for kc in range(8):
                                ins = e.matmul(bank[:, mi * 256:(mi + 1) * 256],
                                               WB[:, kc * 1024 + mc * 128:
                                                  kc * 1024 + (mc + 1) * 128],
                                               B_p[:, kc * BLK:(kc + 1) * BLK],
                                               start=(kc == 0), stop=(kc == 7))
                        return ins
                    gg("pe", xbmm, reads=[WB, B_p], writes=[bank])
                    def xbev(e, dst=dst, bank=bank, hw=hw):
                        for mi in range(4):
                            ins = e.tensor_copy(dst[:, hw * 4 + mi, :],
                                                bank[:, mi * 256:(mi + 1) * 256])
                        return ins
                    gg("dve", xbev, reads=[bank], writes=[dst])
            R = [WA[:, 0:8192], WA[:, 8192:16384], BIG1[:, 8192:16384]]
            def emit_powers():
                dma_sp(R[0].rearrange("p (c n) -> p c n", c=8),
                       atn_d.rearrange("(c p) n -> p c n", p=128))
                dma_sp(R[1].rearrange("p (c n) -> p c n", c=8),
                       ann_d.rearrange("(c p) n -> p c n", p=128))
                emit_powers_body()
            def emit_powers_body():
              i0, i1, i2 = 0, 1, 2
              for lvl in range(8):
                for oc in range(8):
                    ps = psB
                    half = oc % 2
                    pslc = ps[:, half * 1024:(half + 1) * 1024]
                    def sqw(e, oc=oc, i0=i0, i1=i1, ps=ps, half=half):
                        for rc in range(8):
                            for nh in range(2):
                                ins = e.matmul(
                                    ps[:, half * 1024 + nh * 512:half * 1024 + (nh + 1) * 512],
                                    R[i1][:, rc * 1024 + oc * 128:rc * 1024 + (oc + 1) * 128],
                                    R[i0][:, rc * 1024 + nh * 512:rc * 1024 + (nh + 1) * 512],
                                    start=(rc == 0), stop=(rc == 7))
                        return ins
                    gg("pe", sqw, reads=[R[i0], R[i1]], writes=[pslc])
                    dstoc = R[i2][:, oc * 1024:(oc + 1) * 1024]
                    gg("dve", (lambda dstoc=dstoc, pslc=pslc: lambda e: e.tensor_copy(
                        dstoc, pslc))(),
                       reads=[pslc], writes=[dstoc])
                if lvl == 3:
                    dma_sp(m1t_d.rearrange("(c p) n -> p c n", p=128),
                           R[i2].rearrange("p (c n) -> p c n", c=8))
                if lvl == 4:
                    dma_sp(m3t_d.rearrange("(c p) n -> p c n", p=128),
                           R[i2].rearrange("p (c n) -> p c n", c=8))
                if lvl == 7:
                    dma_sp(m2t_d.rearrange("(c p) n -> p c n", p=128),
                           R[i2].rearrange("p (c n) -> p c n", c=8))
                if lvl < 7:
                    natv = R[i1].rearrange("p (c n) -> p c n", c=8)
                    for oc in range(8):
                        for cc2 in range(0, 8, 2):
                            def ftr(e, oc=oc, cc2=cc2, i2=i2):
                                for q in range(2):
                                    ins = e.transpose(
                                        psB_b[:, q * 128:(q + 1) * 128],
                                        R[i2][:, oc * 1024 + (cc2 + q) * 128:
                                              oc * 1024 + (cc2 + q + 1) * 128],
                                        idn_b[:])
                                return ins
                            gg("pe", ftr,
                               reads=[R[i2][:, oc * 1024:(oc + 1) * 1024], idn_b[:]],
                               writes=[psB[:, 0:128]])
                            def fev(e, oc=oc, cc2=cc2, natv=natv):
                                ins = e.tensor_copy(natv[:, cc2, oc * 128:(oc + 1) * 128],
                                                    psB_b[:, 0:128])
                                ins = e.tensor_copy(natv[:, cc2 + 1, oc * 128:(oc + 1) * 128],
                                                    psB_b[:, 128:256])
                                return ins
                            gg("dve", fev, reads=[psB[:, 0:128]],
                               writes=[natv[:, cc2, oc * 128:(oc + 1) * 128],
                                       natv[:, cc2 + 1, oc * 128:(oc + 1) * 128]])
                    i0, i2 = i2, i0

            xb_work = []
            for b0 in range(0, NBLK, 2):
                xb_work += collect(xbp_block, b0) + collect(xbp_block, b0 + 1)
            interleave(xb_work, collect(emit_powers))

        if "scan" in PHASES:
            H1 = SCB[:, 0:2048]
            CCb = SCB[:, 2048:4096]
            AT = WA[:, 0:8192]
            M1T = WA[:, 8192:16384]
            M2T = WA[:, 0:8192]
            dma_sp(AT.rearrange("p (c n) -> p c n", c=8),
                   atn_d.rearrange("(c p) n -> p c n", p=128))
            dma_sp(M1T.rearrange("p (c n) -> p c n", c=8),
                   m1t_d.rearrange("(c p) n -> p c n", p=128))
            xb_ = BIG2[:, 0:8 * T]
            hloc = BIG1[:, 0:8 * T]
            hlv = hloc.rearrange("p (f i j) -> p f i j", f=8, i=C0)
            xbiv = xb_.rearrange("p (f j i) -> p f j i", f=8, i=C0)

            for i in range(C0):
                if i == 0:
                    gg("dve", lambda e: e.tensor_copy(hlv[:, :, 0, :], xbiv[:, :, :, 0]),
                       reads=[xb_], writes=[hlv[:, :, 0, :]])
                    continue
                ps = psA if (i % 2 == 0) else psB
                prev = hlv[:, :, i - 1, :]
                for half in range(2):
                    def a0mm(e, i=i, ps=ps, half=half):
                        for mi in range(4):
                            mc = half * 4 + mi
                            for kc in range(8):
                                ins = e.matmul(ps[:, mc * 256:(mc + 1) * 256],
                                               AT[:, kc * 1024 + mc * 128:kc * 1024 + (mc + 1) * 128],
                                               hloc[:, kc * T + (i - 1) * 256:kc * T + i * 256],
                                               start=(kc == 0), stop=(kc == 7))
                        return ins
                    gg("pe", a0mm, reads=[AT, prev],
                       writes=[ps[:, half * 1024:(half + 1) * 1024]])
                for half in range(2):
                    gg("dve", (lambda i=i, ps=ps, half=half: lambda e: e.tensor_tensor(
                        hlv[:, half * 4:(half + 1) * 4, i, :],
                        ps[:, half * 1024:(half + 1) * 1024]
                        .rearrange("p (q n) -> p q n", q=4),
                        xbiv[:, half * 4:(half + 1) * 4, :, i],
                        ALU.add))(),
                       reads=[ps[:, half * 1024:(half + 1) * 1024], xb_],
                       writes=[hlv[:, half * 4:(half + 1) * 4, i, :]])

            h1v = H1.rearrange("p (f i j) -> p f i j", f=8, i=C1)
            e1full = hlv[:, :, C0 - 1, :]
            e1v = hloc.rearrange("p (f i j2 i2) -> p f i j2 i2", f=8, i=C0, i2=C1)[:, :, C0 - 1, :, :]
            M3T = WA[:, 0:8192]
            dma_sp(M3T.rearrange("p (c n) -> p c n", c=8),
                   m3t_d.rearrange("(c p) n -> p c n", p=128))
            F1 = SCB[:, 2048:4096]
            F1v = F1.rearrange("p (f j2 i2) -> p f j2 i2", f=8, i2=C1)
            def f1mm(e):
                for mc in range(8):
                    for kc in range(8):
                        ins = e.matmul(psB[:, mc * 256:(mc + 1) * 256],
                                       M1T[:, kc * 1024 + mc * 128:kc * 1024 + (mc + 1) * 128],
                                       hloc[:, kc * T + (C0 - 1) * 256:kc * T + C0 * 256],
                                       start=(kc == 0), stop=(kc == 7))
                return ins
            gg("pe", f1mm, reads=[M1T, e1full], writes=[psB[:, 0:2048]])
            gg("dve", lambda e: e.tensor_copy(F1, psB[:, 0:2048]),
               reads=[psB[:, 0:2048]], writes=[F1])
            for i in range(C1):
                if i == 0:
                    gg("dve", lambda e: e.tensor_copy(h1v[:, :, 0, :], e1v[:, :, :, 0]),
                       reads=[e1full], writes=[h1v[:, :, 0, :]])
                    continue
                ps = psA if (i % 2 == 0) else psB
                if i == 1:
                    def a1mm(e, i=i, ps=ps):
                        for mc in range(8):
                            for kc in range(8):
                                ins = e.matmul(ps[:, mc * 16:(mc + 1) * 16],
                                               M1T[:, kc * 1024 + mc * 128:kc * 1024 + (mc + 1) * 128],
                                               H1[:, kc * 256 + (i - 1) * 16:kc * 256 + i * 16],
                                               start=(kc == 0), stop=(kc == 7))
                        return ins
                    gg("pe", a1mm, reads=[M1T, h1v[:, :, i - 1, :]], writes=[ps[:, 0:128]])
                    gg("dve", (lambda i=i, ps=ps: lambda e: e.tensor_tensor(
                        h1v[:, :, i, :], ps[:, 0:128].rearrange("p (f n) -> p f n", f=8),
                        e1v[:, :, :, i], ALU.add))(),
                       reads=[ps[:, 0:128], e1full], writes=[h1v[:, :, i, :]])
                else:
                    def a1mm2(e, i=i, ps=ps):
                        for mc in range(8):
                            for kc in range(8):
                                ins = e.matmul(ps[:, mc * 16:(mc + 1) * 16],
                                               M3T[:, kc * 1024 + mc * 128:kc * 1024 + (mc + 1) * 128],
                                               H1[:, kc * 256 + (i - 2) * 16:kc * 256 + (i - 1) * 16],
                                               start=(kc == 0), stop=(kc == 7))
                        return ins
                    gg("pe", a1mm2, reads=[M3T, h1v[:, :, i - 2, :]], writes=[ps[:, 0:128]])
                    gg("dve", (lambda i=i, ps=ps: lambda e: e.tensor_tensor(
                        h1v[:, :, i, :], ps[:, 0:128].rearrange("p (f n) -> p f n", f=8),
                        F1v[:, :, :, i - 1], ALU.add))(),
                       reads=[ps[:, 0:128], F1], writes=[h1v[:, :, i, :]])
                    gg("dve", (lambda i=i: lambda e: e.tensor_tensor(
                        h1v[:, :, i, :], h1v[:, :, i, :],
                        e1v[:, :, :, i], ALU.add))(),
                       reads=[h1v[:, :, i, :], e1full], writes=[h1v[:, :, i, :]])

            dma_sp(M2T.rearrange("p (c n) -> p c n", c=8),
                   m2t_d.rearrange("(c p) n -> p c n", p=128))
            s2v = S2s[:].rearrange("p (f j) -> p f j", f=8)
            e2v = h1v[:, :, C1 - 1, :]
            for j in range(N1):
                if j == 0:
                    gg("dve", lambda e: e.tensor_copy(s2v[:, :, 0:1], e2v[:, :, 0:1]),
                       reads=[h1v[:, :, C1 - 1, :]], writes=[s2v[:, :, 0:1]])
                    continue
                def l2mm(e, j=j):
                    for mc in range(8):
                        for kc in range(8):
                            ins = e.matmul(psA[:, mc:mc + 1],
                                           M2T[:, kc * 1024 + mc * 128:kc * 1024 + (mc + 1) * 128],
                                           S2s[:, kc * 16 + (j - 1):kc * 16 + j],
                                           start=(kc == 0), stop=(kc == 7))
                    return ins
                gg("pe", l2mm, reads=[M2T, s2v[:, :, j - 1:j]], writes=[psA[:, 0:8]])
                gg("dve", (lambda j=j: lambda e: e.tensor_tensor(
                    s2v[:, :, j:j + 1], psA[:, 0:8].rearrange("p (f o) -> p f o", f=8),
                    e2v[:, :, j:j + 1], ALU.add))(),
                   reads=[psA[:, 0:8], h1v[:, :, C1 - 1, :]], writes=[s2v[:, :, j:j + 1]])

            c1v = C1b[:].rearrange("p (f j) -> p f j", f=8)
            gg("dve", lambda e: e.tensor_copy(c1v[:, :, 1:N1], s2v[:, :, 0:N1 - 1]),
               reads=[S2s[:]], writes=[C1b[:]])
            gg("pool", lambda e: e.memset(c1v[:, :, 0:1], 0.0), writes=[c1v[:, :, 0:1]])
            dma_sp(M3T.rearrange("p (c n) -> p c n", c=8),
                   m3t_d.rearrange("(c p) n -> p c n", p=128))
            for i in range(C1):
                ps = psA if (i % 2 == 0) else psB
                if i < 2:
                    src = C1b if i == 0 else C1c
                    lhs = M1T
                else:
                    src = C1c if i % 2 == 0 else C1b
                    lhs = M3T
                dst = C1c if i % 2 == 0 else C1b
                def c1mm(e, ps=ps, src=src, lhs=lhs):
                    for mc in range(8):
                        for kc in range(8):
                            ins = e.matmul(ps[:, 512 + mc * 16:512 + (mc + 1) * 16],
                                           lhs[:, kc * 1024 + mc * 128:kc * 1024 + (mc + 1) * 128],
                                           src[:, kc * 16:(kc + 1) * 16],
                                           start=(kc == 0), stop=(kc == 7))
                    return ins
                gg("pe", c1mm, reads=[lhs, src[:]], writes=[ps[:, 512:640]])
                gg("dve", (lambda ps=ps, dst=dst: lambda e: e.tensor_copy(
                    dst[:], ps[:, 512:640]))(),
                   reads=[ps[:, 512:640]], writes=[dst[:]])
                gg("dve", (lambda i=i, ps=ps: lambda e: e.tensor_tensor(
                    h1v[:, :, i, :], ps[:, 512:640].rearrange("p (f n) -> p f n", f=8),
                    h1v[:, :, i, :], ALU.add))(),
                   reads=[ps[:, 512:640], h1v[:, :, i, :]], writes=[h1v[:, :, i, :]])

            dma_sp(AT.rearrange("p (c n) -> p c n", c=8),
                   atn_d.rearrange("(c p) n -> p c n", p=128))
            CN = WA[:, 8192:16384]
            dma_sp(CN.rearrange("p (c n) -> p c n", c=8),
                   cnt_d.rearrange("(c p) n -> p c n", p=128))
            ccj = CCb.rearrange("p (f j2 i2) -> p f j2 i2", f=8, i2=16)
            h1j = H1.rearrange("p (f i2 j2) -> p f j2 i2", f=8, i2=16)
            gg("dve", lambda e: e.tensor_copy(ccj[:, :, :, 1:16], h1j[:, :, :, 0:15]),
               reads=[H1], writes=[CCb])
            gg("dve", lambda e: e.tensor_copy(ccj[:, :, 1:16, 0], h1v[:, :, 15, 0:15]),
               reads=[H1], writes=[ccj[:, :, 1:16, 0]])
            gg("pool", lambda e: e.memset(ccj[:, :, 0, 0], 0.0), writes=[ccj[:, :, 0, 0]])

            yv = BIG2[:].rearrange("p (f j i) -> p f j i", f=8, i=C0)

            def c0_step(i):
                def c0mm(e):
                    for mc in range(8):
                        for kc in range(8):
                            ins = e.matmul(psA[:, mc * 256:(mc + 1) * 256],
                                           AT[:, kc * 1024 + mc * 128:kc * 1024 + (mc + 1) * 128],
                                           CCb[:, kc * 256:(kc + 1) * 256],
                                           start=(kc == 0), stop=(kc == 7))
                    return ins
                gg("pe", c0mm, reads=[AT, CCb], writes=[psA[:, 0:2048]])
                gg("dve", lambda e: e.tensor_copy(CCb, psA[:, 0:2048]),
                   reads=[psA[:, 0:2048]], writes=[CCb])
                gg("dve", (lambda i=i: lambda e: e.tensor_tensor(
                    hlv[:, :, i, :],
                    psA[:, 0:2048].rearrange("p (f n) -> p f n", f=8),
                    hlv[:, :, i, :], ALU.add))(),
                   reads=[psA[:, 0:2048], hlv[:, :, i, :]], writes=[hlv[:, :, i, :]])

            def ys_step(i):
                for hw in range(2):
                    bank = psB[:, hw * 1024:(hw + 1) * 1024]
                    def ysmm(e, i=i, hw=hw, bank=bank):
                        for mi in range(4):
                            mc = hw * 4 + mi
                            for kc in range(8):
                                ins = e.matmul(
                                    bank[:, mi * 256:(mi + 1) * 256],
                                    CN[:, kc * 1024 + mc * 128:kc * 1024 + (mc + 1) * 128],
                                    hloc[:, kc * T + i * 256:kc * T + (i + 1) * 256],
                                    start=(kc == 0), stop=(kc == 7))
                        return ins
                    gg("pe", ysmm, reads=[CN, hlv[:, :, i, :]], writes=[bank])
                    ydst = BIG2[:].rearrange(
                        "p (b f jj i) -> p f b jj i", b=NBLK, f=8, i=C0)[
                        :, hw * 4:(hw + 1) * 4, :, :, i]
                    gg("dve", (lambda i=i, hw=hw, bank=bank, ydst=ydst:
                               lambda e: e.tensor_copy(
                        ydst,
                        bank.rearrange("p (q b jj) -> p q b jj", q=4, b=NBLK)))(),
                       reads=[bank], writes=[ydst])

            for i in range(C0):
                interleave(collect(c0_step, i))
                if i >= 1:
                    interleave(collect(ys_step, i - 1))
            interleave(collect(ys_step, C0 - 1))

            ys_subA = "ffn2" in PHASES
            yfv = BIG2[:].rearrange("p (f n) -> p f n", f=8)
            XBR2 = [XB1, XB2]

            def final_pass(b):
                xbuf = XBR2[b % 2]
                yblk = BIG2[:, b * 2048:(b + 1) * 2048]
                src = xres.rearrange("(f p) n -> p f n", p=128)[
                    :, :, b * BLK:(b + 1) * BLK]
                gg("act", (lambda xbuf=xbuf, src=src: lambda e: e.dma_start(
                    out=xbuf.rearrange("p (f n) -> p f n", f=8), in_=src))(),
                   reads=[src], writes=[xbuf], dma=True)
                gg("dve", (lambda xbuf=xbuf, yblk=yblk: lambda e: e.tensor_tensor(
                    xbuf, yblk, xbuf, ALU.add))(),
                   reads=[xbuf, yblk], writes=[xbuf])
                dma_sp(xres.rearrange("(f p) n -> p f n", p=128)
                       [:, :, b * BLK:(b + 1) * BLK],
                       xbuf.rearrange("p (f n) -> p f n", f=8))

            if ys_subA:
                load_ffn_w()
            for b0 in range(0, NBLK, 2):
                streams = [collect(final_pass, b0) + collect(final_pass, b0 + 1)]
                if ys_subA and b0 >= 2:
                    streams += [collect(subA, b0 - 2), collect(subA, b0 - 1)]
                if ys_subA and b0 >= 6:
                    streams += [collect(subB, b0 - 6), collect(subB, b0 - 5)]
                interleave(*streams)
            if ys_subA:
                interleave(collect(subA, NBLK - 2), collect(subA, NBLK - 1),
                           collect(subB, NBLK - 6), collect(subB, NBLK - 5))

        if "ffn2" in PHASES:
            if "scan" in PHASES:
                emit_ffn(2, emit_subA=False, skip_b=12, skip_loads=True)
            else:
                emit_ffn(2)
        else:
            for b in range(NBLK):
                dma_sp(XB3.rearrange("p (f n) -> p f n", f=8),
                       xres.rearrange("(f p) n -> p f n", p=128)[:, :, b * BLK:(b + 1) * BLK])
                dma_sp(xout.rearrange("(f p) n -> p f n", p=128)[:, :, b * BLK:(b + 1) * BLK],
                       XB3.rearrange("p (f n) -> p f n", f=8))

        block = stack.enter_context(nc.Block())
        P.finalize(block)
    return nc


def make_inputs(inputs, core):
    x = inputs["L0_attn_a_x"]
    d = {
        "xT": np.ascontiguousarray(np.asarray(x)[core].T),
        "wqT": np.ascontiguousarray(np.asarray(inputs["L0_attn_a_Wq"]).T),
        "wkvT": np.ascontiguousarray(np.asarray(inputs["L0_attn_a_Wkv_down"]).T),
        "wkuT": np.ascontiguousarray(np.asarray(inputs["L0_attn_a_Wk_up"]).T),
        "wvuT": np.ascontiguousarray(np.asarray(inputs["L0_attn_a_Wv_up"]).T),
        "woT": np.ascontiguousarray(np.asarray(inputs["L0_attn_a_Wo"]).T),
        "wa": np.ascontiguousarray(np.asarray(inputs["L0_attn_b_W_A"])),
        "waT": np.ascontiguousarray(np.asarray(inputs["L0_attn_b_W_A"]).T),
        "wb": np.ascontiguousarray(np.asarray(inputs["L0_attn_b_W_B"])),
        "wbT": np.ascontiguousarray(np.asarray(inputs["L0_attn_b_W_B"]).T),
        "wc": np.ascontiguousarray(np.asarray(inputs["L0_attn_b_W_C"])),
        "wcT": np.ascontiguousarray(np.asarray(inputs["L0_attn_b_W_C"]).T),
        "sraw": np.ascontiguousarray(np.asarray(inputs["L0_ffn_signatures_raw"])),
        "knots": np.asarray(inputs["L0_ffn_knot_values"]).reshape(1, 8),
        "temp": np.asarray(inputs["L0_ffn_temperature"]).reshape(1, 1),
        "cw1": np.ascontiguousarray(np.asarray(inputs["L0_ffn_compress_w1"])),
        "cb1c": np.ascontiguousarray(np.asarray(inputs["L0_ffn_compress_b1"]).reshape(2, 128).T),
        "cw2": np.ascontiguousarray(np.asarray(inputs["L0_ffn_compress_w2"])),
        "cb2": np.asarray(inputs["L0_ffn_compress_b2"]).reshape(2, 1),
        "spl": np.ascontiguousarray(np.concatenate([
            np.asarray(inputs["L0_ffn_spline_w1"])[:, 0, :],
            np.asarray(inputs["L0_ffn_spline_w1"])[:, 1, :],
            np.asarray(inputs["L0_ffn_spline_b1"]),
            np.asarray(inputs["L0_ffn_spline_w2"])[:, :, 0],
            np.asarray(inputs["L0_ffn_spline_b2"])], axis=1)),
        "dirs": np.ascontiguousarray(np.asarray(inputs["L0_ffn_directions"])),
        "gam": np.ascontiguousarray(np.asarray(inputs["L0_ffn_norm_gamma"]).reshape(8, 128).T),
        "bet": np.ascontiguousarray(np.asarray(inputs["L0_ffn_norm_beta"]).reshape(8, 128).T),
        "osc": np.asarray(inputs["L0_ffn_output_scale"]).reshape(1, 1),
        "idn": np.eye(128, dtype=np.float32),
        "onesr": np.ones((1, 128), np.float32),
        "w8": np.tile(np.arange(8, 0, -1, dtype=np.float32)[None, :], (128, 1)),
        "ind8": np.kron(np.eye(8, dtype=np.float32), np.ones((8, 1), np.float32)) / 8.0,
        "epsln": np.full((1, 1), 1e-5, np.float32),
        "epstiny": np.full((1, 1), 1e-30, np.float32),
    }
    return {k: np.ascontiguousarray(np.asarray(v, dtype=np.float32)) for k, v in d.items()}


def kernel(**inputs):
    nc = build_nc()
    in_maps = [make_inputs(inputs, c) for c in range(8)]
    res = run_bass_kernel_spmd(nc, in_maps, core_ids=list(range(8)))
    out = np.stack([np.ascontiguousarray(np.asarray(r["xout"], np.float32).T)
                    for r in res.results], axis=0)
    return out.astype(np.float32)


# version 25
# speedup vs baseline: 1.0149x; 1.0149x over previous
import numpy as np
import ml_dtypes
from contextlib import ExitStack
import concourse.bass as bass
import concourse.mybir as mybir
from concourse.bass_utils import run_bass_kernel_spmd

F32 = mybir.dt.float32
BF16 = mybir.dt.bfloat16
AF = mybir.ActivationFunctionType
ALU = mybir.AluOpType
AX = mybir.AxisListType
BF = ml_dtypes.bfloat16

T = 4096
D = 1024
NH = 8
HD = 256
KVR = 64
BLK = 256
NBLK = T // BLK
C0 = 16
N0 = T // C0
C1 = 16
N1 = N0 // C1

FFN_PIPE = True
ALL_PHASES = ("spec", "ln1", "attn", "wo", "ffn1", "xbp", "scan", "ys", "ffn2")
PHASES = set(ALL_PHASES)
NHEADS_EMIT = NH

_DTSIZE = {F32: 4, BF16: 2, mybir.dt.float16: 2, mybir.dt.int32: 4}


def _span(ap):
    sz = _DTSIZE.get(ap.dtype, 4)
    off = ap.offset
    if not isinstance(off, int):
        off = 0
    dims = list(ap.ap)
    tname = type(ap.tensor).__name__
    if tname in ("SBTensorHandle", "PSumTensorHandle") and len(dims) >= 1:
        pitch = abs(int(dims[0][0])) if dims[0][0] else None
        rest = dims[1:]
        if pitch:
            off = off % pitch
        ext = off
        for step, count in rest:
            ext += abs(int(step)) * (int(count) - 1)
        lo, hi = off * sz, (ext + 1) * sz
        if tname == "PSumTensorHandle":
            lo = (lo // 2048) * 2048
            hi = ((hi + 2047) // 2048) * 2048
        return ap.tensor.name, lo, hi
    ext = off
    for step, count in dims:
        ext += abs(int(step)) * (int(count) - 1)
    return ap.tensor.name, off * sz, (ext + 1) * sz


class Prog:
    ENGS = ("pe", "act", "dve", "pool", "sp")
    NSD = 8

    def __init__(self, nc, stack):
        self.nc = nc
        self.sems = {e: stack.enter_context(nc.semaphore(f"s_{e}")) for e in self.ENGS}
        self.dsems = {}
        for q in ("sp", "pool", "act"):
            for r in range(self.NSD):
                self.dsems[f"d_{q}{r}"] = stack.enter_context(nc.semaphore(f"d_{q}{r}"))
        self.cnt = {e: 0 for e in self.ENGS}
        self.dseq = {"sp": 0, "pool": 0, "act": 0}
        self.dval = {k: 0 for k in self.dsems}
        self.stream = {e: [] for e in self.ENGS}
        self.writers = {}
        self.readers = {}
        self.waited = {e: {} for e in self.ENGS}
        self.sem_by_name = {f"s_{e}": s for e, s in self.sems.items()}
        self.sem_by_name.update(self.dsems)

    def g(self, eng, fn, reads=(), writes=(), accum=False, dma=False):
        pre_wait = None
        if dma:
            q = eng if eng in ("sp", "act") else "pool"
            r = self.dseq[q] % self.NSD
            self.dseq[q] += 1
            my_name = f"d_{q}{r}"
            prev = self.dval[my_name]
            if prev > 0:
                pre_wait = (my_name, prev)
            self.dval[my_name] = prev + 16
            my = (my_name, prev + 16)
            inc_sem, inc_amt = self.dsems[my_name], 16
        else:
            my_name = f"s_{eng}"
            self.cnt[eng] += 1
            my = (my_name, self.cnt[eng])
            inc_sem, inc_amt = self.sems[eng], 1
        waits = {}
        def need(sn, v):
            if waits.get(sn, 0) < v:
                waits[sn] = v
        if pre_wait is not None:
            need(*pre_wait)
        rspans = [_span(a) for a in reads]
        wspans = [_span(a) for a in writes]
        for nm, lo, hi in rspans:
            for wlo, whi, sn, v in self.writers.get(nm, ()):
                if wlo < hi and lo < whi:
                    need(sn, v)
        for nm, lo, hi in wspans:
            for wlo, whi, sn, v in self.writers.get(nm, ()):
                if wlo < hi and lo < whi and not (accum and sn == my_name):
                    need(sn, v)
            for rlo, rhi, sn, v in self.readers.get(nm, ()):
                if rlo < hi and lo < rhi:
                    need(sn, v)
        w2 = {}
        for sn, v in waits.items():
            if self.waited[eng].get(sn, -1) >= v:
                continue
            w2[sn] = v
            self.waited[eng][sn] = v
        for nm, lo, hi in wspans:
            wl = self.writers.setdefault(nm, [])
            wl[:] = [e_ for e_ in wl if not (lo <= e_[0] and e_[1] <= hi)]
            wl.append((lo, hi, my_name, my[1]))
            rl = self.readers.setdefault(nm, [])
            rl[:] = [e_ for e_ in rl if not (lo <= e_[0] and e_[1] <= hi)]
        for nm, lo, hi in rspans:
            self.readers.setdefault(nm, []).append((lo, hi, my_name, my[1]))
        self.stream[eng].append((w2, fn, inc_sem, inc_amt))
        return my

    def run_stream(self, eng, handle):
        for w2, fn, inc_sem, inc_amt in self.stream[eng]:
            for sn, v in w2.items():
                handle.wait_ge(self.sem_by_name[sn], v)
            ins = fn(handle)
            assert ins is not None, f"group on {eng} emitted no instruction"
            ins.then_inc(inc_sem, inc_amt)

    def finalize(self, block):
        p = self

        @block.tensor
        def _(t):
            p.run_stream("pe", t)

        @block.scalar
        def _(s):
            p.run_stream("act", s)
            for r in range(p.NSD):
                nm = f"d_act{r}"
                if p.dval[nm]:
                    s.wait_ge(p.dsems[nm], p.dval[nm])

        @block.vector
        def _(v):
            p.run_stream("dve", v)

        @block.gpsimd
        def _(g):
            p.run_stream("pool", g)
            for r in range(p.NSD):
                nm = f"d_pool{r}"
                if p.dval[nm]:
                    g.wait_ge(p.dsems[nm], p.dval[nm])

        @block.sync
        def _(sy):
            p.run_stream("sp", sy)
            for r in range(p.NSD):
                nm = f"d_sp{r}"
                if p.dval[nm]:
                    sy.wait_ge(p.dsems[nm], p.dval[nm])


def build_nc():
    nc = bass.Bass("TRN2", target_bir_lowering=False, debug=False)
    I = {}
    def inp(name, shape, dt=F32):
        I[name] = nc.dram_tensor(name, shape, dt, kind="ExternalInput").ap()
        return I[name]

    inp("xT", [D, T])
    inp("wqT", [D, NH * HD]); inp("wkvT", [D, KVR])
    inp("wkuT", [KVR, NH * HD]); inp("wvuT", [KVR, NH * HD])
    inp("woT", [NH * HD, D])
    inp("wa", [D, D]); inp("waT", [D, D])
    inp("wb", [D, D]); inp("wbT", [D, D])
    inp("wc", [D, D]); inp("wcT", [D, D])
    inp("sraw", [64, D]); inp("knots", [1, 8]); inp("temp", [1, 1])
    inp("cw1", [D, 256]); inp("cb1c", [128, 2]); inp("cw2", [256, 2])
    inp("cb2", [2, 1]); inp("spl", [64, 65]); inp("dirs", [64, D])
    inp("gam", [128, 8]); inp("bet", [128, 8]); inp("osc", [1, 1])
    inp("idn", [128, 128]); inp("onesr", [1, 128]); inp("w8", [128, 8])
    inp("ind8", [64, 8])
    inp("epsln", [1, 1]); inp("epstiny", [1, 1])

    xout = nc.dram_tensor("xout", [D, T], F32, kind="ExternalOutput").ap()
    xres = nc.dram_tensor("xres", [D, T], F32, kind="Internal").ap()
    ottok = nc.dram_tensor("ottok", [T, NH * HD], BF16, kind="Internal").ap()
    atn_d = nc.dram_tensor("atn_d", [D, D], BF16, kind="Internal").ap()
    ann_d = nc.dram_tensor("ann_d", [D, D], BF16, kind="Internal").ap()
    m1t_d = nc.dram_tensor("m1t_d", [D, D], BF16, kind="Internal").ap()
    m2t_d = nc.dram_tensor("m2t_d", [D, D], BF16, kind="Internal").ap()
    cnt_d = nc.dram_tensor("cnt_d", [D, D], BF16, kind="Internal").ap()
    mrow_d = nc.dram_tensor("mrow_d", [16, 512], F32, kind="Internal").ap()
    m3t_d = nc.dram_tensor("m3t_d", [D, D], BF16, kind="Internal").ap()

    with ExitStack() as stack:
        def sbt(name, shape, dt=F32):
            return stack.enter_context(nc.sbuf_tensor(name, shape, dt))
        def pst(name, shape, dt=F32):
            return stack.enter_context(nc.psum_tensor(name, shape, dt))

        idn_f = sbt("idn_f", [128, 72])
        idn_b = sbt("idn_b", [128, 128], BF16)
        onesr_f = sbt("onesr_f", [1, 128])
        onesc_f = sbt("onesc_f", [128, 1])
        onesc_b = sbt("onesc_b", [128, 1], BF16)
        w8_f = sbt("w8_f", [128, 8])
        eps_c = sbt("eps_c", [1, 1])
        tiny_c = sbt("tiny_c", [1, 1])
        dkn = sbt("dkn", [128, 8])
        itemp = sbt("itemp", [128, 1])
        oscb = sbt("oscb", [128, 1])
        crow = sbt("crow", [128, 17])
        mrow_s = sbt("mrow_s", [1, 1024])
        gam_c = sbt("gam_c", [128, 8])
        bet_c = sbt("bet_c", [128, 8])
        cb1_c = sbt("cb1_c", [128, 2])
        cb1e_c = sbt("cb1e_c", [128, 2])
        cb2_c = sbt("cb2_c", [2, 1])
        sigsT = sbt("sigsT", [128, 8 * 72], BF16)
        spl_b = sbt("spl_b", [64, 65], BF16)
        cw2_b = sbt("cw2_b", [128, 2 * 2], BF16)
        srow = sbt("srow", [1, 64])
        isig = sbt("isig", [128, 4])
        uvb = sbt("uvb", [128, 6 * 8], BF16)
        uvf = sbt("uvf", [128, 3 * 24])
        negC = sbt("negC", [128, 1])
        mstat = sbt("mstat", [1, 4 * BLK])
        RCt = sbt("RCt", [128, 8])
        FRC = sbt("FRC", [128, 8])
        S2s = sbt("S2s", [128, 8 * N1], BF16)
        C1b = sbt("C1b", [128, 8 * N1], BF16)
        C1c = sbt("C1c", [128, 8 * N1], BF16)

        BIG1 = sbt("BIG1", [128, 32768], BF16)
        BIG2 = sbt("BIG2", [128, 32768], BF16)
        WA = sbt("WA", [128, 16384], BF16)
        SCR = sbt("SCR", [128, 6144])
        SCB = sbt("SCB", [128, 4096], BF16)
        XB1 = SCR[:, 0:2048]
        XB2 = SCR[:, 2048:4096]
        XB3 = SCR[:, 4096:6144]
        XBb = SCB[:, 0:2048]
        SQb = SCB[:, 2048:4096]

        psA = pst("psA", [128, 2048])
        psB = pst("psB", [128, 2048])
        psA_b = psA.bitcast(BF16)
        psB_b = psB.bitcast(BF16)

        P = Prog(nc, stack)
        _defer = [None]

        def gg(*a, **k):
            if _defer[0] is not None:
                _defer[0].append((a, k))
                return None
            return P.g(*a, **k)

        def collect(fn, *args, **kw):
            buf = []
            _defer[0] = buf
            try:
                fn(*args, **kw)
            finally:
                _defer[0] = None
            return buf

        def interleave(*bufs):
            n = max(len(b) for b in bufs)
            for i in range(n):
                for b in bufs:
                    if i < len(b):
                        a, k = b[i]
                        P.g(*a, **k)

        def dma_sp(dst, src, extra_reads=()):
            return gg("sp", lambda e: e.dma_start(out=dst, in_=src),
                      reads=[src] + list(extra_reads), writes=[dst], dma=True)

        def dma_cast(dst, src, extra_reads=()):
            return gg("pool", lambda e: e.dma_start(out=dst, in_=src),
                      reads=[src] + list(extra_reads), writes=[dst], dma=True)

        dma_sp(idn_f[:], I["idn"][:, 0:72])
        dma_cast(idn_b[:], I["idn"])
        dma_sp(onesr_f[:], I["onesr"])
        dma_sp(onesc_f[:], I["onesr"].rearrange("o p -> p o"))
        dma_cast(onesc_b[:], I["onesr"].rearrange("o p -> p o"))
        dma_sp(w8_f[:], I["w8"])
        dma_sp(eps_c[:], I["epsln"])
        dma_sp(tiny_c[:], I["epstiny"])
        dma_sp(gam_c[:], I["gam"])
        dma_sp(bet_c[:], I["bet"])
        dma_sp(cb1_c[:], I["cb1c"])
        dma_sp(cb2_c[:], I["cb2"])
        dma_cast(spl_b[:], I["spl"])
        dma_cast(cw2_b[:].rearrange("p (c n) -> p c n", c=2),
                 I["cw2"].rearrange("(c p) n -> p c n", p=128))
        dma_sp(srow[0:1, 0:8], I["knots"])
        dma_sp(srow[0:1, 8:9], I["temp"])
        dma_sp(srow[0:1, 9:10], I["osc"])
        dma_sp(SCR[0:64, 2048:3072], I["sraw"])

        gg("dve", lambda e: e.tensor_scalar_mul(cb1e_c[:], cb1_c[:], 0.7071067811865476),
           reads=[cb1_c[:]], writes=[cb1e_c[:]])
        gg("dve", lambda e: e.tensor_tensor(srow[0:1, 16:23], srow[0:1, 1:8],
                                            srow[0:1, 0:7], ALU.subtract),
           reads=[srow[0:1, 0:8]], writes=[srow[0:1, 16:23]])
        gg("dve", lambda e: e.tensor_scalar_add(srow[0:1, 26:27], srow[0:1, 8:9], 1e-6),
           reads=[srow[0:1, 8:9]], writes=[srow[0:1, 26:27]])
        gg("dve", lambda e: e.reciprocal(srow[0:1, 24:25], srow[0:1, 26:27]),
           reads=[srow[0:1, 26:27]], writes=[srow[0:1, 24:25]])
        gg("dve", lambda e: e.tensor_copy(srow[0:1, 23:24], srow[0:1, 0:1]),
           reads=[srow[0:1, 0:1]], writes=[srow[0:1, 23:24]])
        gg("dve", lambda e: e.tensor_copy(srow[0:1, 25:26], srow[0:1, 9:10]),
           reads=[srow[0:1, 9:10]], writes=[srow[0:1, 25:26]])
        gg("pe", lambda e: e.matmul(psA[:, 0:10], onesr_f[:], srow[0:1, 16:26],
                                    start=True, stop=True),
           reads=[srow[0:1, 16:26], onesr_f[:]], writes=[psA[:, 0:10]])
        gg("dve", lambda e: e.tensor_copy(dkn[:, 0:8], psA[:, 0:8]),
           reads=[psA[:, 0:10]], writes=[dkn[:, 0:8]])
        gg("dve", lambda e: e.tensor_copy(itemp[:], psA[:, 8:9]),
           reads=[psA[:, 0:10]], writes=[itemp[:]])
        gg("dve", lambda e: e.tensor_copy(oscb[:], psA[:, 9:10]),
           reads=[psA[:, 0:10]], writes=[oscb[:]])
        gg("dve", lambda e: e.tensor_copy(srow[0:1, 27:28], srow[0:1, 16:17]),
           reads=[srow[0:1, 16:17]], writes=[srow[0:1, 27:28]])
        gg("dve", lambda e: e.tensor_tensor(srow[0:1, 28:34], srow[0:1, 17:23],
                                            srow[0:1, 16:22], ALU.subtract),
           reads=[srow[0:1, 16:23]], writes=[srow[0:1, 28:34]])
        gg("dve", lambda e: e.tensor_scalar_mul(srow[0:1, 34:35], srow[0:1, 22:23], -1.0),
           reads=[srow[0:1, 22:23]], writes=[srow[0:1, 34:35]])
        gg("pe", lambda e: e.matmul(psA[:, 32:40], onesr_f[:], srow[0:1, 27:35],
                                    start=True, stop=True),
           reads=[srow[0:1, 27:35], onesr_f[:]], writes=[psA[:, 32:40]])
        gg("dve", lambda e: e.tensor_copy(crow[:, 0:8], psA[:, 32:40]),
           reads=[psA[:, 32:40]], writes=[crow[:, 0:8]])
        gg("dve", lambda e: e.tensor_scalar_add(crow[:, 8:16], w8_f[:], -4.5),
           reads=[w8_f[:]], writes=[crow[:, 8:16]])
        gg("dve", lambda e: e.tensor_scalar_mul(crow[:, 16:17], itemp[:], 0.5),
           reads=[itemp[:]], writes=[crow[:, 16:17]])

        sraw_sb = SCR[0:64, 2048:3072]
        gg("dve", lambda e: e.tensor_single_scalar(XB1[0:64, 0:D], sraw_sb, 0.3, ALU.is_gt),
           reads=[sraw_sb], writes=[XB1[0:64, 0:D]])
        gg("dve", lambda e: e.tensor_single_scalar(XB3[0:64, 0:D], sraw_sb, -0.3, ALU.is_lt),
           reads=[sraw_sb], writes=[XB3[0:64, 0:D]])
        gg("dve", lambda e: e.tensor_tensor(XB1[0:64, 1024:2048], XB1[0:64, 0:D],
                                            XB3[0:64, 0:D], ALU.subtract),
           reads=[XB1[0:64, 0:D], XB3[0:64, 0:D]], writes=[XB1[0:64, 1024:2048]])
        gg("pool", lambda e: e.tensor_copy(SCB[0:64, 0:D], XB1[0:64, 1024:2048]),
           reads=[XB1[0:64, 1024:2048]], writes=[SCB[0:64, 0:D]])
        dma_cast(SCB[0:64, 3072:3080], I["ind8"])
        def csig_mm(e):
            for n in range(2):
                ins = e.matmul(psA[0:8, 512 + n * 512:512 + (n + 1) * 512],
                               SCB[0:64, 3072:3080], SCB[0:64, n * 512:(n + 1) * 512],
                               start=True, stop=True)
            return ins
        gg("pe", csig_mm, reads=[SCB[0:64, 3072:3080], SCB[0:64, 0:D]],
           writes=[psA[0:8, 512:1536]])
        gg("dve", lambda e: e.tensor_single_scalar(XB1[0:8, 0:D], psA[0:8, 512:1536],
                                                   0.0, ALU.is_gt),
           reads=[psA[0:8, 512:1536]], writes=[XB1[0:8, 0:D]])
        gg("dve", lambda e: e.tensor_single_scalar(XB3[0:8, 0:D], psA[0:8, 512:1536],
                                                   0.0, ALU.is_lt),
           reads=[psA[0:8, 512:1536]], writes=[XB3[0:8, 0:D]])
        gg("dve", lambda e: e.tensor_tensor(XB1[0:8, 1024:2048], XB1[0:8, 0:D],
                                            XB3[0:8, 0:D], ALU.subtract),
           reads=[XB1[0:8, 0:D], XB3[0:8, 0:D]], writes=[XB1[0:8, 1024:2048]])
        gg("pool", lambda e: e.tensor_copy(SCB[64:72, 0:D], XB1[0:8, 1024:2048]),
           reads=[XB1[0:8, 1024:2048]], writes=[SCB[64:72, 0:D]])
        for fc in range(8):
            gg("pe", (lambda fc=fc: lambda e: e.transpose(
                psA_b[:, 0:64], SCB[0:64, fc * 128:(fc + 1) * 128], idn_b[0:64, 0:64]))(),
               reads=[SCB[0:64, 0:D], idn_b[:]], writes=[psA[:, 0:32]])
            gg("pe", (lambda fc=fc: lambda e: e.transpose(
                psA_b[:, 64:72], SCB[64:72, fc * 128:(fc + 1) * 128], idn_b[64:72, 64:72]))(),
               reads=[SCB[64:72, 0:D], idn_b[:]], writes=[psA[:, 32:36]])
            gg("dve", (lambda fc=fc: lambda e: e.tensor_copy(
                sigsT[:, fc * 72:(fc + 1) * 72], psA_b[:, 0:72]))(),
               reads=[psA[:, 0:36]], writes=[sigsT[:, fc * 72:(fc + 1) * 72]])

        def colsum_f32(src_col, dst_ps):
            gg("pe", lambda e: e.matmul(dst_ps, onesc_f[:], src_col, start=True, stop=True),
               reads=[src_col, onesc_f[:]], writes=[dst_ps])

        def bcast_col(dst_col, src11, ps):
            gg("pe", lambda e: e.matmul(ps, onesr_f[:], src11, start=True, stop=True),
               reads=[src11, onesr_f[:]], writes=[ps])
            gg("dve", lambda e: e.tensor_copy(dst_col, ps),
               reads=[ps], writes=[dst_col])

        def load_x_block(b, src, want16=True):
            dma_sp(XB1.rearrange("p (f n) -> p f n", f=8),
                   src.rearrange("(f p) n -> p f n", p=128)[:, :, b * BLK:(b + 1) * BLK])
            if want16:
                dma_cast(XBb.rearrange("p (f n) -> p f n", f=8),
                         src.rearrange("(f p) n -> p f n", p=128)[:, :, b * BLK:(b + 1) * BLK])

        STB = {0: psB[0:1, 1024:1280], 1: psB[0:1, 1280:1536]}
        STQ = {0: psB[0:1, 1536:1792], 1: psB[0:1, 1792:2048]}
        BCM = {0: psB[:, 0:256], 1: psB[:, 1024:1280]}
        BCR = {0: psB[:, 512:768], 1: psB[:, 1536:1792]}
        MS = {0: (mstat[0:1, 0:256], mstat[0:1, 256:512]),
              1: (mstat[0:1, 512:768], mstat[0:1, 768:1024])}
        def Wbuf(p):
            return XB1 if p == 0 else XB2
        def Bbuf(p):
            return SCB[:, p * 2048:(p + 1) * 2048]

        def mmsum8(src, ps):
            def f(e):
                for fc in range(8):
                    ins = e.matmul(ps, onesc_b[:], src[:, fc * BLK:(fc + 1) * BLK],
                                   start=(fc == 0), stop=(fc == 7))
                return ins
            return f

        def ln_stats2(p, b, src_dram, rmul=None, have_b16=False, bcast=True):
            B_p = Bbuf(p)
            if not have_b16:
                dma_cast(B_p.rearrange("p (f n) -> p f n", f=8),
                         src_dram.rearrange("(f p) n -> p f n", p=128)
                         [:, :, b * BLK:(b + 1) * BLK])
            gg("pe", mmsum8(B_p, STB[p]), reads=[B_p, onesc_b[:]], writes=[STB[p]])
            gg("pool", lambda e: e.tensor_tensor(B_p, B_p, B_p, ALU.mult),
               reads=[B_p], writes=[B_p])
            gg("pe", mmsum8(B_p, STQ[p]), reads=[B_p, onesc_b[:]], writes=[STQ[p]])
            m_r, r_r = MS[p]
            gg("dve", (lambda p=p, m_r=m_r: lambda e: e.tensor_scalar_mul(
                m_r, STB[p], 1.0 / D))(), reads=[STB[p]], writes=[m_r])
            gg("dve", (lambda m_r=m_r, r_r=r_r: lambda e: e.tensor_tensor(
                r_r, m_r, m_r, ALU.mult))(), reads=[m_r], writes=[r_r])
            gg("dve", (lambda p=p, r_r=r_r: lambda e: e.scalar_tensor_tensor(
                r_r, STQ[p], 1.0 / D, r_r, ALU.mult, ALU.subtract))(),
               reads=[STQ[p], r_r], writes=[r_r])
            gg("act", (lambda r_r=r_r: lambda e: e.activation(
                r_r, r_r, AF.Ln, bias=eps_c[:]))(),
               reads=[r_r, eps_c[:]], writes=[r_r])
            gg("act", (lambda r_r=r_r: lambda e: e.activation(
                r_r, r_r, AF.Exp, scale=-0.5))(), reads=[r_r], writes=[r_r])
            if rmul is not None:
                gg("dve", (lambda r_r=r_r: lambda e: e.tensor_scalar(
                    r_r, r_r, rmul, None, ALU.mult))(),
                   reads=[r_r, rmul], writes=[r_r])
            if not bcast:
                return
            gg("pe", (lambda p=p, m_r=m_r: lambda e: e.matmul(
                BCM[p], onesr_f[:], m_r, start=True, stop=True))(),
               reads=[m_r, onesr_f[:]], writes=[BCM[p]])
            gg("pe", (lambda p=p, r_r=r_r: lambda e: e.matmul(
                BCR[p], onesr_f[:], r_r, start=True, stop=True))(),
               reads=[r_r, onesr_f[:]], writes=[BCR[p]])

        def ln_apply16(p, src, dst16v):
            gg("dve", (lambda p=p, src=src: lambda e: [
                e.tensor_tensor(src[:, fc * BLK:(fc + 1) * BLK],
                                src[:, fc * BLK:(fc + 1) * BLK], BCM[p], ALU.subtract)
                for fc in range(8)][-1])(),
               reads=[src, BCM[p]], writes=[src])
            gg("dve", (lambda p=p, src=src, dst16v=dst16v: lambda e: [
                e.tensor_tensor(dst16v[:, fc, :],
                                src[:, fc * BLK:(fc + 1) * BLK], BCR[p], ALU.mult)
                for fc in range(8)][-1])(),
               reads=[src, BCR[p]], writes=[dst16v])

        xn = BIG1[:, 0:8 * T]
        xnv = xn.rearrange("p (f n) -> p f n", f=8)

        def ln1_stats(b):
            p = b % 2
            ln_stats2(p, b, I["xT"], bcast=False)
            dma_sp(mrow_d[b:b + 1, :], mstat[0:1, p * 512:(p + 1) * 512])

        def ln1_apply(b):
            p = b % 2
            mr = mrow_s[0:1, p * 512:(p + 1) * 512]
            dma_sp(mr, mrow_d[b:b + 1, :])
            gg("pe", (lambda p=p, mr=mr: lambda e: e.matmul(
                BCM[p], onesr_f[:], mr[0:1, 0:256], start=True, stop=True))(),
               reads=[mr, onesr_f[:]], writes=[BCM[p]])
            gg("pe", (lambda p=p, mr=mr: lambda e: e.matmul(
                BCR[p], onesr_f[:], mr[0:1, 256:512], start=True, stop=True))(),
               reads=[mr, onesr_f[:]], writes=[BCR[p]])
            B_p = Bbuf(p)
            dma_cast(B_p.rearrange("p (f n) -> p f n", f=8),
                     I["xT"].rearrange("(f p) n -> p f n", p=128)
                     [:, :, b * BLK:(b + 1) * BLK])
            ln_apply16(p, B_p, xnv[:, :, b * BLK:(b + 1) * BLK])

        if "spec" in PHASES:
            SPW = {0: (WA[:, 0:8192], WA[:, 8192:16384]),
                   1: (BIG1[:, 0:8192], BIG1[:, 8192:16384]),
                   2: (BIG2[:, 0:8192], BIG2[:, 8192:16384])}

            def spec_weight(wi, wn_name, wt_name):
                wnat, wtr = SPW[wi]
                ub = uvb[:, wi * 16:wi * 16 + 16]
                uf = uvf[:, wi * 24:(wi + 1) * 24]
                pmv = psA[:, wi * 512:wi * 512 + 8]
                pcs = psA[0:1, wi * 512 + 16:wi * 512 + 17]
                pbc = psA[:, wi * 512 + 17:wi * 512 + 18]
                pds = psA[0:1, wi * 512 + 18:wi * 512 + 19]
                pbc2 = psA[:, wi * 512 + 19:wi * 512 + 20]
                sr1 = srow[0:1, 32 + wi:33 + wi]
                sr2 = srow[0:1, 36 + wi:37 + wi]
                fcol = FRC[:, 4 + wi:5 + wi]
                dma_cast(wnat.rearrange("p (c n) -> p c n", c=8),
                         I[wn_name].rearrange("(c p) n -> p c n", p=128))
                dma_cast(wtr.rearrange("p (c n) -> p c n", c=8),
                         I[wt_name].rearrange("(c p) n -> p c n", p=128))
                gg("pool", (lambda ub=ub: lambda e: e.memset(ub[:, 0:8], 0.03125))(),
                   writes=[ub[:, 0:8]])

                def matvec(lhs, src_sl):
                    def f(e, lhs=lhs, src_sl=src_sl):
                        for mc in range(8):
                            for kc in range(8):
                                ins = e.matmul(pmv[:, mc:mc + 1],
                                               lhs[:, kc * 1024 + mc * 128:
                                                   kc * 1024 + (mc + 1) * 128],
                                               ub[:, src_sl * 8 + kc:src_sl * 8 + kc + 1],
                                               start=(kc == 0), stop=(kc == 7))
                        return ins
                    gg("pe", f, reads=[lhs, ub[:, src_sl * 8:(src_sl + 1) * 8]],
                       writes=[pmv])

                def normalize(dst_sl):
                    gg("dve", lambda e: e.tensor_copy(uf[:, 0:8], pmv),
                       reads=[pmv], writes=[uf[:, 0:8]])
                    gg("dve", lambda e: e.tensor_tensor(uf[:, 8:16], uf[:, 0:8],
                                                        uf[:, 0:8], ALU.mult),
                       reads=[uf[:, 0:8]], writes=[uf[:, 8:16]])
                    gg("dve", lambda e: e.reduce_sum(uf[:, 16:17], uf[:, 8:16], axis=AX.X),
                       reads=[uf[:, 8:16]], writes=[uf[:, 16:17]])
                    gg("pe", lambda e: e.matmul(pcs, onesc_f[:], uf[:, 16:17],
                                                start=True, stop=True),
                       reads=[uf[:, 16:17], onesc_f[:]], writes=[pcs])
                    gg("act", lambda e: e.activation(sr1, pcs, AF.Ln, bias=tiny_c[:]),
                       reads=[pcs, tiny_c[:]], writes=[sr1])
                    gg("act", lambda e: e.activation(sr1, sr1, AF.Exp, scale=-0.5),
                       reads=[sr1], writes=[sr1])
                    gg("pe", lambda e: e.matmul(pbc, onesr_f[:], sr1, start=True, stop=True),
                       reads=[sr1, onesr_f[:]], writes=[pbc])
                    gg("dve", lambda e: e.tensor_copy(fcol, pbc),
                       reads=[pbc], writes=[fcol])
                    gg("dve", (lambda dst_sl=dst_sl: lambda e: e.tensor_scalar(
                        ub[:, dst_sl * 8:(dst_sl + 1) * 8], uf[:, 0:8],
                        fcol, None, ALU.mult))(),
                       reads=[uf[:, 0:8], fcol],
                       writes=[ub[:, dst_sl * 8:(dst_sl + 1) * 8]])

                for it in range(10):
                    matvec(wnat, 0); normalize(1)
                    matvec(wtr, 1); normalize(0)
                matvec(wnat, 0); normalize(1)
                matvec(wtr, 1)
                gg("dve", lambda e: e.tensor_copy(uf[:, 0:8], pmv),
                   reads=[pmv], writes=[uf[:, 0:8]])
                gg("dve", lambda e: e.tensor_tensor(uf[:, 8:16], uf[:, 0:8],
                                                    ub[:, 0:8], ALU.mult),
                   reads=[uf[:, 0:8], ub[:, 0:8]], writes=[uf[:, 8:16]])
                gg("dve", lambda e: e.reduce_sum(uf[:, 16:17], uf[:, 8:16], axis=AX.X),
                   reads=[uf[:, 8:16]], writes=[uf[:, 16:17]])
                gg("pe", lambda e: e.matmul(pds, onesc_f[:], uf[:, 16:17],
                                            start=True, stop=True),
                   reads=[uf[:, 16:17], onesc_f[:]], writes=[pds])
                gg("dve", lambda e: e.reciprocal(sr2, pds),
                   reads=[pds], writes=[sr2])
                gg("pe", lambda e: e.matmul(pbc2, onesr_f[:], sr2, start=True, stop=True),
                   reads=[sr2, onesr_f[:]], writes=[pbc2])
                gg("dve", (lambda wi=wi: lambda e: e.tensor_copy(
                    isig[:, wi:wi + 1], pbc2))(),
                   reads=[pbc2], writes=[isig[:, wi:wi + 1]])

            spec_streams = [collect(spec_weight, 0, "wa", "waT"),
                            collect(spec_weight, 1, "wb", "wbT"),
                            collect(spec_weight, 2, "wc", "wcT")]
            if "ln1" in PHASES:
                def ln1_stats_run(par):
                    for b in range(par, NBLK, 2):
                        ln1_stats(b)
                spec_streams += [collect(ln1_stats_run, 0), collect(ln1_stats_run, 1)]
            interleave(*spec_streams)

            for (srcname, dstd, sg) in (("waT", atn_d, 0), ("wa", ann_d, 0),
                                        ("wcT", cnt_d, 2)):
                for hh in range(2):
                    half_src = I[srcname].rearrange("(c p) n -> p c n", p=128)[:, hh * 4:(hh + 1) * 4, :]
                    dma_sp(SCR[:, 0:4096].rearrange("p (c n) -> p c n", c=4), half_src)
                    gg("dve", (lambda sg=sg: lambda e: e.tensor_scalar(
                        SCB[:, 0:4096], SCR[:, 0:4096], isig[:, sg:sg + 1], None, ALU.mult))(),
                       reads=[SCR[:, 0:4096], isig[:, sg:sg + 1]], writes=[SCB[:, 0:4096]])
                    dma_sp(dstd.rearrange("(c p) n -> p c n", p=128)[:, hh * 4:(hh + 1) * 4, :],
                           SCB[:, 0:4096].rearrange("p (c n) -> p c n", c=4))

        if "ln1" in PHASES and "spec" not in PHASES:
            for b0 in range(0, NBLK, 2):
                interleave(collect(ln1_stats, b0), collect(ln1_stats, b0 + 1))
        if "ln1" in PHASES and "attn" not in PHASES:
            for b0 in range(0, NBLK, 2):
                interleave(collect(ln1_apply, b0), collect(ln1_apply, b0 + 1))

        if "attn" in PHASES:
            qT = BIG2[:, 0:2 * T]
            kT = BIG2[:, 2 * T:4 * T]
            VV = BIG2[:, 4 * T:4 * T + 32 * 257]
            WQ = WA[:, 0:4096]
            LAT = WA[0:64, 4096:8192]
            PT = BIG2[:, 24608:28704]
            OB = WA[:, 10240:11264]
            KUh = WA[0:64, 11264:11520]
            VUh = WA[0:64, 11520:11776]
            kvT_b = WA[:, 11776:12288]
            dma_cast(kvT_b.rearrange("p (c n) -> p c n", c=8),
                     I["wkvT"].rearrange("(c p) n -> p c n", p=128))

            def vones(e):
                for tc in range(32):
                    ins = e.memset(VV[:, tc * 257 + 256:tc * 257 + 257], 1.0)
                return ins
            gg("pool", vones, writes=[BIG2[:, 4 * T:4 * T + 32 * 257]])

            def load_head_w(h):
                hp = h % 2
                WQh_ = WQ[:, hp * 2048:(hp + 1) * 2048]
                dma_cast(WQh_.rearrange("p (c n) -> p c n", c=8),
                         I["wqT"].rearrange("(c p) n -> p c n", p=128)[:, :, h * HD:(h + 1) * HD])
                dma_cast(KUh, I["wkuT"][:, h * HD:(h + 1) * HD])
                dma_cast(VUh, I["wvuT"][:, h * HD:(h + 1) * HD])
            load_head_w(0)

            def attn_nb(nb):
                WQh = WQ[:, 0:2048]
                bk = [psA[:, ((2 * nb + j) % 4) * 512:((2 * nb + j) % 4) * 512 + 512]
                      for j in range(2)]
                def latmm(e, nb=nb, ps=bk[0]):
                    for kc in range(8):
                        ins = e.matmul(ps[0:64, 0:512],
                                       kvT_b[:, kc * KVR:(kc + 1) * KVR],
                                       xn[:, kc * T + nb * 512:kc * T + (nb + 1) * 512],
                                       start=(kc == 0), stop=(kc == 7))
                    return ins
                gg("pe", latmm, reads=[kvT_b, xn], writes=[bk[0][0:64, 0:512]])
                gg("dve", (lambda nb=nb, ps=bk[0]: lambda e: e.tensor_copy(
                    LAT[:, nb * 512:(nb + 1) * 512], ps[0:64, 0:512]))(),
                   reads=[bk[0][0:64, 0:512]], writes=[LAT[:, nb * 512:(nb + 1) * 512]])
                for mc in range(2):
                    ps = bk[1 - mc]
                    def qmm(e, mc=mc, nb=nb, WQh=WQh, ps=ps):
                        for kc in range(8):
                            ins = e.matmul(ps,
                                           WQh[:, kc * 256 + mc * 128:kc * 256 + (mc + 1) * 128],
                                           xn[:, kc * T + nb * 512:kc * T + (nb + 1) * 512],
                                           start=(kc == 0), stop=(kc == 7))
                        return ins
                    gg("pe", qmm, reads=[WQh, xn], writes=[ps])
                    dst = qT[:, mc * T + nb * 512:mc * T + (nb + 1) * 512]
                    gg("dve", (lambda dst=dst, ps=ps: lambda e: e.tensor_copy(dst, ps))(),
                       reads=[ps], writes=[dst])

            for step in range(9):
                streams = []
                if "ln1" in PHASES and step < 8:
                    streams += [collect(ln1_apply, 2 * step),
                                collect(ln1_apply, 2 * step + 1)]
                if step >= 1:
                    streams += [collect(attn_nb, step - 1)]
                if streams:
                    interleave(*streams)

            for h in range(NHEADS_EMIT):
                hp = h % 2
                WQh = WQ[:, hp * 2048:(hp + 1) * 2048]
                if h > 0:
                    for mc in range(2):
                        for nb in range(8):
                            ps = psA[:, (nb % 4) * 512:(nb % 4) * 512 + 512]
                            def qmm(e, mc=mc, nb=nb, WQh=WQh, ps=ps):
                                for kc in range(8):
                                    ins = e.matmul(ps,
                                                   WQh[:, kc * 256 + mc * 128:kc * 256 + (mc + 1) * 128],
                                                   xn[:, kc * T + nb * 512:kc * T + (nb + 1) * 512],
                                                   start=(kc == 0), stop=(kc == 7))
                                return ins
                            gg("pe", qmm, reads=[WQh, xn], writes=[ps])
                            dst = qT[:, mc * T + nb * 512:mc * T + (nb + 1) * 512]
                            gg("dve", (lambda dst=dst, ps=ps: lambda e: e.tensor_copy(dst, ps))(),
                               reads=[ps], writes=[dst])
                for mc in range(2):
                    for nb in range(8):
                        ps = psA[:, (nb % 4) * 512:(nb % 4) * 512 + 512]
                        gg("pe", (lambda mc=mc, nb=nb, ps=ps: lambda e: e.matmul(
                            ps, KUh[:, mc * 128:(mc + 1) * 128],
                            LAT[:, nb * 512:(nb + 1) * 512], start=True, stop=True))(),
                           reads=[KUh, LAT[:]], writes=[ps])
                        dst = kT[:, mc * T + nb * 512:mc * T + (nb + 1) * 512]
                        gg("dve", (lambda dst=dst, ps=ps: lambda e: e.tensor_copy(dst, ps))(),
                           reads=[ps], writes=[dst])
                def vproj():
                    for tc in range(32):
                        ps = psA[:, (tc % 4) * 512:(tc % 4) * 512 + 256]
                        gg("pe", (lambda tc=tc, ps=ps: lambda e: e.matmul(
                            ps, LAT[:, tc * 128:(tc + 1) * 128],
                            VUh, start=True, stop=True))(),
                           reads=[VUh, LAT[:]], writes=[ps])
                        dst = VV[:, tc * 257:tc * 257 + 256]
                        gg("dve", (lambda dst=dst, ps=ps: lambda e: e.tensor_copy(dst, ps))(),
                           reads=[ps], writes=[dst])

                def cbound():
                    for mi, (mat, so) in enumerate(((qT, 40), (kT, 48))):
                        for nb in range(8):
                            def sqf(e, mat=mat, nb=nb):
                                for mc in range(2):
                                    ins = e.tensor_tensor(
                                        SCB[:, mc * 512:(mc + 1) * 512],
                                        mat[:, mc * T + nb * 512:mc * T + (nb + 1) * 512],
                                        mat[:, mc * T + nb * 512:mc * T + (nb + 1) * 512],
                                        ALU.mult)
                                return ins
                            gg("pool", sqf, reads=[mat], writes=[SCB[:, 0:1024]])
                            def csmm(e):
                                for mc in range(2):
                                    ins = e.matmul(psB[0:1, 1024:1536], onesc_b[:],
                                                   SCB[:, mc * 512:(mc + 1) * 512],
                                                   start=(mc == 0), stop=(mc == 1))
                                return ins
                            gg("pe", csmm, reads=[SCB[:, 0:1024], onesc_b[:]],
                               writes=[psB[0:1, 1024:1536]])
                            dst = srow[0:1, so + nb:so + nb + 1]
                            gg("dve", (lambda dst=dst: lambda e: e.reduce_max(
                                dst, psB[0:1, 1024:1536], axis=AX.X))(),
                               reads=[psB[0:1, 1024:1536]], writes=[dst])
                        gg("dve", (lambda so=so: lambda e: e.reduce_max(
                            srow[0:1, so - 2:so - 1], srow[0:1, so:so + 8], axis=AX.X))(),
                           reads=[srow[0:1, so:so + 8]], writes=[srow[0:1, so - 2:so - 1]])
                    gg("act", lambda e: e.activation(srow[0:1, 56:57], srow[0:1, 38:39],
                                                     AF.Ln, bias=tiny_c[:]),
                       reads=[srow[0:1, 38:39], tiny_c[:]], writes=[srow[0:1, 56:57]])
                    gg("act", lambda e: e.activation(srow[0:1, 57:58], srow[0:1, 46:47],
                                                     AF.Ln, bias=tiny_c[:]),
                       reads=[srow[0:1, 46:47], tiny_c[:]], writes=[srow[0:1, 57:58]])
                    gg("dve", lambda e: e.tensor_tensor(srow[0:1, 56:57], srow[0:1, 56:57],
                                                        srow[0:1, 57:58], ALU.add),
                       reads=[srow[0:1, 56:58]], writes=[srow[0:1, 56:57]])
                    gg("act", lambda e: e.activation(srow[0:1, 56:57], srow[0:1, 56:57],
                                                     AF.Exp, scale=0.5),
                       reads=[srow[0:1, 56:57]], writes=[srow[0:1, 56:57]])
                    gg("dve", lambda e: e.tensor_scalar(srow[0:1, 56:57], srow[0:1, 56:57],
                                                        -0.0625, -1.0, ALU.mult, ALU.add),
                       reads=[srow[0:1, 56:57]], writes=[srow[0:1, 56:57]])
                    bcast_col(negC[:], srow[0:1, 56:57], psA[:, 300:301])

                interleave(collect(vproj), collect(cbound))
                if h + 1 < NHEADS_EMIT:
                    hp2 = (h + 1) % 2
                    WQn = WQ[:, hp2 * 2048:(hp2 + 1) * 2048]
                    dma_cast(WQn.rearrange("p (c n) -> p c n", c=8),
                             I["wqT"].rearrange("(c p) n -> p c n", p=128)
                             [:, :, (h + 1) * HD:(h + 2) * HD])
                    dma_cast(KUh, I["wkuT"][:, (h + 1) * HD:(h + 2) * HD])
                    dma_cast(VUh, I["wvuT"][:, (h + 1) * HD:(h + 2) * HD])

                def attn_chain(qb512, c, h=h):
                    psm = psA if c == 0 else psB
                    psm_b = psA_b if c == 0 else psB_b
                    PTc = PT[:, c * 2048:(c + 1) * 2048]
                    q0 = qb512 * 512 + c * 256

                    def emit_S(g):
                        bank = psm[:, (g % 2) * 512:(g % 2) * 512 + 512]
                        def f(e, g=g, bank=bank):
                            for kk in range(2):
                                kc = 2 * g + kk
                                for dc in range(2):
                                    ins = e.matmul(
                                        bank[:, kk * 256:(kk + 1) * 256],
                                        kT[:, dc * T + kc * 128:dc * T + (kc + 1) * 128],
                                        qT[:, dc * T + q0:dc * T + q0 + 256],
                                        start=(dc == 0), stop=(dc == 1))
                            return ins
                        gg("pe", f, reads=[qT, kT], writes=[bank])

                    def emit_exp(g):
                        bank = psm[:, (g % 2) * 512:(g % 2) * 512 + 512]
                        dst = PTc[:, (g % 4) * 512:(g % 4 + 1) * 512]
                        gg("act", (lambda bank=bank, dst=dst: lambda e: e.activation(
                            dst, bank, AF.Exp, bias=negC[:], scale=0.0625))(),
                           reads=[bank, negC[:]], writes=[dst])

                    def emit_PV(kc):
                        g = kc // 2
                        src = PTc[:, (g % 4) * 512 + (kc % 2) * 256:
                                  (g % 4) * 512 + (kc % 2) * 256 + 256]
                        def f(e, kc=kc, src=src):
                            for u in range(2):
                                ins = e.matmul(
                                    psm[:, 1024 + u * 512:1024 + u * 512 + 257],
                                    src[:, u * 128:(u + 1) * 128],
                                    VV[:, kc * 257:(kc + 1) * 257],
                                    start=(kc == 0), stop=(kc == 31))
                            return ins
                        gg("pe", f, reads=[src, BIG2[:, 4 * T:4 * T + 32 * 257]],
                           writes=[psm[:, 1024:2048]], accum=(kc > 0))

                    emit_S(0)
                    emit_S(1)
                    for g in range(16):
                        emit_exp(g)
                        if g + 2 < 16:
                            emit_S(g + 2)
                        emit_PV(2 * g)
                        emit_PV(2 * g + 1)
                    OBc = OB[:, c * 512:(c + 1) * 512]
                    RCc = RCt[:, c * 4:c * 4 + 2]
                    gg("dve", (lambda RCc=RCc: lambda e: [
                        e.reciprocal(RCc[:, u:u + 1],
                                     psm[:, 1024 + u * 512 + 256:1024 + u * 512 + 257])
                        for u in range(2)][-1])(),
                       reads=[psm[:, 1024:2048]], writes=[RCc])
                    gg("dve", (lambda OBc=OBc, RCc=RCc: lambda e: [
                        e.tensor_scalar(OBc[:, u * 256:(u + 1) * 256],
                                        psm[:, 1024 + u * 512:1024 + u * 512 + 256],
                                        RCc[:, u:u + 1], None, ALU.mult)
                        for u in range(2)][-1])(),
                       reads=[psm[:, 1024:2048], RCc], writes=[OBc])
                    dma_sp(ottok[q0:q0 + 256, h * HD:(h + 1) * HD]
                           .rearrange("(u p) c -> p u c", p=128),
                           OBc.rearrange("p (u c) -> p u c", u=2))

                for qb512 in range(8):
                    interleave(collect(attn_chain, qb512, 0),
                               collect(attn_chain, qb512, 1))

        SPP = {0: psA[:, 0:1024], 1: psA[:, 1024:2048]}
        BBP = {0: psB[:, 0:1024], 1: psB[:, 1024:2048]}

        def ln_stats3(p, b, src_dram, bb, have=False, rmul=None, bcast=True,
                      keep=False):
            SP = SPP[p]
            stb = SP[0:1, 0:256]
            stq = SP[0:1, 512:768]
            Mb = SP[:, 256:512]
            Rb = SP[:, 768:1024]
            if not have:
                dma_cast(bb.rearrange("p (f n) -> p f n", f=8),
                         src_dram.rearrange("(f p) n -> p f n", p=128)
                         [:, :, b * BLK:(b + 1) * BLK])
            gg("pe", mmsum8(bb, stb), reads=[bb, onesc_b[:]], writes=[stb])
            sq = WA[:, 11264 + p * 2048:11264 + (p + 1) * 2048] if keep else bb
            gg("pool", (lambda bb=bb, sq=sq: lambda e: e.tensor_tensor(
                sq, bb, bb, ALU.mult))(), reads=[bb], writes=[sq])
            gg("pe", mmsum8(sq, stq), reads=[sq, onesc_b[:]], writes=[stq])
            m_r, r_r = MS[p]
            gg("dve", (lambda m_r=m_r, stb=stb: lambda e: e.tensor_scalar_mul(
                m_r, stb, 1.0 / D))(), reads=[stb], writes=[m_r])
            gg("dve", (lambda m_r=m_r, r_r=r_r: lambda e: e.tensor_tensor(
                r_r, m_r, m_r, ALU.mult))(), reads=[m_r], writes=[r_r])
            gg("dve", (lambda r_r=r_r, stq=stq: lambda e: e.scalar_tensor_tensor(
                r_r, stq, 1.0 / D, r_r, ALU.mult, ALU.subtract))(),
               reads=[stq, r_r], writes=[r_r])
            gg("act", (lambda r_r=r_r: lambda e: e.activation(
                r_r, r_r, AF.Ln, bias=eps_c[:]))(), reads=[r_r, eps_c[:]], writes=[r_r])
            gg("act", (lambda r_r=r_r: lambda e: e.activation(
                r_r, r_r, AF.Exp, scale=-0.5))(), reads=[r_r], writes=[r_r])
            if rmul is not None:
                gg("dve", (lambda r_r=r_r: lambda e: e.tensor_scalar(
                    r_r, r_r, rmul, None, ALU.mult))(), reads=[r_r, rmul], writes=[r_r])
            if not bcast:
                return None, None
            gg("pe", (lambda Mb=Mb, m_r=m_r: lambda e: e.matmul(
                Mb, onesr_f[:], m_r, start=True, stop=True))(),
               reads=[m_r, onesr_f[:]], writes=[Mb])
            gg("pe", (lambda Rb=Rb, r_r=r_r: lambda e: e.matmul(
                Rb, onesr_f[:], r_r, start=True, stop=True))(),
               reads=[r_r, onesr_f[:]], writes=[Rb])
            return Mb, Rb

        xn2 = BIG1[:, 0:8 * T]
        xn2v = xn2.rearrange("p (f n) -> p f n", f=8)

        def subA(b):
            p = b % 2
            B_p = Bbuf(p)
            Mb, Rb = ln_stats3(p, b, xres, B_p, keep=True)
            dma_sp(mrow_d[b:b + 1, :], mstat[0:1, p * 512:(p + 1) * 512])
            dst16v = xn2v[:, :, b * BLK:(b + 1) * BLK]
            gg("dve", (lambda B_p=B_p, Mb=Mb: lambda e: [
                e.tensor_tensor(B_p[:, fc * BLK:(fc + 1) * BLK],
                                B_p[:, fc * BLK:(fc + 1) * BLK], Mb, ALU.subtract)
                for fc in range(8)][-1])(),
               reads=[B_p, Mb], writes=[B_p])
            gg("dve", (lambda B_p=B_p, Rb=Rb, dst16v=dst16v: lambda e: [
                e.tensor_tensor(dst16v[:, fc, :],
                                B_p[:, fc * BLK:(fc + 1) * BLK], Rb, ALU.mult)
                for fc in range(8)][-1])(),
               reads=[B_p, Rb], writes=[dst16v])

        cw1_b = WA[:, 0:2048]
        dirs_b = WA[0:64, 2048:3072]
        fo = BIG2[:, 0:8 * T]
        fov = fo.rearrange("p (f n) -> p f n", f=8)

        def load_ffn_w():
            dma_cast(cw1_b.rearrange("p (c n) -> p c n", c=8),
                     I["cw1"].rearrange("(c p) n -> p c n", p=128))
            dma_cast(dirs_b, I["dirs"])

        if True:

            def subB(b):
                p = b % 2
                BB = BBP[p]
                BB_b = psB_b[:, p * 2048:(p + 1) * 2048]
                FS = XB3[:, p * 1024:(p + 1) * 1024]
                SCF = FS[0:72, 0:256]
                SCT = FS[:, 256:400]
                U_ = FS[:, 400:544]
                G0 = FS[:, 544:688]
                G1 = FS[:, 688:832]
                TB = FS[:, 832:976]
                SEL = FS[:, 0:130]
                CT = FS[:, 400:656]
                T1 = FS[:, 976:992]
                T2 = FS[:, 992:1008]
                RT = FS[:, 1008:1016]
                RT2 = FS[:, 1016:1024]
                FR = FRC if p == 0 else RCt
                BS = WA[:, 4096 + p * 1024:4096 + (p + 1) * 1024]
                HB = BS[:, 0:512]
                OH = BS[:, 512:640]
                OHS = BS[:, 640:768]
                O64 = BS[0:64, 768:1024]
                gg("pe", (lambda b=b, BB=BB: lambda e: [
                    e.matmul(BB[0:72, 0:BLK], sigsT[:, kc * 72:(kc + 1) * 72],
                             xn2[:, kc * T + b * BLK:kc * T + (b + 1) * BLK],
                             start=(kc == 0), stop=(kc == 7)) for kc in range(8)][-1])(),
                   reads=[sigsT[:], xn2v[:, :, b * BLK:(b + 1) * BLK]],
                   writes=[BB[0:72, 0:BLK]])
                gg("act", (lambda BB=BB, SCF=SCF: lambda e: e.copy(
                    SCF, BB[0:72, 0:BLK]))(), reads=[BB[0:72, 0:BLK]], writes=[SCF])
                gg("pe", (lambda BB=BB, SCF=SCF: lambda e: [
                    e.transpose(BB[:, 256 + c * 72:256 + (c + 1) * 72],
                                SCF[0:72, c * 128:(c + 1) * 128],
                                idn_f[0:72, 0:72]) for c in range(2)][-1])(),
                   reads=[SCF, idn_f[:]], writes=[BB[:, 256:400]])
                gg("act", (lambda BB=BB, SCT=SCT: lambda e: e.copy(
                    SCT, BB[:, 256:400]))(), reads=[BB[:, 256:400]], writes=[SCT])
                gg("act", (lambda SCT=SCT, U_=U_: lambda e: e.activation(
                    U_, SCT, AF.Tanh, scale=crow[:, 16:17]))(),
                   reads=[SCT, crow[:]], writes=[U_])
                for kknot in range(7):
                    gg("dve", (lambda U_=U_, TB=TB, kknot=kknot: lambda e: e.tensor_scalar(
                        TB, U_, 3.5, float(3.5 - kknot), ALU.mult, ALU.add))(),
                       reads=[U_], writes=[TB])
                    gg("dve", (lambda TB=TB: lambda e: e.tensor_scalar(
                        TB, TB, 0.0, 1.0, ALU.max, ALU.min))(),
                       reads=[TB], writes=[TB])
                    if kknot == 0:
                        gg("dve", (lambda G0=G0, TB=TB: lambda e: e.tensor_scalar(
                            G0, TB, dkn[:, 0:1], dkn[:, 7:8], ALU.mult, ALU.add))(),
                           reads=[TB, dkn[:]], writes=[G0])
                    else:
                        gg("dve", (lambda kknot=kknot, TB=TB:
                                   lambda e: e.scalar_tensor_tensor(
                                       G0, TB, dkn[:, kknot:kknot + 1], G0,
                                       ALU.mult, ALU.add))(),
                           reads=[TB, dkn[:], G0], writes=[G0])
                G = G0
                for c in range(2):
                    gv = G[:, c * 72:(c + 1) * 72]
                    gg("dve", (lambda gv=gv, FR=FR: lambda e: e.reduce_max(
                        FR[:, 0:1], gv[:, 64:72], axis=AX.X))(),
                       reads=[G], writes=[FR[:, 0:1]])
                    gg("dve", (lambda gv=gv, FR=FR, RT=RT: lambda e: e.tensor_scalar(
                        RT, gv[:, 64:72], FR[:, 0:1], None, ALU.is_ge))(),
                       reads=[G, FR[:, 0:1]], writes=[RT])
                    gg("pool", (lambda RT=RT, RT2=RT2: lambda e: e.tensor_tensor(
                        RT2, RT, w8_f[:], ALU.mult))(),
                       reads=[RT, w8_f[:]], writes=[RT2])
                    gg("dve", (lambda RT2=RT2, FR=FR: lambda e: e.reduce_max(
                        FR[:, 1:2], RT2, axis=AX.X))(), reads=[RT2], writes=[FR[:, 1:2]])
                    gg("dve", (lambda RT=RT, RT2=RT2, FR=FR: lambda e: e.tensor_scalar(
                        RT, RT2, FR[:, 1:2], None, ALU.is_ge))(),
                       reads=[RT2, FR[:, 1:2]], writes=[RT])
                    for cc in range(8):
                        if cc == 0:
                            gg("dve", (lambda gv=gv, RT=RT, T1=T1: lambda e: e.tensor_scalar(
                                T1[:, 0:8], gv[:, 0:8], RT[:, 0:1], None, ALU.mult))(),
                               reads=[G, RT], writes=[T1])
                        else:
                            gg("dve", (lambda gv=gv, RT=RT, T1=T1, cc=cc:
                                       lambda e: e.scalar_tensor_tensor(
                                           T1[:, 0:8], gv[:, cc * 8:(cc + 1) * 8],
                                           RT[:, cc:cc + 1], T1[:, 0:8],
                                           ALU.mult, ALU.add))(),
                               reads=[G, RT, T1], writes=[T1])
                    gg("dve", (lambda T1=T1, FR=FR: lambda e: e.reduce_max(
                        FR[:, 2:3], T1[:, 0:8], axis=AX.X))(),
                       reads=[T1], writes=[FR[:, 2:3]])
                    gg("dve", (lambda T1=T1, T2=T2, FR=FR: lambda e: e.tensor_scalar(
                        T2[:, 0:8], T1[:, 0:8], FR[:, 2:3], None, ALU.is_ge))(),
                       reads=[T1, FR[:, 2:3]], writes=[T2])
                    gg("pool", (lambda T2=T2, T1=T1: lambda e: e.tensor_tensor(
                        T1[:, 8:16], T2[:, 0:8], w8_f[:], ALU.mult))(),
                       reads=[T2, w8_f[:]], writes=[T1])
                    gg("dve", (lambda T1=T1, FR=FR: lambda e: e.reduce_max(
                        FR[:, 3:4], T1[:, 8:16], axis=AX.X))(),
                       reads=[T1], writes=[FR[:, 3:4]])
                    gg("dve", (lambda T1=T1, T2=T2, FR=FR: lambda e: e.tensor_scalar(
                        T2[:, 8:16], T1[:, 8:16], FR[:, 3:4], None, ALU.is_ge))(),
                       reads=[T1, FR[:, 3:4]], writes=[T2])
                    gg("dve", (lambda T2=T2, RT=RT, OH=OH, c=c: lambda e: [
                        e.tensor_scalar(OH[:, c * 64 + cc * 8:c * 64 + (cc + 1) * 8],
                                        T2[:, 8:16], RT[:, cc:cc + 1], None, ALU.mult)
                        for cc in range(8)][-1])(),
                       reads=[T2, RT], writes=[OH])
                gg("pe", (lambda b=b, BB=BB: lambda e: [
                    e.matmul(BB[:, 512 + mc * 256:512 + (mc + 1) * 256],
                             cw1_b[:, kc * 256 + mc * 128:kc * 256 + (mc + 1) * 128],
                             xn2[:, kc * T + b * BLK:kc * T + (b + 1) * BLK],
                             start=(kc == 0), stop=(kc == 7))
                    for mc in range(2) for kc in range(8)][-1])(),
                   reads=[cw1_b, xn2v[:, :, b * BLK:(b + 1) * BLK]],
                   writes=[BB[:, 512:1024]])
                gg("act", (lambda BB=BB, HB=HB: lambda e: [
                    e.activation(HB[:, mc * 256:(mc + 1) * 256],
                                 BB[:, 512 + mc * 256:512 + (mc + 1) * 256],
                                 AF.Gelu, bias=cb1_c[:, mc:mc + 1])
                    for mc in range(2)][-1])(),
                   reads=[BB[:, 512:1024], cb1_c[:]], writes=[HB])
                gg("pe", (lambda BB=BB, HB=HB: lambda e: [
                    e.matmul(BB[0:2, 0:BLK], cw2_b[:, kc * 2:(kc + 1) * 2],
                             HB[:, kc * 256:(kc + 1) * 256],
                             start=(kc == 0), stop=(kc == 1)) for kc in range(2)][-1])(),
                   reads=[HB, cw2_b[:]], writes=[BB[0:2, 0:BLK]])
                gg("act", (lambda BB=BB, CT=CT: lambda e: e.activation(
                    CT[0:2, 0:BLK], BB[0:2, 0:BLK], AF.Tanh, bias=cb2_c[:]))(),
                   reads=[BB[0:2, 0:BLK], cb2_c[:]], writes=[CT])
                gg("pe", (lambda BB=BB, CT=CT: lambda e: [
                    e.transpose(BB[:, 256 + c * 2:256 + (c + 1) * 2],
                                CT[0:2, c * 128:(c + 1) * 128], idn_f[0:2, 0:2])
                    for c in range(2)][-1])(),
                   reads=[CT, idn_f[:]], writes=[BB[:, 256:260]])
                gg("dve", (lambda BB=BB, FR=FR: lambda e: e.tensor_copy(
                    FR[:, 4:8], BB[:, 256:260]))(),
                   reads=[BB[:, 256:260]], writes=[FR[:, 4:8]])
                gg("pe", (lambda BB_b=BB_b, OH=OH: lambda e: [
                    e.transpose(BB_b[0:64, c * 128:(c + 1) * 128],
                                OH[:, c * 64:(c + 1) * 64], idn_b[:])
                    for c in range(2)][-1])(),
                   reads=[OH, idn_b[:]], writes=[BB[:, 0:128]])
                gg("dve", (lambda BB_b=BB_b, O64=O64: lambda e: e.tensor_copy(
                    O64, BB_b[0:64, 0:256]))(), reads=[BB[:, 0:128]], writes=[O64])
                gg("pe", (lambda BB=BB, O64=O64: lambda e: [
                    e.matmul(BB[:, 512 + c * 65:512 + (c + 1) * 65],
                             O64[:, c * 128:(c + 1) * 128], spl_b[0:64, :],
                             start=True, stop=True) for c in range(2)][-1])(),
                   reads=[O64, spl_b[:]], writes=[BB[:, 512:642]])
                gg("act", (lambda BB=BB, SEL=SEL: lambda e: e.copy(
                    SEL, BB[:, 512:642]))(), reads=[BB[:, 512:642]], writes=[SEL])
                for c in range(2):
                    SELc = SEL[:, c * 65:(c + 1) * 65]
                    gg("dve", (lambda SELc=SELc, FR=FR, T1=T1, c=c: lambda e: e.tensor_scalar(
                        T1[:, 0:16], SELc[:, 0:16], FR[:, 4 + 2 * c:5 + 2 * c],
                        None, ALU.mult))(),
                       reads=[SEL, FR[:, 4:8]], writes=[T1])
                    gg("dve", (lambda SELc=SELc, FR=FR, T1=T1, T2=T2, c=c:
                               lambda e: e.scalar_tensor_tensor(
                                   T2[:, 0:16], SELc[:, 16:32],
                                   FR[:, 5 + 2 * c:6 + 2 * c], T1[:, 0:16],
                                   ALU.mult, ALU.add))(),
                       reads=[SEL, FR[:, 4:8], T1], writes=[T2])
                    gg("dve", (lambda SELc=SELc, T1=T1, T2=T2: lambda e: e.tensor_tensor(
                        T1[:, 0:16], T2[:, 0:16], SELc[:, 32:48], ALU.add))(),
                       reads=[SEL, T2], writes=[T1])
                    gg("act", (lambda T1=T1: lambda e: e.activation(
                        T1[:, 0:16], T1[:, 0:16], AF.Relu))(), reads=[T1], writes=[T1])
                    gg("dve", (lambda SELc=SELc, T1=T1, T2=T2: lambda e: e.tensor_tensor(
                        T2[:, 0:16], T1[:, 0:16], SELc[:, 48:64], ALU.mult))(),
                       reads=[SEL, T1], writes=[T2])
                    gg("dve", (lambda T2=T2, FR=FR: lambda e: e.reduce_sum(
                        FR[:, 0:1], T2[:, 0:16], axis=AX.X))(),
                       reads=[T2], writes=[FR[:, 0:1]])
                    gg("dve", (lambda SELc=SELc, FR=FR: lambda e: e.tensor_scalar(
                        FR[:, 1:2], FR[:, 0:1], SELc[:, 64:65], oscb[:],
                        ALU.add, ALU.mult))(),
                       reads=[FR[:, 0:1], SEL, oscb[:]], writes=[FR[:, 1:2]])
                    gg("dve", (lambda OH=OH, OHS=OHS, FR=FR, c=c: lambda e: e.tensor_scalar(
                        OHS[:, c * 64:(c + 1) * 64], OH[:, c * 64:(c + 1) * 64],
                        FR[:, 1:2], None, ALU.mult))(),
                       reads=[OH, FR[:, 1:2]], writes=[OHS])
                gg("pe", (lambda BB_b=BB_b, OHS=OHS: lambda e: [
                    e.transpose(BB_b[0:64, c * 128:(c + 1) * 128],
                                OHS[:, c * 64:(c + 1) * 64], idn_b[:])
                    for c in range(2)][-1])(),
                   reads=[OHS, idn_b[:]], writes=[BB[:, 0:128]])
                gg("dve", (lambda BB_b=BB_b, O64=O64: lambda e: e.tensor_copy(
                    O64, BB_b[0:64, 0:256]))(), reads=[BB[:, 0:128]], writes=[O64])
                for hw in range(2):
                    gg("pe", (lambda BB=BB, O64=O64, hw=hw: lambda e: [
                        e.matmul(BB[:, mi * 256 + c * 128:mi * 256 + (c + 1) * 128],
                                 dirs_b[0:64, (hw * 4 + mi) * 128:(hw * 4 + mi + 1) * 128],
                                 O64[:, c * 128:(c + 1) * 128], start=True, stop=True)
                        for mi in range(4) for c in range(2)][-1])(),
                       reads=[O64, dirs_b], writes=[BB[:, 0:1024]])
                    fdst = fo[:, b * 2048 + hw * 1024:b * 2048 + (hw + 1) * 1024]
                    gg("act", (lambda BB=BB, fdst=fdst: lambda e: e.copy(
                        fdst, BB[:, 0:1024]))(),
                       reads=[BB[:, 0:1024]], writes=[fdst])

            def subC(b, lastphase=False):
                p = b % 2
                W = Wbuf(p)
                CY = WA[:, 6144 + p * 2048:6144 + (p + 1) * 2048]
                MRS = WA[:, 10240 + p * 1024:10240 + (p + 1) * 1024]
                fblk = fo[:, b * 2048:(b + 1) * 2048]
                SP = SPP[p]
                mr = mrow_s[0:1, p * 512:(p + 1) * 512]
                dma_sp(mr, mrow_d[b:b + 1, :])
                gg("pe", (lambda mr=mr, SP=SP: lambda e: e.matmul(
                    SP[:, 0:512], onesr_f[:], mr,
                    start=True, stop=True))(),
                   reads=[mr, onesr_f[:]], writes=[SP[:, 0:512]])
                gg("act", (lambda SP=SP, MRS=MRS: lambda e: e.copy(
                    MRS[:, 0:512], SP[:, 0:512]))(),
                   reads=[SP[:, 0:512]], writes=[MRS[:, 0:512]])
                dma_sp(W.rearrange("p (f n) -> p f n", f=8),
                       xres.rearrange("(f p) n -> p f n", p=128)
                       [:, :, b * BLK:(b + 1) * BLK])
                gg("dve", (lambda W=W, MRS=MRS: lambda e: [
                    e.tensor_tensor(W[:, fc * BLK:(fc + 1) * BLK],
                                    W[:, fc * BLK:(fc + 1) * BLK], MRS[:, 0:256],
                                    ALU.subtract)
                    for fc in range(8)][-1])(),
                   reads=[W, MRS], writes=[W])
                gg("dve", (lambda W=W, MRS=MRS: lambda e: [
                    e.tensor_tensor(W[:, fc * BLK:(fc + 1) * BLK],
                                    W[:, fc * BLK:(fc + 1) * BLK], MRS[:, 256:512],
                                    ALU.mult)
                    for fc in range(8)][-1])(),
                   reads=[W, MRS], writes=[W])
                gg("dve", (lambda W=W, fblk=fblk: lambda e: e.tensor_tensor(
                    W, W, fblk, ALU.add))(),
                   reads=[W, fblk], writes=[W])
                gg("act", (lambda W=W, CY=CY: lambda e: e.copy(CY, W))(),
                   reads=[W], writes=[CY])
                ln_stats3(p, b, xres, CY, have=True, bcast=False)
                m2row = mstat[0:1, p * 512:(p + 1) * 512]
                gg("pe", (lambda m2row=m2row, SP=SP: lambda e: e.matmul(
                    SP[:, 0:512], onesr_f[:], m2row,
                    start=True, stop=True))(),
                   reads=[m2row, onesr_f[:]], writes=[SP[:, 0:512]])
                gg("act", (lambda SP=SP, MRS=MRS: lambda e: e.copy(
                    MRS[:, 512:1024], SP[:, 0:512]))(),
                   reads=[SP[:, 0:512]], writes=[MRS[:, 512:1024]])
                gg("dve", (lambda W=W, MRS=MRS: lambda e: [
                    e.tensor_tensor(W[:, fc * BLK:(fc + 1) * BLK],
                                    W[:, fc * BLK:(fc + 1) * BLK], MRS[:, 512:768],
                                    ALU.subtract)
                    for fc in range(8)][-1])(),
                   reads=[W, MRS], writes=[W])
                gg("dve", (lambda W=W, MRS=MRS: lambda e: [
                    e.tensor_tensor(W[:, fc * BLK:(fc + 1) * BLK],
                                    W[:, fc * BLK:(fc + 1) * BLK], MRS[:, 768:1024],
                                    ALU.mult)
                    for fc in range(8)][-1])(),
                   reads=[W, MRS], writes=[W])
                gg("dve", (lambda W=W: lambda e: [
                    e.tensor_scalar(W[:, fc * BLK:(fc + 1) * BLK],
                                    W[:, fc * BLK:(fc + 1) * BLK],
                                    gam_c[:, fc:fc + 1], bet_c[:, fc:fc + 1],
                                    ALU.mult, ALU.add)
                    for fc in range(8)][-1])(),
                   reads=[W, gam_c[:], bet_c[:]], writes=[W])
                xdst = xres.rearrange("(f p) n -> p f n", p=128)[:, :, b * BLK:(b + 1) * BLK]
                if lastphase:
                    odst = xout.rearrange("(f p) n -> p f n", p=128)[:, :, b * BLK:(b + 1) * BLK]
                    dma_sp(odst, xdst)
                    gg("pool", (lambda W=W, odst=odst: lambda e: e.dma_start(
                        out=odst, in_=W.rearrange("p (f n) -> p f n", f=8),
                        accum_op=ALU.add))(),
                       reads=[W, odst], writes=[odst], dma=True)
                else:
                    gg("pool", (lambda W=W, xdst=xdst: lambda e: e.dma_start(
                        out=xdst, in_=W.rearrange("p (f n) -> p f n", f=8),
                        accum_op=ALU.add))(),
                       reads=[W, xdst], writes=[xdst], dma=True)

        def emit_ffn(app, emit_subA=True, skip_b=0, skip_loads=False):
            lastphase = (app == 2)
            if not skip_loads:
                load_ffn_w()
            if emit_subA:
                for b0 in range(0, NBLK, 2):
                    interleave(collect(subA, b0), collect(subA, b0 + 1))
            for k in range(0, NBLK, 2):
                if k >= skip_b:
                    interleave(collect(subB, k), collect(subB, k + 1))
                if k - 2 >= 0:
                    interleave(collect(subC, k - 2, lastphase),
                               collect(subC, k - 1, lastphase))
            interleave(collect(subC, NBLK - 2, lastphase),
                       collect(subC, NBLK - 1, lastphase))


        if "wo" in PHASES:
            WO = BIG2[:, 16384:32768]
            dma_cast(WO.rearrange("p (c n) -> p c n", c=16),
                     I["woT"].rearrange("(c p) n -> p c n", p=128))
            XBR = [XB1, XB2, XB3]

            def wo_macro(m):
                TRG = BIG2[:, (m % 2) * 8192:(m % 2 + 1) * 8192]
                gg("sp", (lambda m=m, TRG=TRG: lambda e: e.dma_start_transpose(
                    TRG.rearrange("p (c n) -> p c n", c=16),
                    ottok[m * 512:(m + 1) * 512, :]))(),
                   reads=[ottok[m * 512:(m + 1) * 512, :]], writes=[TRG], dma=True)
                xb0 = XBR[(2 * m) % 3]
                xb1 = XBR[(2 * m + 1) % 3]
                for bb, xbuf in ((2 * m, xb0), (2 * m + 1, xb1)):
                    src = I["xT"].rearrange("(f p) n -> p f n", p=128)[
                        :, :, bb * BLK:(bb + 1) * BLK]
                    gg("act", (lambda xbuf=xbuf, src=src: lambda e: e.dma_start(
                        out=xbuf.rearrange("p (f n) -> p f n", f=8), in_=src))(),
                       reads=[src], writes=[xbuf], dma=True)
                for g in range(4):
                    bank = psB[:, (g % 2) * 1024:(g % 2 + 1) * 1024]
                    def womm(e, g=g, bank=bank, TRG=TRG):
                        for mi in range(2):
                            mc = 2 * g + mi
                            for kc in range(16):
                                ins = e.matmul(
                                    bank[:, mi * 512:(mi + 1) * 512],
                                    WO[:, kc * 1024 + mc * 128:kc * 1024 + (mc + 1) * 128],
                                    TRG[:, kc * 512:(kc + 1) * 512],
                                    start=(kc == 0), stop=(kc == 15))
                        return ins
                    gg("pe", womm, reads=[WO, TRG], writes=[bank])
                    def woadd(e, g=g, bank=bank, xb0=xb0, xb1=xb1):
                        for mi in range(2):
                            mc = 2 * g + mi
                            for hh, xbuf in ((0, xb0), (1, xb1)):
                                ins = e.tensor_tensor(
                                    xbuf[:, mc * BLK:(mc + 1) * BLK],
                                    bank[:, mi * 512 + hh * 256:mi * 512 + hh * 256 + 256],
                                    xbuf[:, mc * BLK:(mc + 1) * BLK], ALU.add)
                        return ins
                    gg("dve", woadd, reads=[bank, xb0, xb1],
                       writes=[xb0[:, 2 * g * BLK:(2 * g + 2) * BLK],
                               xb1[:, 2 * g * BLK:(2 * g + 2) * BLK]])
                for bb, xbuf in ((2 * m, xb0), (2 * m + 1, xb1)):
                    dst = xres.rearrange("(f p) n -> p f n", p=128)[
                        :, :, bb * BLK:(bb + 1) * BLK]
                    gg("act", (lambda xbuf=xbuf, dst=dst: lambda e: e.dma_start(
                        out=dst, in_=xbuf.rearrange("p (f n) -> p f n", f=8)))(),
                       reads=[xbuf], writes=[dst], dma=True)

            do_subA = "ffn1" in PHASES
            for m in range(8):
                if m == 0 or not do_subA:
                    interleave(collect(wo_macro, m))
                else:
                    interleave(collect(wo_macro, m),
                               collect(subA, 2 * m - 2), collect(subA, 2 * m - 1))
            if do_subA:
                interleave(collect(subA, 14), collect(subA, 15))

        if "ffn1" in PHASES:
            emit_ffn(1, emit_subA=("wo" not in PHASES))

        if "xbp" in PHASES:
            WB = BIG1[:, 0:8192]
            dma_cast(WB.rearrange("p (c n) -> p c n", c=8),
                     I["wbT"].rearrange("(c p) n -> p c n", p=128))
            xb_ = BIG2[:, 0:8 * T]
            xbv_ = xb_.rearrange("p (f n) -> p f n", f=8)
            XSTB = {0: psA[0:1, 0:256], 1: psA[0:1, 1024:1280]}
            XSTQ = {0: psA[0:1, 512:768], 1: psA[0:1, 1536:1792]}
            XBCM = {0: psA[:, 256:512], 1: psA[:, 1280:1536]}
            XBCR = {0: psA[:, 768:1024], 1: psA[:, 1792:2048]}

            def xbp_ln(p, b, rmul):
                B_p = Bbuf(p)
                dma_cast(B_p.rearrange("p (f n) -> p f n", f=8),
                         xres.rearrange("(f p) n -> p f n", p=128)
                         [:, :, b * BLK:(b + 1) * BLK])
                gg("pe", mmsum8(B_p, XSTB[p]), reads=[B_p, onesc_b[:]], writes=[XSTB[p]])
                gg("pool", (lambda B_p=B_p: lambda e: e.tensor_tensor(
                    B_p, B_p, B_p, ALU.mult))(), reads=[B_p], writes=[B_p])
                gg("pe", mmsum8(B_p, XSTQ[p]), reads=[B_p, onesc_b[:]], writes=[XSTQ[p]])
                m_r, r_r = MS[p]
                gg("dve", (lambda p=p, m_r=m_r: lambda e: e.tensor_scalar_mul(
                    m_r, XSTB[p], 1.0 / D))(), reads=[XSTB[p]], writes=[m_r])
                gg("dve", (lambda m_r=m_r, r_r=r_r: lambda e: e.tensor_tensor(
                    r_r, m_r, m_r, ALU.mult))(), reads=[m_r], writes=[r_r])
                gg("dve", (lambda p=p, r_r=r_r: lambda e: e.scalar_tensor_tensor(
                    r_r, XSTQ[p], 1.0 / D, r_r, ALU.mult, ALU.subtract))(),
                   reads=[XSTQ[p], r_r], writes=[r_r])
                gg("act", (lambda r_r=r_r: lambda e: e.activation(
                    r_r, r_r, AF.Ln, bias=eps_c[:]))(),
                   reads=[r_r, eps_c[:]], writes=[r_r])
                gg("act", (lambda r_r=r_r: lambda e: e.activation(
                    r_r, r_r, AF.Exp, scale=-0.5))(), reads=[r_r], writes=[r_r])
                gg("dve", (lambda r_r=r_r: lambda e: e.tensor_scalar(
                    r_r, r_r, rmul, None, ALU.mult))(),
                   reads=[r_r, rmul], writes=[r_r])
                gg("pe", (lambda p=p, m_r=m_r: lambda e: e.matmul(
                    XBCM[p], onesr_f[:], m_r, start=True, stop=True))(),
                   reads=[m_r, onesr_f[:]], writes=[XBCM[p]])
                gg("pe", (lambda p=p, r_r=r_r: lambda e: e.matmul(
                    XBCR[p], onesr_f[:], r_r, start=True, stop=True))(),
                   reads=[r_r, onesr_f[:]], writes=[XBCR[p]])

            def xbp_block(b):
                p = b % 2
                xbp_ln(p, b, isig[0:1, 1:2])
                B_p = Bbuf(p)
                dma_cast(B_p.rearrange("p (f n) -> p f n", f=8),
                         xres.rearrange("(f p) n -> p f n", p=128)
                         [:, :, b * BLK:(b + 1) * BLK])
                gg("dve", (lambda p=p, B_p=B_p: lambda e: [
                    e.tensor_tensor(B_p[:, fc * BLK:(fc + 1) * BLK],
                                    B_p[:, fc * BLK:(fc + 1) * BLK], XBCM[p], ALU.subtract)
                    for fc in range(8)][-1])(),
                   reads=[B_p, XBCM[p]], writes=[B_p])
                gg("dve", (lambda p=p, B_p=B_p: lambda e: [
                    e.tensor_tensor(B_p[:, fc * BLK:(fc + 1) * BLK],
                                    B_p[:, fc * BLK:(fc + 1) * BLK], XBCR[p], ALU.mult)
                    for fc in range(8)][-1])(),
                   reads=[B_p, XBCR[p]], writes=[B_p])
                bank = psA[:, p * 1024:(p + 1) * 1024]
                dst = xbv_[:, :, b * BLK:(b + 1) * BLK]
                for hw in range(2):
                    def xbmm(e, B_p=B_p, bank=bank, hw=hw):
                        for mi in range(4):
                            mc = hw * 4 + mi
                            for kc in range(8):
                                ins = e.matmul(bank[:, mi * 256:(mi + 1) * 256],
                                               WB[:, kc * 1024 + mc * 128:
                                                  kc * 1024 + (mc + 1) * 128],
                                               B_p[:, kc * BLK:(kc + 1) * BLK],
                                               start=(kc == 0), stop=(kc == 7))
                        return ins
                    gg("pe", xbmm, reads=[WB, B_p], writes=[bank])
                    def xbev(e, dst=dst, bank=bank, hw=hw):
                        for mi in range(4):
                            ins = e.tensor_copy(dst[:, hw * 4 + mi, :],
                                                bank[:, mi * 256:(mi + 1) * 256])
                        return ins
                    gg("dve", xbev, reads=[bank], writes=[dst])
            R = [WA[:, 0:8192], WA[:, 8192:16384], BIG1[:, 8192:16384]]
            def emit_powers():
                dma_sp(R[0].rearrange("p (c n) -> p c n", c=8),
                       atn_d.rearrange("(c p) n -> p c n", p=128))
                dma_sp(R[1].rearrange("p (c n) -> p c n", c=8),
                       ann_d.rearrange("(c p) n -> p c n", p=128))
                emit_powers_body()
            def emit_powers_body():
              i0, i1, i2 = 0, 1, 2
              for lvl in range(8):
                for oc in range(8):
                    ps = psB
                    half = oc % 2
                    pslc = ps[:, half * 1024:(half + 1) * 1024]
                    def sqw(e, oc=oc, i0=i0, i1=i1, ps=ps, half=half):
                        for rc in range(8):
                            for nh in range(2):
                                ins = e.matmul(
                                    ps[:, half * 1024 + nh * 512:half * 1024 + (nh + 1) * 512],
                                    R[i1][:, rc * 1024 + oc * 128:rc * 1024 + (oc + 1) * 128],
                                    R[i0][:, rc * 1024 + nh * 512:rc * 1024 + (nh + 1) * 512],
                                    start=(rc == 0), stop=(rc == 7))
                        return ins
                    gg("pe", sqw, reads=[R[i0], R[i1]], writes=[pslc])
                    dstoc = R[i2][:, oc * 1024:(oc + 1) * 1024]
                    gg("dve", (lambda dstoc=dstoc, pslc=pslc: lambda e: e.tensor_copy(
                        dstoc, pslc))(),
                       reads=[pslc], writes=[dstoc])
                if lvl == 3:
                    dma_sp(m1t_d.rearrange("(c p) n -> p c n", p=128),
                           R[i2].rearrange("p (c n) -> p c n", c=8))
                if lvl == 4:
                    dma_sp(m3t_d.rearrange("(c p) n -> p c n", p=128),
                           R[i2].rearrange("p (c n) -> p c n", c=8))
                if lvl == 7:
                    dma_sp(m2t_d.rearrange("(c p) n -> p c n", p=128),
                           R[i2].rearrange("p (c n) -> p c n", c=8))
                if lvl < 7:
                    natv = R[i1].rearrange("p (c n) -> p c n", c=8)
                    for oc in range(8):
                        for cc2 in range(0, 8, 2):
                            def ftr(e, oc=oc, cc2=cc2, i2=i2):
                                for q in range(2):
                                    ins = e.transpose(
                                        psB_b[:, q * 128:(q + 1) * 128],
                                        R[i2][:, oc * 1024 + (cc2 + q) * 128:
                                              oc * 1024 + (cc2 + q + 1) * 128],
                                        idn_b[:])
                                return ins
                            gg("pe", ftr,
                               reads=[R[i2][:, oc * 1024:(oc + 1) * 1024], idn_b[:]],
                               writes=[psB[:, 0:128]])
                            def fev(e, oc=oc, cc2=cc2, natv=natv):
                                ins = e.tensor_copy(natv[:, cc2, oc * 128:(oc + 1) * 128],
                                                    psB_b[:, 0:128])
                                ins = e.tensor_copy(natv[:, cc2 + 1, oc * 128:(oc + 1) * 128],
                                                    psB_b[:, 128:256])
                                return ins
                            gg("dve", fev, reads=[psB[:, 0:128]],
                               writes=[natv[:, cc2, oc * 128:(oc + 1) * 128],
                                       natv[:, cc2 + 1, oc * 128:(oc + 1) * 128]])
                    i0, i2 = i2, i0

            xb_work = []
            for b0 in range(0, NBLK, 2):
                xb_work += collect(xbp_block, b0) + collect(xbp_block, b0 + 1)
            interleave(xb_work, collect(emit_powers))

        if "scan" in PHASES:
            H1 = SCB[:, 0:2048]
            CCb = SCB[:, 2048:4096]
            AT = WA[:, 0:8192]
            M1T = WA[:, 8192:16384]
            M2T = WA[:, 0:8192]
            dma_sp(AT.rearrange("p (c n) -> p c n", c=8),
                   atn_d.rearrange("(c p) n -> p c n", p=128))
            dma_sp(M1T.rearrange("p (c n) -> p c n", c=8),
                   m1t_d.rearrange("(c p) n -> p c n", p=128))
            xb_ = BIG2[:, 0:8 * T]
            hloc = BIG1[:, 0:8 * T]
            hlv = hloc.rearrange("p (f i j) -> p f i j", f=8, i=C0)
            xbiv = xb_.rearrange("p (f j i) -> p f j i", f=8, i=C0)

            for i in range(C0):
                if i == 0:
                    gg("dve", lambda e: e.tensor_copy(hlv[:, :, 0, :], xbiv[:, :, :, 0]),
                       reads=[xb_], writes=[hlv[:, :, 0, :]])
                    continue
                ps = psA if (i % 2 == 0) else psB
                prev = hlv[:, :, i - 1, :]
                for half in range(2):
                    def a0mm(e, i=i, ps=ps, half=half):
                        for mi in range(4):
                            mc = half * 4 + mi
                            for kc in range(8):
                                ins = e.matmul(ps[:, mc * 256:(mc + 1) * 256],
                                               AT[:, kc * 1024 + mc * 128:kc * 1024 + (mc + 1) * 128],
                                               hloc[:, kc * T + (i - 1) * 256:kc * T + i * 256],
                                               start=(kc == 0), stop=(kc == 7))
                        return ins
                    gg("pe", a0mm, reads=[AT, prev],
                       writes=[ps[:, half * 1024:(half + 1) * 1024]])
                for half in range(2):
                    gg("dve", (lambda i=i, ps=ps, half=half: lambda e: e.tensor_tensor(
                        hlv[:, half * 4:(half + 1) * 4, i, :],
                        ps[:, half * 1024:(half + 1) * 1024]
                        .rearrange("p (q n) -> p q n", q=4),
                        xbiv[:, half * 4:(half + 1) * 4, :, i],
                        ALU.add))(),
                       reads=[ps[:, half * 1024:(half + 1) * 1024], xb_],
                       writes=[hlv[:, half * 4:(half + 1) * 4, i, :]])

            h1v = H1.rearrange("p (f i j) -> p f i j", f=8, i=C1)
            e1full = hlv[:, :, C0 - 1, :]
            e1v = hloc.rearrange("p (f i j2 i2) -> p f i j2 i2", f=8, i=C0, i2=C1)[:, :, C0 - 1, :, :]
            M3T = WA[:, 0:8192]
            dma_sp(M3T.rearrange("p (c n) -> p c n", c=8),
                   m3t_d.rearrange("(c p) n -> p c n", p=128))
            F1 = SCB[:, 2048:4096]
            F1v = F1.rearrange("p (f j2 i2) -> p f j2 i2", f=8, i2=C1)
            def f1mm(e):
                for mc in range(8):
                    for kc in range(8):
                        ins = e.matmul(psB[:, mc * 256:(mc + 1) * 256],
                                       M1T[:, kc * 1024 + mc * 128:kc * 1024 + (mc + 1) * 128],
                                       hloc[:, kc * T + (C0 - 1) * 256:kc * T + C0 * 256],
                                       start=(kc == 0), stop=(kc == 7))
                return ins
            gg("pe", f1mm, reads=[M1T, e1full], writes=[psB[:, 0:2048]])
            gg("dve", lambda e: e.tensor_copy(F1, psB[:, 0:2048]),
               reads=[psB[:, 0:2048]], writes=[F1])
            for i in range(C1):
                if i == 0:
                    gg("dve", lambda e: e.tensor_copy(h1v[:, :, 0, :], e1v[:, :, :, 0]),
                       reads=[e1full], writes=[h1v[:, :, 0, :]])
                    continue
                ps = psA if (i % 2 == 0) else psB
                if i == 1:
                    def a1mm(e, i=i, ps=ps):
                        for mc in range(8):
                            for kc in range(8):
                                ins = e.matmul(ps[:, mc * 16:(mc + 1) * 16],
                                               M1T[:, kc * 1024 + mc * 128:kc * 1024 + (mc + 1) * 128],
                                               H1[:, kc * 256 + (i - 1) * 16:kc * 256 + i * 16],
                                               start=(kc == 0), stop=(kc == 7))
                        return ins
                    gg("pe", a1mm, reads=[M1T, h1v[:, :, i - 1, :]], writes=[ps[:, 0:128]])
                    gg("dve", (lambda i=i, ps=ps: lambda e: e.tensor_tensor(
                        h1v[:, :, i, :], ps[:, 0:128].rearrange("p (f n) -> p f n", f=8),
                        e1v[:, :, :, i], ALU.add))(),
                       reads=[ps[:, 0:128], e1full], writes=[h1v[:, :, i, :]])
                else:
                    def a1mm2(e, i=i, ps=ps):
                        for mc in range(8):
                            for kc in range(8):
                                ins = e.matmul(ps[:, mc * 16:(mc + 1) * 16],
                                               M3T[:, kc * 1024 + mc * 128:kc * 1024 + (mc + 1) * 128],
                                               H1[:, kc * 256 + (i - 2) * 16:kc * 256 + (i - 1) * 16],
                                               start=(kc == 0), stop=(kc == 7))
                        return ins
                    gg("pe", a1mm2, reads=[M3T, h1v[:, :, i - 2, :]], writes=[ps[:, 0:128]])
                    gg("dve", (lambda i=i, ps=ps: lambda e: e.tensor_tensor(
                        h1v[:, :, i, :], ps[:, 0:128].rearrange("p (f n) -> p f n", f=8),
                        F1v[:, :, :, i - 1], ALU.add))(),
                       reads=[ps[:, 0:128], F1], writes=[h1v[:, :, i, :]])
                    gg("dve", (lambda i=i: lambda e: e.tensor_tensor(
                        h1v[:, :, i, :], h1v[:, :, i, :],
                        e1v[:, :, :, i], ALU.add))(),
                       reads=[h1v[:, :, i, :], e1full], writes=[h1v[:, :, i, :]])

            dma_sp(M2T.rearrange("p (c n) -> p c n", c=8),
                   m2t_d.rearrange("(c p) n -> p c n", p=128))
            s2v = S2s[:].rearrange("p (f j) -> p f j", f=8)
            e2v = h1v[:, :, C1 - 1, :]
            for j in range(N1):
                if j == 0:
                    gg("dve", lambda e: e.tensor_copy(s2v[:, :, 0:1], e2v[:, :, 0:1]),
                       reads=[h1v[:, :, C1 - 1, :]], writes=[s2v[:, :, 0:1]])
                    continue
                def l2mm(e, j=j):
                    for mc in range(8):
                        for kc in range(8):
                            ins = e.matmul(psA[:, mc:mc + 1],
                                           M2T[:, kc * 1024 + mc * 128:kc * 1024 + (mc + 1) * 128],
                                           S2s[:, kc * 16 + (j - 1):kc * 16 + j],
                                           start=(kc == 0), stop=(kc == 7))
                    return ins
                gg("pe", l2mm, reads=[M2T, s2v[:, :, j - 1:j]], writes=[psA[:, 0:8]])
                gg("dve", (lambda j=j: lambda e: e.tensor_tensor(
                    s2v[:, :, j:j + 1], psA[:, 0:8].rearrange("p (f o) -> p f o", f=8),
                    e2v[:, :, j:j + 1], ALU.add))(),
                   reads=[psA[:, 0:8], h1v[:, :, C1 - 1, :]], writes=[s2v[:, :, j:j + 1]])

            c1v = C1b[:].rearrange("p (f j) -> p f j", f=8)
            gg("dve", lambda e: e.tensor_copy(c1v[:, :, 1:N1], s2v[:, :, 0:N1 - 1]),
               reads=[S2s[:]], writes=[C1b[:]])
            gg("pool", lambda e: e.memset(c1v[:, :, 0:1], 0.0), writes=[c1v[:, :, 0:1]])
            dma_sp(M3T.rearrange("p (c n) -> p c n", c=8),
                   m3t_d.rearrange("(c p) n -> p c n", p=128))
            for i in range(C1):
                ps = psA if (i % 2 == 0) else psB
                if i < 2:
                    src = C1b if i == 0 else C1c
                    lhs = M1T
                else:
                    src = C1c if i % 2 == 0 else C1b
                    lhs = M3T
                dst = C1c if i % 2 == 0 else C1b
                def c1mm(e, ps=ps, src=src, lhs=lhs):
                    for mc in range(8):
                        for kc in range(8):
                            ins = e.matmul(ps[:, 512 + mc * 16:512 + (mc + 1) * 16],
                                           lhs[:, kc * 1024 + mc * 128:kc * 1024 + (mc + 1) * 128],
                                           src[:, kc * 16:(kc + 1) * 16],
                                           start=(kc == 0), stop=(kc == 7))
                    return ins
                gg("pe", c1mm, reads=[lhs, src[:]], writes=[ps[:, 512:640]])
                gg("dve", (lambda ps=ps, dst=dst: lambda e: e.tensor_copy(
                    dst[:], ps[:, 512:640]))(),
                   reads=[ps[:, 512:640]], writes=[dst[:]])
                gg("dve", (lambda i=i, ps=ps: lambda e: e.tensor_tensor(
                    h1v[:, :, i, :], ps[:, 512:640].rearrange("p (f n) -> p f n", f=8),
                    h1v[:, :, i, :], ALU.add))(),
                   reads=[ps[:, 512:640], h1v[:, :, i, :]], writes=[h1v[:, :, i, :]])

            dma_sp(AT.rearrange("p (c n) -> p c n", c=8),
                   atn_d.rearrange("(c p) n -> p c n", p=128))
            CN = WA[:, 8192:16384]
            dma_sp(CN.rearrange("p (c n) -> p c n", c=8),
                   cnt_d.rearrange("(c p) n -> p c n", p=128))
            ccj = CCb.rearrange("p (f j2 i2) -> p f j2 i2", f=8, i2=16)
            h1j = H1.rearrange("p (f i2 j2) -> p f j2 i2", f=8, i2=16)
            gg("dve", lambda e: e.tensor_copy(ccj[:, :, :, 1:16], h1j[:, :, :, 0:15]),
               reads=[H1], writes=[CCb])
            gg("dve", lambda e: e.tensor_copy(ccj[:, :, 1:16, 0], h1v[:, :, 15, 0:15]),
               reads=[H1], writes=[ccj[:, :, 1:16, 0]])
            gg("pool", lambda e: e.memset(ccj[:, :, 0, 0], 0.0), writes=[ccj[:, :, 0, 0]])

            yv = BIG2[:].rearrange("p (f j i) -> p f j i", f=8, i=C0)

            def c0_step(i):
                def c0mm(e):
                    for mc in range(8):
                        for kc in range(8):
                            ins = e.matmul(psA[:, mc * 256:(mc + 1) * 256],
                                           AT[:, kc * 1024 + mc * 128:kc * 1024 + (mc + 1) * 128],
                                           CCb[:, kc * 256:(kc + 1) * 256],
                                           start=(kc == 0), stop=(kc == 7))
                    return ins
                gg("pe", c0mm, reads=[AT, CCb], writes=[psA[:, 0:2048]])
                gg("dve", lambda e: e.tensor_copy(CCb, psA[:, 0:2048]),
                   reads=[psA[:, 0:2048]], writes=[CCb])
                gg("dve", (lambda i=i: lambda e: e.tensor_tensor(
                    hlv[:, :, i, :],
                    psA[:, 0:2048].rearrange("p (f n) -> p f n", f=8),
                    hlv[:, :, i, :], ALU.add))(),
                   reads=[psA[:, 0:2048], hlv[:, :, i, :]], writes=[hlv[:, :, i, :]])

            def ys_step(i):
                for hw in range(2):
                    bank = psB[:, hw * 1024:(hw + 1) * 1024]
                    def ysmm(e, i=i, hw=hw, bank=bank):
                        for mi in range(4):
                            mc = hw * 4 + mi
                            for kc in range(8):
                                ins = e.matmul(
                                    bank[:, mi * 256:(mi + 1) * 256],
                                    CN[:, kc * 1024 + mc * 128:kc * 1024 + (mc + 1) * 128],
                                    hloc[:, kc * T + i * 256:kc * T + (i + 1) * 256],
                                    start=(kc == 0), stop=(kc == 7))
                        return ins
                    gg("pe", ysmm, reads=[CN, hlv[:, :, i, :]], writes=[bank])
                    ydst = BIG2[:].rearrange(
                        "p (b f jj i) -> p f b jj i", b=NBLK, f=8, i=C0)[
                        :, hw * 4:(hw + 1) * 4, :, :, i]
                    gg("dve", (lambda i=i, hw=hw, bank=bank, ydst=ydst:
                               lambda e: e.tensor_copy(
                        ydst,
                        bank.rearrange("p (q b jj) -> p q b jj", q=4, b=NBLK)))(),
                       reads=[bank], writes=[ydst])

            for i in range(C0):
                interleave(collect(c0_step, i))
                if i >= 1:
                    interleave(collect(ys_step, i - 1))
            interleave(collect(ys_step, C0 - 1))

            ys_subA = "ffn2" in PHASES
            yfv = BIG2[:].rearrange("p (f n) -> p f n", f=8)
            XBR2 = [XB1, XB2]

            def final_pass(b):
                xbuf = XBR2[b % 2]
                yblk = BIG2[:, b * 2048:(b + 1) * 2048]
                src = xres.rearrange("(f p) n -> p f n", p=128)[
                    :, :, b * BLK:(b + 1) * BLK]
                gg("act", (lambda xbuf=xbuf, src=src: lambda e: e.dma_start(
                    out=xbuf.rearrange("p (f n) -> p f n", f=8), in_=src))(),
                   reads=[src], writes=[xbuf], dma=True)
                gg("dve", (lambda xbuf=xbuf, yblk=yblk: lambda e: e.tensor_tensor(
                    xbuf, yblk, xbuf, ALU.add))(),
                   reads=[xbuf, yblk], writes=[xbuf])
                dma_sp(xres.rearrange("(f p) n -> p f n", p=128)
                       [:, :, b * BLK:(b + 1) * BLK],
                       xbuf.rearrange("p (f n) -> p f n", f=8))

            if ys_subA:
                load_ffn_w()
            for b0 in range(0, NBLK, 2):
                streams = [collect(final_pass, b0) + collect(final_pass, b0 + 1)]
                if ys_subA and b0 >= 2:
                    streams += [collect(subA, b0 - 2), collect(subA, b0 - 1)]
                if ys_subA and b0 >= 6:
                    streams += [collect(subB, b0 - 6), collect(subB, b0 - 5)]
                interleave(*streams)
            if ys_subA:
                interleave(collect(subA, NBLK - 2), collect(subA, NBLK - 1),
                           collect(subB, NBLK - 6), collect(subB, NBLK - 5))

        if "ffn2" in PHASES:
            if "scan" in PHASES:
                emit_ffn(2, emit_subA=False, skip_b=12, skip_loads=True)
            else:
                emit_ffn(2)
        else:
            for b in range(NBLK):
                dma_sp(XB3.rearrange("p (f n) -> p f n", f=8),
                       xres.rearrange("(f p) n -> p f n", p=128)[:, :, b * BLK:(b + 1) * BLK])
                dma_sp(xout.rearrange("(f p) n -> p f n", p=128)[:, :, b * BLK:(b + 1) * BLK],
                       XB3.rearrange("p (f n) -> p f n", f=8))

        block = stack.enter_context(nc.Block())
        P.finalize(block)
    return nc


def make_inputs(inputs, core):
    x = inputs["L0_attn_a_x"]
    d = {
        "xT": np.ascontiguousarray(np.asarray(x)[core].T),
        "wqT": np.ascontiguousarray(np.asarray(inputs["L0_attn_a_Wq"]).T),
        "wkvT": np.ascontiguousarray(np.asarray(inputs["L0_attn_a_Wkv_down"]).T),
        "wkuT": np.ascontiguousarray(np.asarray(inputs["L0_attn_a_Wk_up"]).T),
        "wvuT": np.ascontiguousarray(np.asarray(inputs["L0_attn_a_Wv_up"]).T),
        "woT": np.ascontiguousarray(np.asarray(inputs["L0_attn_a_Wo"]).T),
        "wa": np.ascontiguousarray(np.asarray(inputs["L0_attn_b_W_A"])),
        "waT": np.ascontiguousarray(np.asarray(inputs["L0_attn_b_W_A"]).T),
        "wb": np.ascontiguousarray(np.asarray(inputs["L0_attn_b_W_B"])),
        "wbT": np.ascontiguousarray(np.asarray(inputs["L0_attn_b_W_B"]).T),
        "wc": np.ascontiguousarray(np.asarray(inputs["L0_attn_b_W_C"])),
        "wcT": np.ascontiguousarray(np.asarray(inputs["L0_attn_b_W_C"]).T),
        "sraw": np.ascontiguousarray(np.asarray(inputs["L0_ffn_signatures_raw"])),
        "knots": np.asarray(inputs["L0_ffn_knot_values"]).reshape(1, 8),
        "temp": np.asarray(inputs["L0_ffn_temperature"]).reshape(1, 1),
        "cw1": np.ascontiguousarray(np.asarray(inputs["L0_ffn_compress_w1"])),
        "cb1c": np.ascontiguousarray(np.asarray(inputs["L0_ffn_compress_b1"]).reshape(2, 128).T),
        "cw2": np.ascontiguousarray(np.asarray(inputs["L0_ffn_compress_w2"])),
        "cb2": np.asarray(inputs["L0_ffn_compress_b2"]).reshape(2, 1),
        "spl": np.ascontiguousarray(np.concatenate([
            np.asarray(inputs["L0_ffn_spline_w1"])[:, 0, :],
            np.asarray(inputs["L0_ffn_spline_w1"])[:, 1, :],
            np.asarray(inputs["L0_ffn_spline_b1"]),
            np.asarray(inputs["L0_ffn_spline_w2"])[:, :, 0],
            np.asarray(inputs["L0_ffn_spline_b2"])], axis=1)),
        "dirs": np.ascontiguousarray(np.asarray(inputs["L0_ffn_directions"])),
        "gam": np.ascontiguousarray(np.asarray(inputs["L0_ffn_norm_gamma"]).reshape(8, 128).T),
        "bet": np.ascontiguousarray(np.asarray(inputs["L0_ffn_norm_beta"]).reshape(8, 128).T),
        "osc": np.asarray(inputs["L0_ffn_output_scale"]).reshape(1, 1),
        "idn": np.eye(128, dtype=np.float32),
        "onesr": np.ones((1, 128), np.float32),
        "w8": np.tile(np.arange(8, 0, -1, dtype=np.float32)[None, :], (128, 1)),
        "ind8": np.kron(np.eye(8, dtype=np.float32), np.ones((8, 1), np.float32)) / 8.0,
        "epsln": np.full((1, 1), 1e-5, np.float32),
        "epstiny": np.full((1, 1), 1e-30, np.float32),
    }
    return {k: np.ascontiguousarray(np.asarray(v, dtype=np.float32)) for k, v in d.items()}


def kernel(**inputs):
    nc = build_nc()
    in_maps = [make_inputs(inputs, c) for c in range(8)]
    res = run_bass_kernel_spmd(nc, in_maps, core_ids=list(range(8)))
    out = np.stack([np.ascontiguousarray(np.asarray(r["xout"], np.float32).T)
                    for r in res.results], axis=0)
    return out.astype(np.float32)
